# version 06
# baseline (speedup 1.0000x reference)
import numpy as np
import ml_dtypes
import concourse.bass as bass
import concourse.mybir as mybir
from concourse.bass_utils import run_bass_kernel_spmd
from contextlib import ExitStack

F32 = mybir.dt.float32
BF16 = mybir.dt.bfloat16
AF = mybir.ActivationFunctionType
ALU = mybir.AluOpType
BF = ml_dtypes.bfloat16

C = 1024
L = 128
TT_TILE = 512
NCHUNK = TT_TILE // L
PC_CLAMP = -15.0
GC1 = 0.044715 * 0.7978845608028654
GC2 = 0.7978845608028654


def _lhsT(Wcd, ng):
    return np.ascontiguousarray(
        np.asarray(Wcd, np.float32).reshape(ng, 128, -1).transpose(1, 0, 2))


def _fm(v):
    return np.ascontiguousarray(np.asarray(v, np.float32).reshape(8, 128).T)


class Sched:
    ENGS = "TVSDG"
    SEMS = "TVSDGP"

    def __init__(self, serialize=False):
        self.q = {e: [] for e in self.ENGS}
        self.cnt = {e: 0 for e in self.SEMS}
        self.serialize = serialize
        self.last = None

    def emit(self, e, fn, waits=(), inc=False, dma=False):
        w = [t for t in waits if t is not None]
        if self.serialize:
            inc = True
            if self.last is not None:
                w = w + [self.last]
        sem = e
        if e == "G" and not dma:
            sem = "P"
        amt = 16 if dma else 1
        self.q[e].append((w, fn, inc, amt, sem))
        if inc:
            self.cnt[sem] += amt
            tok = (sem, self.cnt[sem])
            if self.serialize:
                self.last = tok
            return tok
        return None

    def tok(self, e):
        return (e, self.cnt[e])

    def check(self):
        ptr = {e: 0 for e in self.ENGS}
        done = {s: 0 for s in self.SEMS}
        progress = True
        while progress:
            progress = False
            for e in self.ENGS:
                while ptr[e] < len(self.q[e]):
                    waits, _, inc, amt, sem = self.q[e][ptr[e]]
                    if any(done[we] < wv for we, wv in waits):
                        break
                    if inc:
                        done[sem] += amt
                    ptr[e] += 1
                    progress = True
        stuck = {e: (ptr[e], len(self.q[e])) for e in self.ENGS if ptr[e] < len(self.q[e])}
        if stuck:
            msgs = []
            for e in stuck:
                waits, _, _, _, _ = self.q[e][ptr[e]]
                unmet = [(we, wv, done[we]) for we, wv in waits if done[we] < wv]
                msgs.append(f"{e}@{ptr[e]}/{len(self.q[e])} needs {unmet}")
            raise RuntimeError("deadlock: " + "; ".join(msgs))


def build_program(T, dbg=False, serialize=False, stop_stage='out'):
    nt = T // TT_TILE
    nc = bass.Bass("TRN2", target_bir_lowering=False, detect_race_conditions=False)

    def din(name, shape, dtype=mybir.dt.float32):
        return nc.dram_tensor(name, list(shape), dtype, kind="ExternalInput").ap()

    x_d = din("x", [T, C])
    out_d = nc.dram_tensor("out", [T, C], mybir.dt.float32, kind="ExternalOutput").ap()
    wk_d = din("WkT", [128, 8192]); wv_d = din("WvT", [128, 8192])
    wr_d = din("WrT", [128, 8192]); wcr_d = din("WcrT", [128, 8192])
    wck_d = din("WckT", [128, 32768]); wcv_d = din("WcvT", [128, 32768])
    u1_d = din("U1T", [128, 2048]); v1_d = din("V1T", [128, 8192])
    u2_d = din("U2T", [128, 8192]); v2_d = din("V2T", [128, 2048])
    fmr_d = din("fmrows", [128, 8, 9])
    fb1_d = din("fb1fm", [128, 32])
    wrow_d = din("wrow", [1, 1024]); urow_d = din("urow", [1, 1024])
    fb2_d = din("fb2row", [1, 1024])
    Rf_d = din("Rf32", [128, 128]); Rb_d = din("Rbf", [128, 128], BF16)
    id_d = din("identbf", [128, 128], BF16)
    iota_d = din("iotacol", [128, 1]); ones1_d = din("ones1", [1, 128])
    onescb_d = din("onescb", [128, 1], BF16); onescf_d = din("onescf", [128, 1])
    wscr = {}
    for nm, w in [("WkT", 8192), ("WvT", 8192), ("WrT", 8192), ("WcrT", 8192),
                  ("WckT", 32768), ("WcvT", 32768), ("U1T", 2048), ("V1T", 8192),
                  ("U2T", 8192), ("V2T", 2048)]:
        wscr[nm] = nc.dram_tensor("s_" + nm, [128, w], BF16).ap()
    wraw = {"WkT": wk_d, "WvT": wv_d, "WrT": wr_d, "WcrT": wcr_d, "WckT": wck_d,
            "WcvT": wcv_d, "U1T": u1_d, "V1T": v1_d, "U2T": u2_d, "V2T": v2_d}
    dbg_d = {}
    if dbg:
        for nm in ["h1", "x1", "attn", "x2"]:
            dbg_d[nm] = nc.dram_tensor("dbg_" + nm, [T, C], mybir.dt.float32,
                                       kind="ExternalOutput").ap()

    es = ExitStack()
    with es:
        sb = lambda n, s, d=F32: es.enter_context(nc.sbuf_tensor(n, list(s), d))
        pst = lambda n, s, d=F32: es.enter_context(nc.psum_tensor(n, list(s), d))
        POOLW = sb("poolw", [128, 24576], BF16)
        SA = sb("sarena", [128, 11, 1024])
        SAb = SA.bitcast(BF16)
        GB = sb("gbpair", [128, 2048], BF16)
        MIX = sb("mix", [128, 3, 8, 512], BF16)
        TRF = sb("trfm", [128, 8, 513], BF16)
        TMS = sb("tmstage", [128, 1024], BF16)
        XC = sb("xc", [128, 1024])
        H1 = sb("h1t", [128, 4, 1024])
        KCA = sb("kcarena", [128, 16384], BF16)
        PQB = sb("pqbuf", [128, 1024], BF16)
        ROWA = sb("rowa", [128, 1024]); ROWB = sb("rowb", [128, 1024])
        ROWC = sb("rowc", [128, 1024])
        STAT = sb("stat", [128, 64])
        IOTW = sb("iotw", [128, 1024]); EU = sb("eu", [128, 1024])
        RF = sb("rf", [128, 128]); RB = sb("rb", [128, 128], BF16)
        IDB = sb("idb", [128, 128], BF16)
        ONES1 = sb("onesone", [1, 128]); IOTAC = sb("iotac", [128, 1])
        ONESCB = sb("onescbs", [128, 1], BF16); ONESCF = sb("onescfs", [128, 1])
        MIXCO = sb("mixco", [128, 8, 17])
        FB1 = sb("fb1", [128, 32])
        CAR1 = sb("car1", [128, 8, 1]); CAR2 = sb("car2", [128, 8, 1])
        CONSTS = sb("consts", [128, 3])
        nc.const_aps.register(F32, 1e-5, CONSTS[:, 0:1])
        nc.const_aps.register(F32, 1e-38, CONSTS[:, 1:2])
        nc.const_aps.register(F32, 1.0, CONSTS[:, 2:3])
        PA = pst("PA", [128, 1024]); PB = pst("PB", [128, 1024])
        PT = pst("PT", [128, 1024], BF16)
        PTOT = pst("PTOT", [1, 1024])
        sem = {e: es.enter_context(nc.semaphore("sem" + e)) for e in Sched.SEMS}
        blk = es.enter_context(nc.Block())

        K = Sched(serialize=serialize)
        E = K.emit
        FMR = SA[:, 10, 0:72].rearrange("p (a k) -> p a k", a=8)
        r_phc = ROWA[0:1, :]; r_phat = ROWA[32:33, :]; r_s0 = ROWA[64:65, :]
        r_mu = ROWB[0:1, :]; r_nhat = ROWB[32:33, :]; r_s1 = ROWB[64:65, :]
        r_nu = ROWC[0:1, :]; r_Lw = ROWC[32:33, :]; r_s2 = ROWC[64:65, :]
        r_wu = ROWA[0:1, :]; r_fb2 = ROWC[0:1, :]; r_s3 = ROWB[64:65, :]
        r_phat32 = ROWA[32:33, :]; r_lnt2 = ROWB[32:33, :]
        SL = lambda i: SA[:, i, :]
        SLh = lambda i, h: SA[:, i, 512 * h:512 * (h + 1)]

        ldt = []
        for dst, src in [(RF[:], Rf_d), (RB[:], Rb_d), (IDB[:], id_d),
                         (ONES1[:], ones1_d), (IOTAC[:], iota_d),
                         (ONESCB[:], onescb_d), (ONESCF[:], onescf_d),
                         (FMR, fmr_d), (FB1[:], fb1_d),
                         (r_wu, wrow_d), (r_s3, urow_d)]:
            ldt.append(E("D", lambda d, o=dst, s=src: d.dma_start(out=o, in_=s),
                         inc=True, dma=True))
        tld = ldt[-1]
        t_lw = E("S", lambda s: s.activation(r_Lw, r_wu, AF.Identity, scale=float(L)),
                 waits=[tld], inc=True)
        t_eur = E("S", lambda s: s.activation(r_mu, r_s3, AF.Exp), inc=True)
        E("T", lambda t: t.matmul(PA[:, 0:512], ONES1[:], r_mu[:, 0:512],
                                  start=True, stop=True), waits=[t_eur])
        t_mm1 = E("T", lambda t: t.matmul(PA[:, 512:1024], ONES1[:], r_mu[:, 512:1024],
                                          start=True, stop=True), inc=True)
        E("T", lambda t: t.matmul(PB[:, 0:512], ONES1[:], r_wu[:, 0:512],
                                  start=True, stop=True), waits=[tld])
        t_mm2 = E("T", lambda t: t.matmul(PB[:, 512:1024], ONES1[:], r_wu[:, 512:1024],
                                          start=True, stop=True), inc=True)
        t_eu = E("V", lambda v: v.tensor_copy(EU[:], PA[:]), waits=[t_mm1], inc=True)
        t_iotw = E("V", lambda v: v.tensor_scalar(IOTW[:], PB[:], IOTAC[:, 0:1], None,
                                                  ALU.mult, ALU.bypass),
                   waits=[t_mm2], inc=True)
        fv = lambda i: FMR[:, :, i]
        mco = lambda i: MIXCO[:, :, i]
        tpm = None
        for dsti, ai, bi in [(0, 0, 5), (2, 1, 5), (4, 2, 5), (7, 3, 7), (9, 4, 7)]:
            tpm = E("V", lambda v, d=dsti, a=ai, b=bi: v.tensor_tensor(mco(d), fv(a), fv(b),
                                                                       ALU.mult),
                    waits=[tld], inc=True)
        for dsti, gi, pi in [(1, 5, 0), (3, 5, 2), (5, 5, 4), (8, 7, 7), (10, 7, 9)]:
            tpm = E("V", lambda v, d=dsti, g=gi, p=pi: v.tensor_tensor(mco(d), fv(g), mco(p),
                                                                       ALU.subtract),
                    waits=[tpm], inc=True)
        E("V", lambda v: v.tensor_copy(mco(6), fv(6)))
        E("V", lambda v: v.tensor_copy(mco(11), fv(8)))
        for dsti, ri, bi in [(12, 0, 6), (13, 1, 6), (14, 2, 6), (15, 3, 8), (16, 4, 8)]:
            tpm = E("V", lambda v, d=dsti, r=ri, b=bi: v.tensor_tensor(mco(d), fv(r), fv(b),
                                                                       ALU.mult),
                    waits=[tpm], inc=True)
        E("V", lambda v: v.memset(CONSTS[:, 0:1], 1e-5))
        E("V", lambda v: v.memset(CONSTS[:, 1:2], 1e-38))
        E("V", lambda v: v.memset(CONSTS[:, 2:3], 1.0))
        E("V", lambda v: v.memset(r_phat, -10000.0))
        E("V", lambda v: v.memset(r_nhat, 0.0))
        E("V", lambda v: v.memset(CAR1[:], 0.0))
        t_init = E("V", lambda v: v.memset(CAR2[:], 0.0), inc=True)

        jobs = []
        for nm in ["WkT", "WvT", "WrT", "WcrT", "WckT", "WcvT", "U1T", "V1T", "U2T", "V2T"]:
            tot = wraw[nm].shape[1]
            for c0 in range(0, tot, 2048):
                jobs.append((wraw[nm][:, c0:c0 + 2048], wscr[nm][:, c0:c0 + 2048]))
        prev_out = [None, None]
        t_cast_done = None
        for i, (srcap, dstap) in enumerate(jobs):
            b = i % 2
            stg = SA[:, 2 * b:2 * b + 2, :].rearrange("p a k -> p (a k)")
            tin = E("D", lambda d, s=srcap, g=stg: d.dma_start(out=g, in_=s),
                    waits=[prev_out[b]], inc=True, dma=True)
            tout = E("G", lambda g, s=stg, dd=dstap: g.dma_start(out=dd, in_=s),
                     waits=[tin], inc=True, dma=True)
            prev_out[b] = tout
            t_cast_done = tout

        last_ln = [t_init]

        def ln_stats(src_ap, idx, tok_in, psum_src=False):
            sm = STAT[:, 4 * idx:4 * idx + 1]
            sq = STAT[:, 4 * idx + 2:4 * idx + 3]
            tsq = E("S", lambda s, a=src_ap: s.activation(SL(10), a, AF.Square),
                    waits=list(tok_in) + [last_ln[0]], inc=True)
            t1 = E("V", lambda v, a=src_ap, o=sm: v.reduce_sum(o, a, axis=mybir.AxisListType.X),
                   waits=list(tok_in) + [last_ln[0]] + ([tsq] if psum_src else []), inc=True)
            t2 = E("V", lambda v, o=sq: v.reduce_sum(o, SL(10), axis=mybir.AxisListType.X),
                   waits=[tsq], inc=True)
            m = STAT[:, 32:33]; v2 = STAT[:, 36:37]; rstd = STAT[:, 40:41]
            nmr = STAT[:, 44:45]; tmp = STAT[:, 48:49]; tmp2 = STAT[:, 52:53]
            ta = E("V", lambda v: v.tensor_single_scalar(m, sm, 1.0 / 1024.0, ALU.mult),
                   waits=[t1, t2], inc=True)
            tb = E("V", lambda v: v.tensor_single_scalar(v2, sq, 1.0 / 1024.0, ALU.mult),
                   waits=[ta], inc=True)
            tc2 = E("V", lambda v: v.tensor_tensor(tmp, m, m, ALU.mult), waits=[tb], inc=True)
            td2 = E("V", lambda v: v.tensor_tensor(v2, v2, tmp, ALU.subtract),
                    waits=[tc2], inc=True)
            tv = E("V", lambda v: v.tensor_single_scalar(v2, v2, 1e-5, ALU.add),
                   waits=[td2], inc=True)
            E("S", lambda s: s.activation(tmp2, v2, AF.Ln), waits=[tv])
            tr = E("S", lambda s: s.activation(rstd, tmp2, AF.Exp, scale=-0.5), inc=True)
            te2 = E("V", lambda v: v.tensor_tensor(tmp, m, rstd, ALU.mult), waits=[tr], inc=True)
            tn = E("V", lambda v: v.tensor_single_scalar(nmr, tmp, -1.0, ALU.mult),
                   waits=[te2], inc=True)
            last_ln[0] = tn
            return [tr, tn], rstd, nmr

        def transpose_to_fm(tm_bf_ap, dst_fm_ap, waits, pt_free):
            tt = None
            for a in range(8):
                tt = E("T", lambda t, a=a, s=tm_bf_ap: t.transpose(
                    PT[:, 128 * a:128 * (a + 1)], s[:, 128 * a:128 * (a + 1)], IDB[:]),
                    waits=(list(waits) + [pt_free]) if a == 0 else (), inc=(a == 7))
            return E("S", lambda s, d=dst_fm_ap: s.copy(
                d, PT[:, :].rearrange("p (a t) -> p a t", a=8)), waits=[tt], inc=True)

        tok_PA_free = t_eu; tok_PB_free = t_iotw
        tok_PT_free = K.tok("V")
        tok_PTOT_free = t_init
        tok_x0_free = None
        tok_h1_free = None
        tok_nu_guard = None
        tok_prol_guard = t_eur
        tok_mix_free = None
        tok_state = t_init
        tok_trf_free = None
        tok_gb_free = None
        tok_gb_free2 = None
        tok_sl_free = {i: None for i in range(11)}
        tok_poolw_free = None

        x_r2 = x_d.rearrange("(n tc p) c -> n tc p c", p=128, tc=4)
        out_r = out_d.rearrange("(n tc p) c -> n tc p c", p=128, tc=4)
        dbg_r = {k2: v2.rearrange("(n tc p) c -> n tc p c", p=128, tc=4)
                 for k2, v2 in dbg_d.items()}

        def load_w(dst2d, nm, waits):
            return E("D", lambda d, o=dst2d, s=wscr[nm]: d.dma_start(out=o, in_=s),
                     waits=waits, inc=True, dma=True)

        for n in range(nt):
            wk_sb = POOLW[:, 0:8192]; wv_sb = POOLW[:, 8192:16384]; wr_sb = POOLW[:, 16384:24576]
            t_wk = load_w(wk_sb, "WkT", [tok_poolw_free, t_cast_done])
            t_wv = load_w(wv_sb, "WvT", [])
            t_wr = load_w(wr_sb, "WrT", [])
            wk3 = wk_sb.rearrange("p (a k) -> p a k", a=8)
            wv3 = wv_sb.rearrange("p (a k) -> p a k", a=8)
            wr3 = wr_sb.rearrange("p (a k) -> p a k", a=8)

            tcar = E("V", lambda v: v.tensor_copy(TRF[:, :, 0:1], CAR1[:]),
                     waits=[tok_trf_free, t_init], inc=True)
            t_h1 = []
            t_xc = None
            for tc in range(4):
                x0c = XC[:, :]; h1c = H1[:, tc, :]
                t_x = E("D", lambda d, n=n, tc=tc: d.dma_start(out=XC[:], in_=x_r2[n][tc]),
                        waits=[t_xc, tok_x0_free] + ([t_cast_done] if tc == 0 else []),
                        inc=True, dma=True)
                st, rstd, nmr = ln_stats(x0c, 0, [t_x])
                th = E("S", lambda s, a=x0c, o=h1c, r=rstd, nm=nmr: s.activation(
                    o, a, AF.Identity, scale=r, bias=nm), waits=st + [tok_h1_free], inc=True)
                t_xc = th
                if stop_stage == 'h1x':
                    t_h1.append(th)
                    t_xc = th
                    continue
                st2, rstd2, nmr2 = ln_stats(h1c, 1, [th])
                tx = E("S", lambda s, a=h1c, r=rstd2, nm=nmr2: s.activation(
                    TMS[:], a, AF.Identity, scale=r, bias=nm), waits=st2, inc=True)
                te = transpose_to_fm(TMS[:], TRF[:, :, 1 + 128 * tc:1 + 128 * (tc + 1)],
                                     [tx, tcar], tok_PT_free)
                tok_PT_free = te
                t_h1.append(th)
                if dbg:
                    E("G", lambda g, n=n, tc=tc, a=h1c: g.dma_start(
                        out=dbg_r["h1"][n][tc], in_=a), waits=[th], inc=True, dma=True)
            t_trf = tok_PT_free

            def stage_dump(waits):
                tt = None
                for tc2 in range(4):
                    tt = E("G", lambda g, n=n, tc2=tc2: g.dma_start(out=out_r[n][tc2],
                                                                   in_=H1[:, tc2, :]),
                           waits=list(waits), inc=True, dma=True)
                return tt

            if stop_stage in ('h1', 'h1x'):
                stage_dump([K.tok("S"), K.tok("V")])
                tok_x0_free = K.tok("G"); tok_h1_free = K.tok("G")
                tok_poolw_free = K.tok("T"); tok_mix_free = K.tok("T")
                tok_trf_free = t_trf
                continue

            tmix = None
            for mi, (cg, cg1) in enumerate([(0, 1), (2, 3), (4, 5)]):
                for a in range(8):
                    cur = TRF[:, a, 1:513]; prv = TRF[:, a, 0:512]
                    o = MIX[:, mi, a, :]
                    E("V", lambda v, c=cur, g=cg, a=a: v.tensor_scalar(
                        SLh(10, 0), c, MIXCO[:, a, g:g + 1], MIXCO[:, a, 6:7],
                        ALU.mult, ALU.add),
                      waits=[t_trf, tok_mix_free] if (mi == 0 and a == 0) else ())
                    E("V", lambda v, p=prv, g=cg1, a=a: v.tensor_scalar(
                        SLh(10, 1), p, MIXCO[:, a, g:g + 1], None, ALU.mult, ALU.bypass))
                    tmix = E("V", lambda v, o=o: v.tensor_tensor(o, SLh(10, 0), SLh(10, 1),
                                                                 ALU.add), inc=True)
                    if n == 0:
                        tmix = E("V", lambda v, o=o, c=cur, g=cg, a=a, mi=mi: v.tensor_scalar(
                            o[:, 0:1], c[:, 0:1], MIXCO[:, a, g:g + 1],
                            MIXCO[:, a, 12 + mi:13 + mi], ALU.mult, ALU.add), inc=True)
            tcarS = E("V", lambda v: v.tensor_copy(CAR1[:], TRF[:, :, 512:513]), inc=True)
            t_mixA = tmix

            tzq_prev = None
            for tc in range(NCHUNK):
                def ddump(name, ap, tok, dt2=F32, _n=n, _tc=tc):
                    if not (dbg and _n == 0 and _tc == 0):
                        return None
                    dd = nc.dram_tensor("dbgw_" + name, list(ap.shape), dt2,
                                        kind="ExternalOutput").ap()
                    return E("G", lambda g, a=ap, o=dd: g.dma_start(out=o, in_=a),
                             waits=[tok], inc=True, dma=True)
                ts1 = E("V", lambda v: v.tensor_single_scalar(r_phc, r_phat, PC_CLAMP, ALU.max),
                        waits=[tok_state, tok_prol_guard], inc=True)
                tok_prol_guard = None
                ts2 = E("V", lambda v: v.tensor_scalar(r_s0, r_phat, -PC_CLAMP, 0.0,
                                                       ALU.add, ALU.min), waits=[ts1], inc=True)
                ts3a = E("S", lambda s: s.activation(r_mu, r_s0, AF.Exp), waits=[ts2], inc=True)
                ts3 = E("V", lambda v: v.tensor_single_scalar(r_mu, r_mu, 1e-38, ALU.add),
                       waits=[ts3a], inc=True)
                ts4a = E("V", lambda v: v.tensor_copy(r_nu, r_nhat),
                         waits=[tok_nu_guard, ts3], inc=True)
                ts4 = E("V", lambda v: v.tensor_tensor(r_nu, r_mu, r_nu, ALU.mult),
                        waits=[ts4a], inc=True)
                tok_nu_guard = None
                tbc = None
                for h in range(2):
                    tbc = E("T", lambda t, h=h: t.matmul(
                        PA[:, 512 * h:512 * (h + 1)], ONES1[:],
                        r_phc[:, 512 * h:512 * (h + 1)], start=True, stop=True),
                        waits=[ts2, tok_PA_free] if h == 0 else (), inc=(h == 1))
                tk = None
                for h in range(2):
                    for a in range(8):
                        tk = E("T", lambda t, h=h, a=a, tc=tc: t.matmul(
                            PB[:, 512 * h:512 * (h + 1)],
                            MIX[:, 0, a, 128 * tc:128 * (tc + 1)],
                            wk3[:, a, 512 * h:512 * (h + 1)],
                            start=(a == 0), stop=(a == 7)),
                            waits=[t_mixA, t_wk, tok_PB_free] if (h == 0 and a == 0) else (),
                            inc=(h == 1 and a == 7))
                tv1 = E("V", lambda v: v.tensor_tensor(SL(0), PB[:], IOTW[:], ALU.subtract),
                        waits=[tk, tok_sl_free[0]], inc=True)
                tz = E("V", lambda v: v.tensor_tensor(SL(1), SL(0), PA[:], ALU.subtract),
                       waits=[tbc, tok_sl_free[1]], inc=True)
                tok_PB_free = tz; tok_PA_free = tz
                tvp = None
                for h in range(2):
                    for a in range(8):
                        tvp = E("T", lambda t, h=h, a=a, tc=tc: t.matmul(
                            PA[:, 512 * h:512 * (h + 1)],
                            MIX[:, 1, a, 128 * tc:128 * (tc + 1)],
                            wv3[:, a, 512 * h:512 * (h + 1)],
                            start=(a == 0), stop=(a == 7)),
                            waits=[tz, t_wv] if (h == 0 and a == 0) else (),
                            inc=(h == 1 and a == 7))
                tg = E("S", lambda s: s.activation(SL(2), SL(1), AF.Exp),
                       waits=[tz, tok_sl_free[2]], inc=True)
                tgb = E("S", lambda s: s.copy(GB[:, 0:1024], SL(2)),
                        waits=[tg, tok_gb_free, tok_gb_free2], inc=True)
                tvf = E("S", lambda s: s.copy(SL(3), PA[:]), waits=[tvp, tok_sl_free[3]],
                        inc=True)
                tok_PA_free = tvf
                trp = None
                for h in range(2):
                    for a in range(8):
                        trp = E("T", lambda t, h=h, a=a, tc=tc: t.matmul(
                            PA[:, 512 * h:512 * (h + 1)],
                            MIX[:, 2, a, 128 * tc:128 * (tc + 1)],
                            wr3[:, a, 512 * h:512 * (h + 1)],
                            start=(a == 0), stop=(a == 7)),
                            waits=[tvf, t_wr] if (h == 0 and a == 0) else (),
                            inc=(h == 1 and a == 7))
                tF = None
                for h in range(2):
                    E("T", lambda t, h=h: t.matmul(
                        PB[:, 512 * h:512 * (h + 1)], RB[:], GB[:, 512 * h:512 * (h + 1)],
                        start=True, stop=False), waits=[tgb] if h == 0 else ())
                    tF = E("T", lambda t, h=h: t.matmul(
                        PB[:, 512 * h:512 * (h + 1)], ONES1[:],
                        r_mu[:, 512 * h:512 * (h + 1)], start=False, stop=True),
                        waits=[ts3] if h == 0 else (), inc=(h == 1))
                tFt = None
                for h in range(2):
                    tFt = E("T", lambda t, h=h: t.matmul(
                        PTOT[0:1, 512 * h:512 * (h + 1)], ONESCB[:],
                        GB[:, 512 * h:512 * (h + 1)], start=True, stop=True),
                        waits=[tok_PTOT_free] if h == 0 else (), inc=(h == 1))
                ter = E("S", lambda s: s.activation(SL(4), PA[:], AF.Exp, scale=-1.0),
                        waits=[trp, tok_sl_free[4]], inc=True)
                tok_PA_free = ter
                tFE = E("V", lambda v: v.tensor_tensor(SL(5), PB[:], GB[:, 0:1024],
                                                       ALU.subtract),
                        waits=[tF, tok_sl_free[5]], inc=True)
                tok_PB_free = tFE
                tt3 = E("S", lambda s: s.activation(SL(6), SL(5), AF.Ln),
                        waits=[tFE, tok_sl_free[6]], inc=True)
                td = E("G", lambda g: g.tensor_tensor(SL(7), SL(6), SL(1), ALU.subtract),
                       waits=[tt3, tzq_prev, tok_sl_free[7]], inc=True)
                tad = E("S", lambda s: s.activation(SL(6), SL(7), AF.Abs),
                        waits=[td], inc=True)
                ted = E("S", lambda s: s.activation(SL(7), SL(6), AF.Exp, scale=-1.0),
                        waits=[tad], inc=True)
                tp1 = E("V", lambda v: v.tensor_single_scalar(SL(7), SL(7), 1.0, ALU.add),
                        waits=[ted], inc=True)
                tgam = E("S", lambda s: s.activation(SL(6), SL(7), AF.Ln), waits=[tp1], inc=True)
                tst5 = E("V", lambda v: v.tensor_tensor(r_s0, r_mu, PTOT[0:1, :], ALU.add),
                         waits=[tFt], inc=True)
                tst5b = E("V", lambda v: v.reciprocal(r_s1, r_s0), waits=[tst5], inc=True)
                tst6 = E("S", lambda s: s.activation(r_lnt2, r_s0, AF.Ln), waits=[tst5b],
                         inc=True)
                tst7a = E("V", lambda v: v.tensor_copy(r_phat32, r_phc), waits=[tst6], inc=True)
                tst7b = E("V", lambda v: v.tensor_tensor(r_phat32, r_phat32, r_Lw, ALU.add),
                         waits=[tst7a], inc=True)
                tst8 = E("V", lambda v: v.tensor_tensor(r_phat, r_phat32, r_lnt2, ALU.add),
                        waits=[tst7b], inc=True)
                tQ = None
                for h in range(2):
                    tQ = E("T", lambda t, h=h: t.matmul(
                        PB[:, 512 * h:512 * (h + 1)], RF[:], SLh(6, h), start=True, stop=True),
                        waits=[tgam, tFE] if h == 0 else (), inc=(h == 1))
                tQt = None
                for h in range(2):
                    tQt = E("T", lambda t, h=h: t.matmul(
                        PTOT[0:1, 512 * h:512 * (h + 1)], ONESCF[:], SLh(6, h),
                        start=True, stop=True),
                        waits=[tst5, tgam] if h == 0 else (), inc=(h == 1))
                tqp = E("V", lambda v: v.tensor_tensor(SL(8), PB[:], SL(6), ALU.subtract),
                        waits=[tQ, tok_sl_free[8]], inc=True)
                tok_PB_free = tqp
                tEQ = E("S", lambda s: s.activation(SL(9), SL(8), AF.Exp),
                        waits=[tqp, tok_sl_free[9]], inc=True)
                dz = ddump("z", SL(1), tz)
                dg = ddump("g", SL(2), tg)
                dfe = ddump("fe", SL(5), tFE)
                dgam = ddump("gam", SL(6), tgam)
                dqp = ddump("qp", SL(8), tqp)
                tzq = E("V", lambda v: v.tensor_tensor(SL(1), SL(1), SL(8), ALU.subtract),
                        waits=[tqp, dz], inc=True)
                tgt = E("S", lambda s: s.activation(SL(0), SL(1), AF.Exp), waits=[tzq],
                        inc=True)
                tgtv = E("G", lambda g: g.tensor_tensor(GB[:, 1024:2048], SL(0), SL(3),
                                                        ALU.mult),
                         waits=[tgt, tvf, tok_gb_free, tok_gb_free2], inc=True)
                tP = None
                for h in range(2):
                    E("T", lambda t, h=h: t.matmul(
                        PA[:, 512 * h:512 * (h + 1)], RB[:],
                        GB[:, 1024 + 512 * h:1024 + 512 * (h + 1)], start=True, stop=False),
                      waits=[tgtv, ter] if h == 0 else ())
                    tP = E("T", lambda t, h=h: t.matmul(
                        PA[:, 512 * h:512 * (h + 1)], ONES1[:],
                        r_nu[:, 512 * h:512 * (h + 1)], start=False, stop=True),
                        waits=[ts4] if h == 0 else (), inc=(h == 1))
                tst10 = E("S", lambda s: s.activation(r_s2, PTOT[0:1, :], AF.Exp),
                          waits=[tQt], inc=True)
                tPt = None
                for h in range(2):
                    tPt = E("T", lambda t, h=h: t.matmul(
                        PTOT[0:1, 512 * h:512 * (h + 1)], ONESCB[:],
                        GB[:, 1024 + 512 * h:1024 + 512 * (h + 1)], start=True, stop=True),
                        waits=[tst10, tgtv] if h == 0 else (), inc=(h == 1))
                deq = ddump("eq", SL(9), tEQ)
                dgtv = ddump("gtv", GB[:, 1024:2048], tgtv, BF16)
                ten = E("V", lambda v: v.tensor_tensor(SL(7), EU[:], SL(2), ALU.mult),
                        waits=[tgam], inc=True)
                tden = E("V", lambda v: v.tensor_tensor(SL(5), SL(5), SL(7), ALU.add),
                         waits=[ten, tt3, dfe], inc=True)
                te1p = E("V", lambda v: v.tensor_single_scalar(SL(4), SL(4), 1.0, ALU.add),
                         waits=[ter], inc=True)
                tden2 = E("V", lambda v: v.tensor_tensor(SL(5), SL(5), SL(4), ALU.mult),
                          inc=True)
                dden2 = ddump("den2", SL(5), tden2)
                trec = E("V", lambda v: v.reciprocal(SL(8), SL(5)),
                         waits=[tEQ, dqp, dden2], inc=True)
                tn1 = E("V", lambda v: v.tensor_tensor(SL(6), PA[:], GB[:, 1024:2048],
                                                       ALU.subtract),
                       waits=[tP, tQt, dgam], inc=True)
                tok_PA_free = tn1
                tn2 = E("V", lambda v: v.tensor_tensor(SL(6), SL(6), SL(9), ALU.mult),
                        inc=True)
                tdn = E("V", lambda v: v.tensor_tensor(SL(7), SL(7), SL(3), ALU.mult),
                       waits=[tden], inc=True)
                tnum = E("V", lambda v: v.tensor_tensor(SL(6), SL(6), SL(7), ALU.add),
                         inc=True)
                dnum = ddump("num", SL(6), tnum)
                tyz = E("V", lambda v: v.tensor_tensor(SL(6), SL(6), SL(8), ALU.mult),
                        waits=[trec, dnum], inc=True)
                tx1 = E("V", lambda v, tc=tc: v.tensor_tensor(
                    H1[:, tc, :], H1[:, tc, :], SL(6), ALU.add),
                    waits=[t_h1[tc]], inc=True)
                tst11 = E("V", lambda v: v.tensor_tensor(r_s0, r_nu, PTOT[0:1, :], ALU.add),
                          waits=[tPt, tst8], inc=True)
                tok_PTOT_free = tst11
                tst12 = E("V", lambda v: v.tensor_tensor(r_s0, r_s0, r_s2, ALU.mult),
                         waits=[tst11], inc=True)
                tst13 = E("V", lambda v: v.tensor_tensor(r_nhat, r_s0, r_s1, ALU.mult),
                         waits=[tst12], inc=True)
                tok_state = tst13
                tok_sl_free[0] = tgtv; tok_sl_free[1] = tgt; tok_sl_free[2] = tdn
                tok_sl_free[3] = tdn; tok_sl_free[4] = tden2; tok_sl_free[5] = trec
                tok_sl_free[6] = tyz; tok_sl_free[7] = tnum; tok_sl_free[8] = tyz
                tok_sl_free[9] = tn2
                tok_gb_free = tn1; tok_gb_free2 = tPt
                if dbg:
                    E("G", lambda g, n=n, tc=tc: g.dma_start(
                        out=dbg_r["x1"][n][tc], in_=H1[:, tc, :]),
                      waits=[tx1], inc=True, dma=True)
            t_x1 = K.tok("V")
            tok_mix_free = K.tok("T")
            tok_poolw_free = K.tok("T")

            if stop_stage == 'x1':
                stage_dump([t_x1, K.tok("S")])
                tok_x0_free = K.tok("G"); tok_h1_free = K.tok("G")
                tok_trf_free = t_trf
                continue

            tcar2 = E("V", lambda v: v.tensor_copy(TRF[:, :, 0:1], CAR2[:]),
                      waits=[tcarS], inc=True)
            for tc in range(4):
                x1c = H1[:, tc, :]
                st, rstd, nmr = ln_stats(x1c, 2, [t_x1])
                tx = E("S", lambda s, a=x1c, r=rstd, nm=nmr: s.activation(
                    TMS[:], a, AF.Identity, scale=r, bias=nm), waits=st, inc=True)
                te = transpose_to_fm(TMS[:], TRF[:, :, 1 + 128 * tc:1 + 128 * (tc + 1)],
                                     [tx, tcar2], tok_PT_free)
                tok_PT_free = te
            t_trf2 = tok_PT_free
            tmixb = None
            for mi, (cg, cg1) in enumerate([(7, 8), (9, 10)]):
                for a in range(8):
                    cur = TRF[:, a, 1:513]; prv = TRF[:, a, 0:512]
                    o = MIX[:, mi, a, :]
                    E("V", lambda v, c=cur, g=cg, a=a: v.tensor_scalar(
                        SLh(10, 0), c, MIXCO[:, a, g:g + 1], MIXCO[:, a, 11:12],
                        ALU.mult, ALU.add),
                      waits=[t_trf2, tok_mix_free] if (mi == 0 and a == 0) else ())
                    E("V", lambda v, p=prv, g=cg1, a=a: v.tensor_scalar(
                        SLh(10, 1), p, MIXCO[:, a, g:g + 1], None, ALU.mult, ALU.bypass))
                    tmixb = E("V", lambda v, o=o: v.tensor_tensor(o, SLh(10, 0), SLh(10, 1),
                                                                  ALU.add), inc=True)
                    if n == 0:
                        tmixb = E("V", lambda v, o=o, c=cur, g=cg, a=a, mi=mi: v.tensor_scalar(
                            o[:, 0:1], c[:, 0:1], MIXCO[:, a, g:g + 1],
                            MIXCO[:, a, 15 + mi:16 + mi], ALU.mult, ALU.add), inc=True)
            tcarS2 = E("V", lambda v: v.tensor_copy(CAR2[:], TRF[:, :, 512:513]), inc=True)
            t_mixB = tmixb

            wcr_sb = POOLW[:, 0:8192]
            t_wcr = load_w(wcr_sb, "WcrT", [tok_poolw_free])
            wcr3 = wcr_sb.rearrange("p (a k) -> p a k", a=8)
            wckh_sb = POOLW[:, 8192:24576]
            ECR = SAb[:, 4:6, :].rearrange("p a k -> p (a k)")
            tecr = []
            for tc in range(4):
                trc = None
                for h in range(2):
                    for a in range(8):
                        trc = E("T", lambda t, h=h, a=a, tc=tc: t.matmul(
                            PB[:, 512 * h:512 * (h + 1)],
                            MIX[:, 1, a, 128 * tc:128 * (tc + 1)],
                            wcr3[:, a, 512 * h:512 * (h + 1)],
                            start=(a == 0), stop=(a == 7)),
                            waits=[t_mixB, t_wcr, tok_PB_free] if (h == 0 and a == 0) else (),
                            inc=(h == 1 and a == 7))
                t_e = E("S", lambda s, tc=tc: s.activation(
                    ECR[:, 1024 * tc:1024 * (tc + 1)], PB[:], AF.Exp, scale=-1.0),
                    waits=[trc] + ([tok_sl_free[4], tok_sl_free[5]] if tc == 0 else []),
                    inc=True)
                tok_PB_free = t_e
                tecr.append(t_e)
            KC = KCA[:, :]
            t_kc = None
            for wh in range(2):
                twl = E("D", lambda d, wh=wh: d.dma_start(
                    out=wckh_sb, in_=wscr["WckT"][:, 16384 * wh:16384 * (wh + 1)]),
                    waits=[tok_poolw_free if wh == 0 else t_kc], inc=True, dma=True)
                wck4 = wckh_sb.rearrange("p (hs a k) -> p hs a k", hs=16, a=8)
                for hs in range(16):
                    hsg = 16 * wh + hs
                    tkp = None
                    for a in range(8):
                        tkp = E("T", lambda t, hs=hs, a=a, w4=wck4: t.matmul(
                            PA[:, 0:512], w4[:, hs, a, :], MIX[:, 0, a, :],
                            start=(a == 0), stop=(a == 7)),
                            waits=([twl, t_mixB, tok_PA_free] +
                                   ([tok_sl_free[0], tok_sl_free[1], tok_sl_free[2],
                                     tok_sl_free[3], tok_sl_free[6], tok_sl_free[7]]
                                    if (wh == 0 and hs == 0) else []))
                            if a == 0 else (), inc=(a == 7))
                    tr1 = E("V", lambda v: v.tensor_single_scalar(
                        SLh(10, 0), PA[:, 0:512], 0.0, ALU.max),
                        waits=[tkp, t_kc], inc=True)
                    t_kc = E("G", lambda g, hsg=hsg: g.tensor_tensor(
                        KC[:, 512 * hsg:512 * (hsg + 1)], SLh(10, 0), SLh(10, 0), ALU.mult),
                        waits=[tr1], inc=True)
                    tok_PA_free = t_kc

            if dbg and n == 0:
                for _nm, _ap, _tok, _dt in [("xck", MIX[:, 0, :, :], t_mixB, BF16),
                                            ("xcr", MIX[:, 1, :, :], t_mixB, BF16),
                                            ("kc", KC, t_kc, BF16),
                                            ("ecr", ECR, tecr[3], BF16)]:
                    _dd = nc.dram_tensor("dbgw_" + _nm, list(_ap.shape), _dt,
                                         kind="ExternalOutput").ap()
                    E("G", lambda g, a=_ap, o=_dd: g.dma_start(out=o, in_=a),
                      waits=[_tok], inc=True, dma=True)
            wcv_sb = POOLW[:, 0:16384]
            kcv = KC.rearrange("p (g k) -> p g k", g=32)
            tep_last = None
            tat_prev = None
            tsv_prev = None
            t_sv = {}
            for dh in range(2):
                t_wcv = E("D", lambda d, dh=dh: d.dma_start(
                    out=wcv_sb, in_=wscr["WcvT"][:, 16384 * dh:16384 * (dh + 1)]),
                    waits=[K.tok("T")], inc=True, dma=True)
                wcv3 = wcv_sb.rearrange("p (a k) -> p a k", a=32)
                for tc in range(4):
                    PX = (PA if tc % 2 == 0 else PB)
                    col = 512 * (tc // 2)
                    tvc = None
                    for hs in range(32):
                        tvc = E("T", lambda t, hs=hs, tc=tc, PX=PX, col=col, w3=wcv3: t.matmul(
                            PX[:, col:col + 512], kcv[:, hs, 128 * tc:128 * (tc + 1)],
                            w3[:, hs, :], start=(hs == 0), stop=(hs == 31)),
                            waits=([t_wcv, t_kc, t_sv.get((dh - 1, tc)),
                                    t_sv.get((dh, tc - 2))] +
                                   ([tok_PB_free] if (dh == 0 and tc == 1) else []))
                            if hs == 0 else (),
                            inc=(hs == 31))
                    tep = E("G", lambda g, tc=tc, dh=dh: g.tensor_single_scalar(
                        SLh(6, 0), ECR[:, 1024 * tc + 512 * dh:1024 * tc + 512 * dh + 512],
                        1.0, ALU.add),
                        waits=[tecr[tc], tvc, tat_prev] +
                              ([tok_sl_free[6]] if (dh == 0 and tc == 0) else []),
                        inc=True)
                    trc2 = E("V", lambda v: v.reciprocal(SLh(7, 0), SLh(6, 0)),
                             waits=[tep, tsv_prev] +
                                   ([tok_sl_free[7]] if (dh == 0 and tc == 0) else []),
                             inc=True)
                    tsv = E("V", lambda v, PX=PX, col=col: v.tensor_tensor(
                        SLh(6, 0), PX[:, col:col + 512], SLh(7, 0), ALU.mult),
                        waits=[trc2], inc=True)
                    tsv_prev = tsv
                    tat = E("G", lambda g, tc=tc, dh=dh: g.tensor_tensor(
                        H1[:, tc, 512 * dh:512 * dh + 512], H1[:, tc, 512 * dh:512 * dh + 512],
                        SLh(6, 0), ALU.add), waits=[tsv], inc=True)
                    tat_prev = tat
                    t_sv[(dh, tc)] = tsv
                    tep_last = tat
                    if dbg:
                        E("G", lambda g, n=n, tc=tc: g.dma_start(
                            out=dbg_r["attn"][n][tc], in_=H1[:, tc, :]),
                          waits=[tat], inc=True, dma=True)
            tok_PB_free = tep_last; tok_PA_free = tep_last
            t_attn = K.tok("V")
            tok_sl_free[4] = tep_last; tok_sl_free[5] = tep_last
            tok_sl_free[6] = tep_last; tok_sl_free[7] = tep_last
            tok_poolw_free = K.tok("T")

            if stop_stage == 'attn':
                stage_dump([t_attn, K.tok("S")])
                tok_x0_free = K.tok("G"); tok_h1_free = K.tok("G")
                tok_trf_free = t_trf
                continue

            prev_c = None
            for tc in range(4):
                ac = H1[:, tc, :]
                t_x = E("D", lambda d, n=n, tc=tc: d.dma_start(out=XC[:], in_=x_r2[n][tc]),
                        waits=[t_xc, prev_c], inc=True, dma=True)
                st, rstd, nmr = ln_stats(ac, 3, [t_attn, prev_c])
                tnm = E("S", lambda s, a=ac, r=rstd, nm=nmr: s.activation(
                    SL(10), a, AF.Identity, scale=r, bias=nm), waits=st, inc=True)
                tx2 = E("V", lambda v, tc=tc: v.tensor_tensor(H1[:, tc, :], XC[:], SL(10),
                                                              ALU.add),
                        waits=[tnm, t_x], inc=True)
                t_xc = tx2
                st2, rstd2, nmr2 = ln_stats(ac, 4, [tx2])
                th2 = E("S", lambda s, x=ac, r=rstd2, nm=nmr2: s.activation(
                    TMS[:], x, AF.Identity, scale=r, bias=nm), waits=st2, inc=True)
                te = transpose_to_fm(TMS[:], MIX[:, 0, :, 128 * tc:128 * (tc + 1)],
                                     [th2, K.tok("T")], tok_PT_free)
                tok_PT_free = te
                prev_c = tx2
                if dbg:
                    E("G", lambda g, n=n, tc=tc: g.dma_start(
                        out=dbg_r["x2"][n][tc], in_=H1[:, tc, :]),
                      waits=[tx2], inc=True, dma=True)
            t_h2f = tok_PT_free
            t_x2 = K.tok("V")

            if stop_stage == 'x2':
                stage_dump([t_x2, K.tok("S"), K.tok("T")])
                tok_x0_free = K.tok("G"); tok_h1_free = K.tok("G")
                tok_trf_free = t_trf
                continue

            u1_sb = POOLW[:, 0:2048]; v1_sb = POOLW[:, 2048:10240]
            u2_sb = POOLW[:, 10240:18432]; v2_sb = POOLW[:, 18432:20480]
            t_u1 = load_w(u1_sb, "U1T", [tok_poolw_free])
            t_v1 = load_w(v1_sb, "V1T", [])
            t_u2 = load_w(u2_sb, "U2T", [])
            t_v2 = load_w(v2_sb, "V2T", [])
            PBUF = PQB[:, :].rearrange("p (a k) -> p a k", a=2)
            QBUF = PBUF
            u13 = u1_sb.rearrange("p (a k) -> p a k", a=8)
            v13 = v1_sb.rearrange("p (a k) -> p a k", a=2)
            u23 = u2_sb.rearrange("p (a k) -> p a k", a=32)
            v23 = v2_sb.rearrange("p (a k) -> p a k", a=2)
            for rs in range(2):
                for a in range(8):
                    E("T", lambda t, rs=rs, a=a: t.matmul(
                        PA[:, 512 * rs:512 * (rs + 1)], u13[:, a, 128 * rs:128 * (rs + 1)],
                        MIX[:, 0, a, :], start=(a == 0), stop=(a == 7)),
                        waits=[t_u1, t_h2f, tok_PA_free] if (rs == 0 and a == 0) else (),
                        inc=(rs == 1 and a == 7))
            tpb = E("S", lambda s: s.copy(PQB[:, :], PA[:]),
                    waits=[K.tok("T")], inc=True)
            tok_PA_free = tpb
            if stop_stage == 'ffp':
                stage_dump([tpb, K.tok("S"), K.tok("T")])
                tok_x0_free = K.tok("G"); tok_h1_free = K.tok("G")
                tok_trf_free = t_trf; tok_nu_guard = K.tok("T")
                continue
            HB = KC
            t_hh = None
            tsg_p = None
            tr2 = None
            for fs in range(32):
                th = None
                for rs in range(2):
                    th = E("T", lambda t, fs=fs, rs=rs: t.matmul(
                        PB[:, 0:512], v13[:, rs, 128 * fs:128 * (fs + 1)], PBUF[:, rs, :],
                        start=(rs == 0), stop=(rs == 1)),
                        waits=([t_v1, tpb, tok_PB_free] if fs == 0 else [tok_PB_free])
                        if rs == 0 else (),
                        inc=(rs == 1))
                tug = E("V", lambda v, fs=fs: v.tensor_scalar(
                    SLh(10, 0), PB[:, 0:512], FB1[:, fs:fs + 1], None, ALU.add, ALU.bypass),
                    waits=[th], inc=True)
                tok_PB_free = tug
                E("V", lambda v: v.tensor_tensor(SLh(10, 1), SLh(10, 0), SLh(10, 0), ALU.mult))
                E("V", lambda v: v.tensor_scalar(SLh(10, 1), SLh(10, 1), GC1, GC2,
                                                 ALU.mult, ALU.add))
                t3 = E("V", lambda v: v.tensor_tensor(SLh(10, 1), SLh(10, 1), SLh(10, 0),
                                                      ALU.mult), inc=True)
                tsg = E("S", lambda s: s.activation(SLh(9, 0), SLh(10, 1), AF.Exp,
                                                    scale=-2.0),
                        waits=[t3] + ([tok_sl_free[9]] if fs == 0 else []), inc=True)
                te1 = E("V", lambda v: v.tensor_single_scalar(SLh(9, 0), SLh(9, 0), 1.0,
                                                              ALU.add), waits=[tsg], inc=True)
                tr2 = E("V", lambda v: v.reciprocal(SLh(9, 1), SLh(9, 0)), inc=True)
                t_hh = E("V", lambda v, fs=fs: v.tensor_tensor(
                    HB[:, 512 * fs:512 * (fs + 1)], SLh(10, 0), SLh(9, 1), ALU.mult),
                    waits=[t_kc] if fs == 0 else [], inc=True)
            if stop_stage == 'ffh':
                stage_dump([t_hh, K.tok("S"), K.tok("T")])
                tok_x0_free = K.tok("G"); tok_h1_free = K.tok("G")
                tok_trf_free = t_trf; tok_nu_guard = K.tok("T")
                continue
            for rs in range(2):
                for fs in range(32):
                    E("T", lambda t, rs=rs, fs=fs: t.matmul(
                        PA[:, 512 * rs:512 * (rs + 1)], u23[:, fs, 128 * rs:128 * (rs + 1)],
                        HB[:, 512 * fs:512 * (fs + 1)], start=(fs == 0), stop=(fs == 31)),
                        waits=[t_u2, t_hh, tok_PA_free] if (rs == 0 and fs == 0) else (),
                        inc=(rs == 1 and fs == 31))
            tqb = E("S", lambda s: s.copy(PQB[:, :], PA[:]),
                    waits=[K.tok("T")], inc=True)
            tok_PA_free = tqb
            if stop_stage == 'ffq':
                stage_dump([tqb, K.tok("S"), K.tok("T")])
                tok_x0_free = K.tok("G"); tok_h1_free = K.tok("G")
                tok_trf_free = t_trf; tok_nu_guard = K.tok("T")
                continue
            t_fb2 = E("D", lambda d: d.dma_start(out=r_fb2, in_=fb2_d),
                      waits=[tok_mix_free, tok_state], inc=True, dma=True)
            prev_e = None
            for tc in range(4):
                tf = None
                for h in range(2):
                    for rs in range(2):
                        E("T", lambda t, h=h, rs=rs, tc=tc: t.matmul(
                            PB[:, 512 * h:512 * (h + 1)],
                            QBUF[:, rs, 128 * tc:128 * (tc + 1)],
                            v23[:, rs, 512 * h:512 * (h + 1)], start=(rs == 0), stop=False),
                            waits=[t_v2, tqb, tok_PB_free, t_fb2]
                            if (h == 0 and rs == 0) else ())
                    tf = E("T", lambda t, h=h: t.matmul(
                        PB[:, 512 * h:512 * (h + 1)], ONES1[:],
                        r_fb2[:, 512 * h:512 * (h + 1)], start=False, stop=True),
                        inc=(h == 1))
                if stop_stage == 'ffe1':
                    tok_PB_free = E("V", lambda v: v.tensor_copy(SL(10), PB[:]),
                                    waits=[tf], inc=True)
                    prev_e = tok_PB_free
                    continue
                st, rstd, nmr = ln_stats(PB[:], 5, [tf, prev_e], psum_src=True)
                if stop_stage == 'ffe2':
                    tok_PB_free = st[1]
                    prev_e = st[1]
                    continue
                tnm = E("S", lambda s, r=rstd, nm=nmr: s.activation(
                    SL(10), PB[:], AF.Identity, scale=r, bias=nm), waits=st, inc=True)
                tok_PB_free = tnm
                if stop_stage == 'ffe3':
                    prev_e = tnm
                    continue
                tout = E("G", lambda g, tc=tc: g.tensor_tensor(
                    H1[:, tc, :], H1[:, tc, :], SL(10), ALU.add),
                    waits=[tnm], inc=True)
                prev_e = tout
                E("G", lambda g, n=n, tc=tc: g.dma_start(out=out_r[n][tc], in_=H1[:, tc, :]),
                  inc=True, dma=True)
            if stop_stage in ('ffe1', 'ffe2', 'ffe3'):
                stage_dump([K.tok("S"), K.tok("V"), K.tok("T")])
            tok_x0_free = K.tok("G")
            tok_h1_free = K.tok("G")
            tok_nu_guard = K.tok("T")
            for i in range(10):
                tok_sl_free[i] = tqb
            tok_poolw_free = K.tok("T")
            t_cast_done = None

        E("G", lambda g: g.wait_ge(sem["G"], K.cnt["G"]))
        K.check()

        waited = {e: {} for e in Sched.ENGS}

        def replay(e, eng):
            for waits, fn, inc, amt, sm in K.q[e]:
                for we, wv in waits:
                    if waited[e].get(we, 0) < wv:
                        eng.wait_ge(sem[we], wv)
                        waited[e][we] = wv
                inst = fn(eng)
                if inc:
                    inst.then_inc(sem[sm], amt)

        @blk.tensor
        def _(tensor):
            replay("T", tensor)

        @blk.vector
        def _(vector):
            replay("V", vector)

        @blk.scalar
        def _(scalar):
            replay("S", scalar)

        @blk.sync
        def _(sync):
            replay("D", sync)

        @blk.gpsimd
        def _(gpsimd):
            replay("G", gpsimd)

    return nc


def make_inputs(inp, b, T):
    p = {k[8:]: np.asarray(v, np.float32) for k, v in inp.items()
         if k.startswith("L0_attn_") and k != "L0_attn_x"}
    f = {k[7:]: np.asarray(v, np.float32) for k, v in inp.items() if k.startswith("L0_ffn_")}
    x = np.asarray(inp["L0_attn_x"][b], np.float32)[:T]
    wck = np.ascontiguousarray(
        p["Wck"].T.reshape(8, 128, 32, 128).transpose(1, 2, 0, 3).reshape(128, 32768))
    d = {
        "x": np.ascontiguousarray(x),
        "WkT": _lhsT(p["Wk"].T, 8).reshape(128, -1),
        "WvT": _lhsT(p["Wv"].T, 8).reshape(128, -1),
        "WrT": _lhsT(p["Wr"].T, 8).reshape(128, -1),
        "WcrT": _lhsT(p["Wcr"].T, 8).reshape(128, -1),
        "WckT": wck,
        "WcvT": np.ascontiguousarray(
            p["Wcv"].T.astype(np.float32).reshape(32, 128, 2, 512)
            .transpose(1, 2, 0, 3).reshape(128, 32768)),
        "U1T": _lhsT(f["U1"], 8).reshape(128, -1),
        "V1T": _lhsT(f["V1"], 2).reshape(128, -1),
        "U2T": _lhsT(f["U2"], 32).reshape(128, -1),
        "V2T": _lhsT(f["V2"], 2).reshape(128, -1),
        "fmrows": np.ascontiguousarray(np.stack(
            [_fm(p["rk"]), _fm(p["rv"]), _fm(p["rr"]), _fm(p["crk"]), _fm(p["crr"]),
             _fm(p["ln1_g"]), _fm(p["ln1_b"]), _fm(p["ln2_g"]), _fm(p["ln2_b"])], axis=2)),
        "fb1fm": np.ascontiguousarray(np.asarray(f["b1"], np.float32).reshape(32, 128).T),
        "wrow": np.asarray(p["w"], np.float32).reshape(1, -1).copy(),
        "urow": np.asarray(p["u"], np.float32).reshape(1, -1).copy(),
        "fb2row": np.asarray(f["b2"], np.float32).reshape(1, -1).copy(),
        "Rf32": np.triu(np.ones((128, 128), np.float32)),
        "Rbf": np.triu(np.ones((128, 128))).astype(BF),
        "identbf": np.eye(128).astype(BF),
        "iotacol": np.arange(128, dtype=np.float32).reshape(-1, 1),
        "ones1": np.ones((1, 128), np.float32),
        "onescb": np.ones((128, 1)).astype(BF),
        "onescf": np.ones((128, 1), np.float32),
    }
    return d


def kernel(**inputs):
    T = 8192
    nc = build_program(T)
    in_maps = [make_inputs(inputs, b, T) for b in range(8)]
    res = run_bass_kernel_spmd(nc, in_maps, core_ids=list(range(8)))
    return np.stack([res.results[b]["out"] for b in range(8)], axis=0)


# version 07
# speedup vs baseline: 1.0024x; 1.0024x over previous
import numpy as np
import ml_dtypes
import concourse.bass as bass
import concourse.mybir as mybir
from concourse.bass_utils import run_bass_kernel_spmd
from contextlib import ExitStack

F32 = mybir.dt.float32
BF16 = mybir.dt.bfloat16
AF = mybir.ActivationFunctionType
ALU = mybir.AluOpType
BF = ml_dtypes.bfloat16

C = 1024
L = 128
TT_TILE = 512
NCHUNK = TT_TILE // L
PC_CLAMP = -15.0
GC1 = 0.044715 * 0.7978845608028654
GC2 = 0.7978845608028654


def _lhsT(Wcd, ng):
    return np.ascontiguousarray(
        np.asarray(Wcd, np.float32).reshape(ng, 128, -1).transpose(1, 0, 2))


def _fm(v):
    return np.ascontiguousarray(np.asarray(v, np.float32).reshape(8, 128).T)


class Sched:
    ENGS = "TVSDG"
    SEMS = "TVSDGP"

    def __init__(self, serialize=False):
        self.q = {e: [] for e in self.ENGS}
        self.cnt = {e: 0 for e in self.SEMS}
        self.serialize = serialize
        self.last = None

    def emit(self, e, fn, waits=(), inc=False, dma=False):
        w = [t for t in waits if t is not None]
        if self.serialize:
            inc = True
            if self.last is not None:
                w = w + [self.last]
        sem = e
        if e == "G" and not dma:
            sem = "P"
        amt = 16 if dma else 1
        self.q[e].append((w, fn, inc, amt, sem))
        if inc:
            self.cnt[sem] += amt
            tok = (sem, self.cnt[sem])
            if self.serialize:
                self.last = tok
            return tok
        return None

    def tok(self, e):
        return (e, self.cnt[e])

    def check(self):
        ptr = {e: 0 for e in self.ENGS}
        done = {s: 0 for s in self.SEMS}
        progress = True
        while progress:
            progress = False
            for e in self.ENGS:
                while ptr[e] < len(self.q[e]):
                    waits, _, inc, amt, sem = self.q[e][ptr[e]]
                    if any(done[we] < wv for we, wv in waits):
                        break
                    if inc:
                        done[sem] += amt
                    ptr[e] += 1
                    progress = True
        stuck = {e: (ptr[e], len(self.q[e])) for e in self.ENGS if ptr[e] < len(self.q[e])}
        if stuck:
            msgs = []
            for e in stuck:
                waits, _, _, _, _ = self.q[e][ptr[e]]
                unmet = [(we, wv, done[we]) for we, wv in waits if done[we] < wv]
                msgs.append(f"{e}@{ptr[e]}/{len(self.q[e])} needs {unmet}")
            raise RuntimeError("deadlock: " + "; ".join(msgs))


def build_program(T, dbg=False, serialize=False, stop_stage='out'):
    nt = T // TT_TILE
    nc = bass.Bass("TRN2", target_bir_lowering=False, detect_race_conditions=False)

    def din(name, shape, dtype=mybir.dt.float32):
        return nc.dram_tensor(name, list(shape), dtype, kind="ExternalInput").ap()

    x_d = din("x", [T, C])
    out_d = nc.dram_tensor("out", [T, C], mybir.dt.float32, kind="ExternalOutput").ap()
    wk_d = din("WkT", [128, 8192]); wv_d = din("WvT", [128, 8192])
    wr_d = din("WrT", [128, 8192]); wcr_d = din("WcrT", [128, 8192])
    wck_d = din("WckT", [128, 32768]); wcv_d = din("WcvT", [128, 32768])
    u1_d = din("U1T", [128, 2048]); v1_d = din("V1T", [128, 8192])
    u2_d = din("U2T", [128, 8192]); v2_d = din("V2T", [128, 2048])
    fmr_d = din("fmrows", [128, 8, 9])
    fb1_d = din("fb1fm", [128, 32])
    wrow_d = din("wrow", [1, 1024]); urow_d = din("urow", [1, 1024])
    fb2_d = din("fb2row", [1, 1024])
    Rf_d = din("Rf32", [128, 128]); Rb_d = din("Rbf", [128, 128], BF16)
    id_d = din("identbf", [128, 128], BF16)
    iota_d = din("iotacol", [128, 1]); ones1_d = din("ones1", [1, 128])
    onescb_d = din("onescb", [128, 1], BF16); onescf_d = din("onescf", [128, 1])
    wscr = {}
    for nm, w in [("WkT", 8192), ("WvT", 8192), ("WrT", 8192), ("WcrT", 8192),
                  ("WckT", 32768), ("WcvT", 32768), ("U1T", 2048), ("V1T", 8192),
                  ("U2T", 8192), ("V2T", 2048)]:
        wscr[nm] = nc.dram_tensor("s_" + nm, [128, w], BF16).ap()
    wraw = {"WkT": wk_d, "WvT": wv_d, "WrT": wr_d, "WcrT": wcr_d, "WckT": wck_d,
            "WcvT": wcv_d, "U1T": u1_d, "V1T": v1_d, "U2T": u2_d, "V2T": v2_d}
    dbg_d = {}
    if dbg:
        for nm in ["h1", "x1", "attn", "x2"]:
            dbg_d[nm] = nc.dram_tensor("dbg_" + nm, [T, C], mybir.dt.float32,
                                       kind="ExternalOutput").ap()

    es = ExitStack()
    with es:
        sb = lambda n, s, d=F32: es.enter_context(nc.sbuf_tensor(n, list(s), d))
        pst = lambda n, s, d=F32: es.enter_context(nc.psum_tensor(n, list(s), d))
        POOLW = sb("poolw", [128, 24576], BF16)
        SA = sb("sarena", [128, 11, 1024])
        SAb = SA.bitcast(BF16)
        GB = sb("gbpair", [128, 2048], BF16)
        MIX = sb("mix", [128, 3, 8, 512], BF16)
        TRF = sb("trfm", [128, 8, 513], BF16)
        TMS = sb("tmstage", [128, 1024], BF16)
        XC = sb("xc", [128, 1024])
        H1 = sb("h1t", [128, 4, 1024])
        KCA = sb("kcarena", [128, 16384], BF16)
        PQB = sb("pqbuf", [128, 1024], BF16)
        ROWA = sb("rowa", [128, 1024]); ROWB = sb("rowb", [128, 1024])
        ROWC = sb("rowc", [128, 1024])
        STAT = sb("stat", [128, 64])
        IOTW = sb("iotw", [128, 1024]); EU = sb("eu", [128, 1024])
        RF = sb("rf", [128, 128]); RB = sb("rb", [128, 128], BF16)
        IDB = sb("idb", [128, 128], BF16)
        ONES1 = sb("onesone", [1, 128]); IOTAC = sb("iotac", [128, 1])
        ONESCB = sb("onescbs", [128, 1], BF16); ONESCF = sb("onescfs", [128, 1])
        MIXCO = sb("mixco", [128, 8, 17])
        FB1 = sb("fb1", [128, 32])
        CAR1 = sb("car1", [128, 8, 1]); CAR2 = sb("car2", [128, 8, 1])
        CONSTS = sb("consts", [128, 3])
        nc.const_aps.register(F32, 1e-5, CONSTS[:, 0:1])
        nc.const_aps.register(F32, 1e-38, CONSTS[:, 1:2])
        nc.const_aps.register(F32, 1.0, CONSTS[:, 2:3])
        PA = pst("PA", [128, 1024]); PB = pst("PB", [128, 1024])
        PT = pst("PT", [128, 1024], BF16)
        PTOT = pst("PTOT", [1, 1024])
        sem = {e: es.enter_context(nc.semaphore("sem" + e)) for e in Sched.SEMS}
        blk = es.enter_context(nc.Block())

        K = Sched(serialize=serialize)
        E = K.emit
        FMR = SA[:, 10, 0:72].rearrange("p (a k) -> p a k", a=8)
        r_phc = ROWA[0:1, :]; r_phat = ROWA[32:33, :]; r_s0 = ROWA[64:65, :]
        r_mu = ROWB[0:1, :]; r_nhat = ROWB[32:33, :]; r_s1 = ROWB[64:65, :]
        r_nu = ROWC[0:1, :]; r_Lw = ROWC[32:33, :]; r_s2 = ROWC[64:65, :]
        r_wu = ROWA[0:1, :]; r_fb2 = ROWC[0:1, :]; r_s3 = ROWB[64:65, :]
        r_phat32 = ROWA[32:33, :]; r_lnt2 = ROWB[32:33, :]
        SL = lambda i: SA[:, i, :]
        SLh = lambda i, h: SA[:, i, 512 * h:512 * (h + 1)]

        ldt = []
        for dst, src in [(RF[:], Rf_d), (RB[:], Rb_d), (IDB[:], id_d),
                         (ONES1[:], ones1_d), (IOTAC[:], iota_d),
                         (ONESCB[:], onescb_d), (ONESCF[:], onescf_d),
                         (FMR, fmr_d), (FB1[:], fb1_d),
                         (r_wu, wrow_d), (r_s3, urow_d)]:
            ldt.append(E("D", lambda d, o=dst, s=src: d.dma_start(out=o, in_=s),
                         inc=True, dma=True))
        tld = ldt[-1]
        t_lw = E("S", lambda s: s.activation(r_Lw, r_wu, AF.Identity, scale=float(L)),
                 waits=[tld], inc=True)
        t_eur = E("S", lambda s: s.activation(r_mu, r_s3, AF.Exp), inc=True)
        E("T", lambda t: t.matmul(PA[:, 0:512], ONES1[:], r_mu[:, 0:512],
                                  start=True, stop=True), waits=[t_eur])
        t_mm1 = E("T", lambda t: t.matmul(PA[:, 512:1024], ONES1[:], r_mu[:, 512:1024],
                                          start=True, stop=True), inc=True)
        E("T", lambda t: t.matmul(PB[:, 0:512], ONES1[:], r_wu[:, 0:512],
                                  start=True, stop=True), waits=[tld])
        t_mm2 = E("T", lambda t: t.matmul(PB[:, 512:1024], ONES1[:], r_wu[:, 512:1024],
                                          start=True, stop=True), inc=True)
        t_eu = E("V", lambda v: v.tensor_copy(EU[:], PA[:]), waits=[t_mm1], inc=True)
        t_iotw = E("V", lambda v: v.tensor_scalar(IOTW[:], PB[:], IOTAC[:, 0:1], None,
                                                  ALU.mult, ALU.bypass),
                   waits=[t_mm2], inc=True)
        fv = lambda i: FMR[:, :, i]
        mco = lambda i: MIXCO[:, :, i]
        tpm = None
        for dsti, ai, bi in [(0, 0, 5), (2, 1, 5), (4, 2, 5), (7, 3, 7), (9, 4, 7)]:
            tpm = E("V", lambda v, d=dsti, a=ai, b=bi: v.tensor_tensor(mco(d), fv(a), fv(b),
                                                                       ALU.mult),
                    waits=[tld], inc=True)
        for dsti, gi, pi in [(1, 5, 0), (3, 5, 2), (5, 5, 4), (8, 7, 7), (10, 7, 9)]:
            tpm = E("V", lambda v, d=dsti, g=gi, p=pi: v.tensor_tensor(mco(d), fv(g), mco(p),
                                                                       ALU.subtract),
                    waits=[tpm], inc=True)
        E("V", lambda v: v.tensor_copy(mco(6), fv(6)))
        E("V", lambda v: v.tensor_copy(mco(11), fv(8)))
        for dsti, ri, bi in [(12, 0, 6), (13, 1, 6), (14, 2, 6), (15, 3, 8), (16, 4, 8)]:
            tpm = E("V", lambda v, d=dsti, r=ri, b=bi: v.tensor_tensor(mco(d), fv(r), fv(b),
                                                                       ALU.mult),
                    waits=[tpm], inc=True)
        E("V", lambda v: v.memset(CONSTS[:, 0:1], 1e-5))
        E("V", lambda v: v.memset(CONSTS[:, 1:2], 1e-38))
        E("V", lambda v: v.memset(CONSTS[:, 2:3], 1.0))
        E("V", lambda v: v.memset(r_phat, -10000.0))
        E("V", lambda v: v.memset(r_nhat, 0.0))
        E("V", lambda v: v.memset(CAR1[:], 0.0))
        t_init = E("V", lambda v: v.memset(CAR2[:], 0.0), inc=True)

        jobs = []
        for nm in ["WkT", "WvT", "WrT", "WcrT", "WckT", "WcvT", "U1T", "V1T", "U2T", "V2T"]:
            tot = wraw[nm].shape[1]
            for c0 in range(0, tot, 2048):
                jobs.append((wraw[nm][:, c0:c0 + 2048], wscr[nm][:, c0:c0 + 2048]))
        prev_out = [None, None]
        t_cast_done = None
        for i, (srcap, dstap) in enumerate(jobs):
            b = i % 2
            stg = SA[:, 2 * b:2 * b + 2, :].rearrange("p a k -> p (a k)")
            tin = E("D", lambda d, s=srcap, g=stg: d.dma_start(out=g, in_=s),
                    waits=[prev_out[b]], inc=True, dma=True)
            tout = E("G", lambda g, s=stg, dd=dstap: g.dma_start(out=dd, in_=s),
                     waits=[tin], inc=True, dma=True)
            prev_out[b] = tout
            t_cast_done = tout

        last_ln = [t_init]

        def ln_stats(src_ap, idx, tok_in, psum_src=False):
            sm = STAT[:, 4 * idx:4 * idx + 1]
            sq = STAT[:, 4 * idx + 2:4 * idx + 3]
            tsq = E("S", lambda s, a=src_ap: s.activation(SL(10), a, AF.Square),
                    waits=list(tok_in) + [last_ln[0]], inc=True)
            t1 = E("V", lambda v, a=src_ap, o=sm: v.reduce_sum(o, a, axis=mybir.AxisListType.X),
                   waits=list(tok_in) + [last_ln[0]] + ([tsq] if psum_src else []), inc=True)
            t2 = E("V", lambda v, o=sq: v.reduce_sum(o, SL(10), axis=mybir.AxisListType.X),
                   waits=[tsq], inc=True)
            m = STAT[:, 32:33]; v2 = STAT[:, 36:37]; rstd = STAT[:, 40:41]
            nmr = STAT[:, 44:45]; tmp = STAT[:, 48:49]; tmp2 = STAT[:, 52:53]
            ta = E("V", lambda v: v.tensor_single_scalar(m, sm, 1.0 / 1024.0, ALU.mult),
                   waits=[t1, t2], inc=True)
            tb = E("V", lambda v: v.tensor_single_scalar(v2, sq, 1.0 / 1024.0, ALU.mult),
                   waits=[ta], inc=True)
            tc2 = E("V", lambda v: v.tensor_tensor(tmp, m, m, ALU.mult), waits=[tb], inc=True)
            td2 = E("V", lambda v: v.tensor_tensor(v2, v2, tmp, ALU.subtract),
                    waits=[tc2], inc=True)
            tv = E("V", lambda v: v.tensor_single_scalar(v2, v2, 1e-5, ALU.add),
                   waits=[td2], inc=True)
            E("S", lambda s: s.activation(tmp2, v2, AF.Ln), waits=[tv])
            tr = E("S", lambda s: s.activation(rstd, tmp2, AF.Exp, scale=-0.5), inc=True)
            te2 = E("V", lambda v: v.tensor_tensor(tmp, m, rstd, ALU.mult), waits=[tr], inc=True)
            tn = E("V", lambda v: v.tensor_single_scalar(nmr, tmp, -1.0, ALU.mult),
                   waits=[te2], inc=True)
            last_ln[0] = tn
            return [tr, tn], rstd, nmr

        def transpose_to_fm(tm_bf_ap, dst_fm_ap, waits, pt_free):
            tt = None
            for a in range(8):
                tt = E("T", lambda t, a=a, s=tm_bf_ap: t.transpose(
                    PT[:, 128 * a:128 * (a + 1)], s[:, 128 * a:128 * (a + 1)], IDB[:]),
                    waits=(list(waits) + [pt_free]) if a == 0 else (), inc=(a == 7))
            return E("S", lambda s, d=dst_fm_ap: s.copy(
                d, PT[:, :].rearrange("p (a t) -> p a t", a=8)), waits=[tt], inc=True)

        tok_PA_free = t_eu; tok_PB_free = t_iotw
        tok_PT_free = K.tok("V")
        tok_PTOT_free = t_init
        tok_x0_free = None
        tok_h1_free = None
        tok_nu_guard = None
        tok_prol_guard = t_eur
        tok_mix_free = None
        tok_state = t_init
        tok_trf_free = None
        tok_gb_free = None
        tok_gb_free2 = None
        tok_sl_free = {i: None for i in range(11)}
        tok_poolw_free = None

        x_r2 = x_d.rearrange("(n tc p) c -> n tc p c", p=128, tc=4)
        out_r = out_d.rearrange("(n tc p) c -> n tc p c", p=128, tc=4)
        dbg_r = {k2: v2.rearrange("(n tc p) c -> n tc p c", p=128, tc=4)
                 for k2, v2 in dbg_d.items()}

        def load_w(dst2d, nm, waits):
            return E("D", lambda d, o=dst2d, s=wscr[nm]: d.dma_start(out=o, in_=s),
                     waits=waits, inc=True, dma=True)

        for n in range(nt):
            wk_sb = POOLW[:, 0:8192]; wv_sb = POOLW[:, 8192:16384]; wr_sb = POOLW[:, 16384:24576]
            t_wk = load_w(wk_sb, "WkT", [tok_poolw_free, t_cast_done])
            t_wv = load_w(wv_sb, "WvT", [])
            t_wr = load_w(wr_sb, "WrT", [])
            wk3 = wk_sb.rearrange("p (a k) -> p a k", a=8)
            wv3 = wv_sb.rearrange("p (a k) -> p a k", a=8)
            wr3 = wr_sb.rearrange("p (a k) -> p a k", a=8)

            tcar = E("V", lambda v: v.tensor_copy(TRF[:, :, 0:1], CAR1[:]),
                     waits=[tok_trf_free, t_init], inc=True)
            t_h1 = []
            t_xc = None
            for tc in range(4):
                x0c = XC[:, :]; h1c = H1[:, tc, :]
                t_x = E("D", lambda d, n=n, tc=tc: d.dma_start(out=XC[:], in_=x_r2[n][tc]),
                        waits=[t_xc, tok_x0_free] + ([t_cast_done] if tc == 0 else []),
                        inc=True, dma=True)
                st, rstd, nmr = ln_stats(x0c, 0, [t_x])
                th = E("S", lambda s, a=x0c, o=h1c, r=rstd, nm=nmr: s.activation(
                    o, a, AF.Identity, scale=r, bias=nm), waits=st + [tok_h1_free], inc=True)
                t_xc = th
                if stop_stage == 'h1x':
                    t_h1.append(th)
                    t_xc = th
                    continue
                st2, rstd2, nmr2 = ln_stats(h1c, 1, [th])
                tx = E("S", lambda s, a=h1c, r=rstd2, nm=nmr2: s.activation(
                    TMS[:], a, AF.Identity, scale=r, bias=nm), waits=st2, inc=True)
                te = transpose_to_fm(TMS[:], TRF[:, :, 1 + 128 * tc:1 + 128 * (tc + 1)],
                                     [tx, tcar], tok_PT_free)
                tok_PT_free = te
                t_h1.append(th)
                if dbg:
                    E("G", lambda g, n=n, tc=tc, a=h1c: g.dma_start(
                        out=dbg_r["h1"][n][tc], in_=a), waits=[th], inc=True, dma=True)
            t_trf = tok_PT_free

            def stage_dump(waits):
                tt = None
                for tc2 in range(4):
                    tt = E("G", lambda g, n=n, tc2=tc2: g.dma_start(out=out_r[n][tc2],
                                                                   in_=H1[:, tc2, :]),
                           waits=list(waits), inc=True, dma=True)
                return tt

            if stop_stage in ('h1', 'h1x'):
                stage_dump([K.tok("S"), K.tok("V")])
                tok_x0_free = K.tok("G"); tok_h1_free = K.tok("G")
                tok_poolw_free = K.tok("T"); tok_mix_free = K.tok("T")
                tok_trf_free = t_trf
                continue

            tmix = None
            for mi, (cg, cg1) in enumerate([(0, 1), (2, 3), (4, 5)]):
                for a in range(8):
                    cur = TRF[:, a, 1:513]; prv = TRF[:, a, 0:512]
                    o = MIX[:, mi, a, :]
                    E("V", lambda v, c=cur, g=cg, a=a: v.tensor_scalar(
                        SLh(10, 0), c, MIXCO[:, a, g:g + 1], MIXCO[:, a, 6:7],
                        ALU.mult, ALU.add),
                      waits=[t_trf, tok_mix_free] if (mi == 0 and a == 0) else ())
                    E("V", lambda v, p=prv, g=cg1, a=a: v.tensor_scalar(
                        SLh(10, 1), p, MIXCO[:, a, g:g + 1], None, ALU.mult, ALU.bypass))
                    tmix = E("V", lambda v, o=o: v.tensor_tensor(o, SLh(10, 0), SLh(10, 1),
                                                                 ALU.add), inc=True)
                    if n == 0:
                        tmix = E("V", lambda v, o=o, c=cur, g=cg, a=a, mi=mi: v.tensor_scalar(
                            o[:, 0:1], c[:, 0:1], MIXCO[:, a, g:g + 1],
                            MIXCO[:, a, 12 + mi:13 + mi], ALU.mult, ALU.add), inc=True)
            tcarS = E("V", lambda v: v.tensor_copy(CAR1[:], TRF[:, :, 512:513]), inc=True)
            t_mixA = tmix

            for tc in range(NCHUNK):
                def ddump(name, ap, tok, dt2=F32, _n=n, _tc=tc):
                    if not (dbg and _n == 0 and _tc == 0):
                        return None
                    dd = nc.dram_tensor("dbgw_" + name, list(ap.shape), dt2,
                                        kind="ExternalOutput").ap()
                    return E("G", lambda g, a=ap, o=dd: g.dma_start(out=o, in_=a),
                             waits=[tok], inc=True, dma=True)
                ts1 = E("V", lambda v: v.tensor_single_scalar(r_phc, r_phat, PC_CLAMP, ALU.max),
                        waits=[tok_state, tok_prol_guard], inc=True)
                tok_prol_guard = None
                ts2 = E("V", lambda v: v.tensor_scalar(r_s0, r_phat, -PC_CLAMP, 0.0,
                                                       ALU.add, ALU.min), waits=[ts1], inc=True)
                ts3a = E("S", lambda s: s.activation(r_mu, r_s0, AF.Exp), waits=[ts2], inc=True)
                ts3 = E("V", lambda v: v.tensor_single_scalar(r_mu, r_mu, 1e-38, ALU.add),
                       waits=[ts3a], inc=True)
                ts4a = E("V", lambda v: v.tensor_copy(r_nu, r_nhat),
                         waits=[tok_nu_guard, ts3], inc=True)
                ts4 = E("V", lambda v: v.tensor_tensor(r_nu, r_mu, r_nu, ALU.mult),
                        waits=[ts4a], inc=True)
                tok_nu_guard = None
                tbc = None
                for h in range(2):
                    tbc = E("T", lambda t, h=h: t.matmul(
                        PA[:, 512 * h:512 * (h + 1)], ONES1[:],
                        r_phc[:, 512 * h:512 * (h + 1)], start=True, stop=True),
                        waits=[ts2, tok_PA_free] if h == 0 else (), inc=(h == 1))
                tk = None
                for h in range(2):
                    for a in range(8):
                        tk = E("T", lambda t, h=h, a=a, tc=tc: t.matmul(
                            PB[:, 512 * h:512 * (h + 1)],
                            MIX[:, 0, a, 128 * tc:128 * (tc + 1)],
                            wk3[:, a, 512 * h:512 * (h + 1)],
                            start=(a == 0), stop=(a == 7)),
                            waits=[t_mixA, t_wk, tok_PB_free] if (h == 0 and a == 0) else (),
                            inc=(h == 1 and a == 7))
                tv1 = E("V", lambda v: v.tensor_tensor(SL(0), PB[:], IOTW[:], ALU.subtract),
                        waits=[tk, tok_sl_free[0]], inc=True)
                tz = E("V", lambda v: v.tensor_tensor(SL(1), SL(0), PA[:], ALU.subtract),
                       waits=[tbc, tok_sl_free[1]], inc=True)
                tok_PB_free = tz; tok_PA_free = tz
                tvp = None
                for h in range(2):
                    for a in range(8):
                        tvp = E("T", lambda t, h=h, a=a, tc=tc: t.matmul(
                            PA[:, 512 * h:512 * (h + 1)],
                            MIX[:, 1, a, 128 * tc:128 * (tc + 1)],
                            wv3[:, a, 512 * h:512 * (h + 1)],
                            start=(a == 0), stop=(a == 7)),
                            waits=[tz, t_wv] if (h == 0 and a == 0) else (),
                            inc=(h == 1 and a == 7))
                tg = E("S", lambda s: s.activation(SL(2), SL(1), AF.Exp),
                       waits=[tz, tok_sl_free[2]], inc=True)
                tgb = E("S", lambda s: s.copy(GB[:, 0:1024], SL(2)),
                        waits=[tg, tok_gb_free, tok_gb_free2], inc=True)
                tvf = E("S", lambda s: s.copy(SL(3), PA[:]), waits=[tvp, tok_sl_free[3]],
                        inc=True)
                tok_PA_free = tvf
                trp = None
                for h in range(2):
                    for a in range(8):
                        trp = E("T", lambda t, h=h, a=a, tc=tc: t.matmul(
                            PA[:, 512 * h:512 * (h + 1)],
                            MIX[:, 2, a, 128 * tc:128 * (tc + 1)],
                            wr3[:, a, 512 * h:512 * (h + 1)],
                            start=(a == 0), stop=(a == 7)),
                            waits=[tvf, t_wr] if (h == 0 and a == 0) else (),
                            inc=(h == 1 and a == 7))
                tF = None
                for h in range(2):
                    E("T", lambda t, h=h: t.matmul(
                        PB[:, 512 * h:512 * (h + 1)], RB[:], GB[:, 512 * h:512 * (h + 1)],
                        start=True, stop=False), waits=[tgb] if h == 0 else ())
                    tF = E("T", lambda t, h=h: t.matmul(
                        PB[:, 512 * h:512 * (h + 1)], ONES1[:],
                        r_mu[:, 512 * h:512 * (h + 1)], start=False, stop=True),
                        waits=[ts3] if h == 0 else (), inc=(h == 1))
                tFt = None
                for h in range(2):
                    tFt = E("T", lambda t, h=h: t.matmul(
                        PTOT[0:1, 512 * h:512 * (h + 1)], ONESCB[:],
                        GB[:, 512 * h:512 * (h + 1)], start=True, stop=True),
                        waits=[tok_PTOT_free] if h == 0 else (), inc=(h == 1))
                ter = E("S", lambda s: s.activation(SL(4), PA[:], AF.Exp, scale=-1.0),
                        waits=[trp, tok_sl_free[4]], inc=True)
                tok_PA_free = ter
                tFE = E("V", lambda v: v.tensor_tensor(SL(5), PB[:], GB[:, 0:1024],
                                                       ALU.subtract),
                        waits=[tF, tok_sl_free[5]], inc=True)
                tok_PB_free = tFE
                tt3 = E("S", lambda s: s.activation(SL(6), SL(5), AF.Ln),
                        waits=[tFE, tok_sl_free[6]], inc=True)
                td = E("V", lambda v: v.tensor_tensor(SL(7), SL(6), SL(1), ALU.subtract),
                       waits=[tt3, tok_sl_free[7]], inc=True)
                tad = E("S", lambda s: s.activation(SL(6), SL(7), AF.Abs),
                        waits=[td], inc=True)
                ted = E("S", lambda s: s.activation(SL(7), SL(6), AF.Exp, scale=-1.0),
                        waits=[tad], inc=True)
                tp1 = E("V", lambda v: v.tensor_single_scalar(SL(7), SL(7), 1.0, ALU.add),
                        waits=[ted], inc=True)
                tgam = E("S", lambda s: s.activation(SL(6), SL(7), AF.Ln), waits=[tp1], inc=True)
                tst5 = E("V", lambda v: v.tensor_tensor(r_s0, r_mu, PTOT[0:1, :], ALU.add),
                         waits=[tFt], inc=True)
                tst5b = E("V", lambda v: v.reciprocal(r_s1, r_s0), waits=[tst5], inc=True)
                tst6 = E("S", lambda s: s.activation(r_lnt2, r_s0, AF.Ln), waits=[tst5b],
                         inc=True)
                tst7a = E("V", lambda v: v.tensor_copy(r_phat32, r_phc), waits=[tst6], inc=True)
                tst7b = E("V", lambda v: v.tensor_tensor(r_phat32, r_phat32, r_Lw, ALU.add),
                         waits=[tst7a], inc=True)
                tst8 = E("V", lambda v: v.tensor_tensor(r_phat, r_phat32, r_lnt2, ALU.add),
                        waits=[tst7b], inc=True)
                tQ = None
                for h in range(2):
                    tQ = E("T", lambda t, h=h: t.matmul(
                        PB[:, 512 * h:512 * (h + 1)], RF[:], SLh(6, h), start=True, stop=True),
                        waits=[tgam, tFE] if h == 0 else (), inc=(h == 1))
                tQt = None
                for h in range(2):
                    tQt = E("T", lambda t, h=h: t.matmul(
                        PTOT[0:1, 512 * h:512 * (h + 1)], ONESCF[:], SLh(6, h),
                        start=True, stop=True),
                        waits=[tst5, tgam] if h == 0 else (), inc=(h == 1))
                tqp = E("V", lambda v: v.tensor_tensor(SL(8), PB[:], SL(6), ALU.subtract),
                        waits=[tQ, tok_sl_free[8]], inc=True)
                tok_PB_free = tqp
                tEQ = E("S", lambda s: s.activation(SL(9), SL(8), AF.Exp),
                        waits=[tqp, tok_sl_free[9]], inc=True)
                dz = ddump("z", SL(1), tz)
                dg = ddump("g", SL(2), tg)
                dfe = ddump("fe", SL(5), tFE)
                dgam = ddump("gam", SL(6), tgam)
                dqp = ddump("qp", SL(8), tqp)
                tzq = E("V", lambda v: v.tensor_tensor(SL(1), SL(1), SL(8), ALU.subtract),
                        waits=[tqp, dz], inc=True)
                tgt = E("S", lambda s: s.activation(SL(0), SL(1), AF.Exp), waits=[tzq],
                        inc=True)
                tgtv = E("V", lambda v: v.tensor_tensor(GB[:, 1024:2048], SL(0), SL(3),
                                                        ALU.mult),
                         waits=[tgt, tvf, tok_gb_free, tok_gb_free2], inc=True)
                tP = None
                for h in range(2):
                    E("T", lambda t, h=h: t.matmul(
                        PA[:, 512 * h:512 * (h + 1)], RB[:],
                        GB[:, 1024 + 512 * h:1024 + 512 * (h + 1)], start=True, stop=False),
                      waits=[tgtv, ter] if h == 0 else ())
                    tP = E("T", lambda t, h=h: t.matmul(
                        PA[:, 512 * h:512 * (h + 1)], ONES1[:],
                        r_nu[:, 512 * h:512 * (h + 1)], start=False, stop=True),
                        waits=[ts4] if h == 0 else (), inc=(h == 1))
                tst10 = E("S", lambda s: s.activation(r_s2, PTOT[0:1, :], AF.Exp),
                          waits=[tQt], inc=True)
                tPt = None
                for h in range(2):
                    tPt = E("T", lambda t, h=h: t.matmul(
                        PTOT[0:1, 512 * h:512 * (h + 1)], ONESCB[:],
                        GB[:, 1024 + 512 * h:1024 + 512 * (h + 1)], start=True, stop=True),
                        waits=[tst10, tgtv] if h == 0 else (), inc=(h == 1))
                deq = ddump("eq", SL(9), tEQ)
                dgtv = ddump("gtv", GB[:, 1024:2048], tgtv, BF16)
                ten = E("V", lambda v: v.tensor_tensor(SL(7), EU[:], SL(2), ALU.mult),
                        waits=[tgam], inc=True)
                tden = E("V", lambda v: v.tensor_tensor(SL(5), SL(5), SL(7), ALU.add),
                         waits=[ten, tt3, dfe], inc=True)
                te1p = E("V", lambda v: v.tensor_single_scalar(SL(4), SL(4), 1.0, ALU.add),
                         waits=[ter], inc=True)
                tden2 = E("V", lambda v: v.tensor_tensor(SL(5), SL(5), SL(4), ALU.mult),
                          inc=True)
                dden2 = ddump("den2", SL(5), tden2)
                trec = E("V", lambda v: v.reciprocal(SL(8), SL(5)),
                         waits=[tEQ, dqp, dden2], inc=True)
                tn1 = E("V", lambda v: v.tensor_tensor(SL(6), PA[:], GB[:, 1024:2048],
                                                       ALU.subtract),
                       waits=[tP, tQt, dgam], inc=True)
                tok_PA_free = tn1
                tn2 = E("V", lambda v: v.tensor_tensor(SL(6), SL(6), SL(9), ALU.mult),
                        inc=True)
                tdn = E("V", lambda v: v.tensor_tensor(SL(7), SL(7), SL(3), ALU.mult),
                       waits=[tden], inc=True)
                tnum = E("V", lambda v: v.tensor_tensor(SL(6), SL(6), SL(7), ALU.add),
                         inc=True)
                dnum = ddump("num", SL(6), tnum)
                tyz = E("V", lambda v: v.tensor_tensor(SL(6), SL(6), SL(8), ALU.mult),
                        waits=[trec, dnum], inc=True)
                tx1 = E("V", lambda v, tc=tc: v.tensor_tensor(
                    H1[:, tc, :], H1[:, tc, :], SL(6), ALU.add),
                    waits=[t_h1[tc]], inc=True)
                tst11 = E("V", lambda v: v.tensor_tensor(r_s0, r_nu, PTOT[0:1, :], ALU.add),
                          waits=[tPt, tst8], inc=True)
                tok_PTOT_free = tst11
                tst12 = E("V", lambda v: v.tensor_tensor(r_s0, r_s0, r_s2, ALU.mult),
                         waits=[tst11], inc=True)
                tst13 = E("V", lambda v: v.tensor_tensor(r_nhat, r_s0, r_s1, ALU.mult),
                         waits=[tst12], inc=True)
                tok_state = tst13
                tok_sl_free[0] = tgtv; tok_sl_free[1] = tgt; tok_sl_free[2] = tdn
                tok_sl_free[3] = tdn; tok_sl_free[4] = tden2; tok_sl_free[5] = trec
                tok_sl_free[6] = tyz; tok_sl_free[7] = tnum; tok_sl_free[8] = tyz
                tok_sl_free[9] = tn2
                tok_gb_free = tn1; tok_gb_free2 = tPt
                if dbg:
                    E("G", lambda g, n=n, tc=tc: g.dma_start(
                        out=dbg_r["x1"][n][tc], in_=H1[:, tc, :]),
                      waits=[tx1], inc=True, dma=True)
            t_x1 = K.tok("V")
            tok_mix_free = K.tok("T")
            tok_poolw_free = K.tok("T")

            if stop_stage == 'x1':
                stage_dump([t_x1, K.tok("S")])
                tok_x0_free = K.tok("G"); tok_h1_free = K.tok("G")
                tok_trf_free = t_trf
                continue

            tcar2 = E("V", lambda v: v.tensor_copy(TRF[:, :, 0:1], CAR2[:]),
                      waits=[tcarS], inc=True)
            for tc in range(4):
                x1c = H1[:, tc, :]
                st, rstd, nmr = ln_stats(x1c, 2, [t_x1])
                tx = E("S", lambda s, a=x1c, r=rstd, nm=nmr: s.activation(
                    TMS[:], a, AF.Identity, scale=r, bias=nm), waits=st, inc=True)
                te = transpose_to_fm(TMS[:], TRF[:, :, 1 + 128 * tc:1 + 128 * (tc + 1)],
                                     [tx, tcar2], tok_PT_free)
                tok_PT_free = te
            t_trf2 = tok_PT_free
            tmixb = None
            for mi, (cg, cg1) in enumerate([(7, 8), (9, 10)]):
                for a in range(8):
                    cur = TRF[:, a, 1:513]; prv = TRF[:, a, 0:512]
                    o = MIX[:, mi, a, :]
                    E("V", lambda v, c=cur, g=cg, a=a: v.tensor_scalar(
                        SLh(10, 0), c, MIXCO[:, a, g:g + 1], MIXCO[:, a, 11:12],
                        ALU.mult, ALU.add),
                      waits=[t_trf2, tok_mix_free] if (mi == 0 and a == 0) else ())
                    E("V", lambda v, p=prv, g=cg1, a=a: v.tensor_scalar(
                        SLh(10, 1), p, MIXCO[:, a, g:g + 1], None, ALU.mult, ALU.bypass))
                    tmixb = E("V", lambda v, o=o: v.tensor_tensor(o, SLh(10, 0), SLh(10, 1),
                                                                  ALU.add), inc=True)
                    if n == 0:
                        tmixb = E("V", lambda v, o=o, c=cur, g=cg, a=a, mi=mi: v.tensor_scalar(
                            o[:, 0:1], c[:, 0:1], MIXCO[:, a, g:g + 1],
                            MIXCO[:, a, 15 + mi:16 + mi], ALU.mult, ALU.add), inc=True)
            tcarS2 = E("V", lambda v: v.tensor_copy(CAR2[:], TRF[:, :, 512:513]), inc=True)
            t_mixB = tmixb

            wcr_sb = POOLW[:, 0:8192]
            t_wcr = load_w(wcr_sb, "WcrT", [tok_poolw_free])
            wcr3 = wcr_sb.rearrange("p (a k) -> p a k", a=8)
            wckh_sb = POOLW[:, 8192:24576]
            ECR = SAb[:, 4:6, :].rearrange("p a k -> p (a k)")
            tecr = []
            for tc in range(4):
                trc = None
                for h in range(2):
                    for a in range(8):
                        trc = E("T", lambda t, h=h, a=a, tc=tc: t.matmul(
                            PB[:, 512 * h:512 * (h + 1)],
                            MIX[:, 1, a, 128 * tc:128 * (tc + 1)],
                            wcr3[:, a, 512 * h:512 * (h + 1)],
                            start=(a == 0), stop=(a == 7)),
                            waits=[t_mixB, t_wcr, tok_PB_free] if (h == 0 and a == 0) else (),
                            inc=(h == 1 and a == 7))
                t_e = E("S", lambda s, tc=tc: s.activation(
                    ECR[:, 1024 * tc:1024 * (tc + 1)], PB[:], AF.Exp, scale=-1.0),
                    waits=[trc] + ([tok_sl_free[4], tok_sl_free[5]] if tc == 0 else []),
                    inc=True)
                tok_PB_free = t_e
                tecr.append(t_e)
            KC = KCA[:, :]
            t_kc = None
            for wh in range(2):
                twl = E("D", lambda d, wh=wh: d.dma_start(
                    out=wckh_sb, in_=wscr["WckT"][:, 16384 * wh:16384 * (wh + 1)]),
                    waits=[tok_poolw_free if wh == 0 else t_kc], inc=True, dma=True)
                wck4 = wckh_sb.rearrange("p (hs a k) -> p hs a k", hs=16, a=8)
                for hs in range(16):
                    hsg = 16 * wh + hs
                    tkp = None
                    for a in range(8):
                        tkp = E("T", lambda t, hs=hs, a=a, w4=wck4: t.matmul(
                            PA[:, 0:512], w4[:, hs, a, :], MIX[:, 0, a, :],
                            start=(a == 0), stop=(a == 7)),
                            waits=([twl, t_mixB, tok_PA_free] +
                                   ([tok_sl_free[0], tok_sl_free[1], tok_sl_free[2],
                                     tok_sl_free[3], tok_sl_free[6], tok_sl_free[7]]
                                    if (wh == 0 and hs == 0) else []))
                            if a == 0 else (), inc=(a == 7))
                    tr1 = E("V", lambda v: v.tensor_single_scalar(
                        SLh(10, 0), PA[:, 0:512], 0.0, ALU.max),
                        waits=[tkp, t_kc], inc=True)
                    t_kc = E("G", lambda g, hsg=hsg: g.tensor_tensor(
                        KC[:, 512 * hsg:512 * (hsg + 1)], SLh(10, 0), SLh(10, 0), ALU.mult),
                        waits=[tr1], inc=True)
                    tok_PA_free = t_kc

            if dbg and n == 0:
                for _nm, _ap, _tok, _dt in [("xck", MIX[:, 0, :, :], t_mixB, BF16),
                                            ("xcr", MIX[:, 1, :, :], t_mixB, BF16),
                                            ("kc", KC, t_kc, BF16),
                                            ("ecr", ECR, tecr[3], BF16)]:
                    _dd = nc.dram_tensor("dbgw_" + _nm, list(_ap.shape), _dt,
                                         kind="ExternalOutput").ap()
                    E("G", lambda g, a=_ap, o=_dd: g.dma_start(out=o, in_=a),
                      waits=[_tok], inc=True, dma=True)
            wcv_sb = POOLW[:, 0:16384]
            kcv = KC.rearrange("p (g k) -> p g k", g=32)
            tep_last = None
            tat_prev = None
            tsv_prev = None
            t_sv = {}
            for dh in range(2):
                t_wcv = E("D", lambda d, dh=dh: d.dma_start(
                    out=wcv_sb, in_=wscr["WcvT"][:, 16384 * dh:16384 * (dh + 1)]),
                    waits=[K.tok("T")], inc=True, dma=True)
                wcv3 = wcv_sb.rearrange("p (a k) -> p a k", a=32)
                for tc in range(4):
                    PX = (PA if tc % 2 == 0 else PB)
                    col = 512 * (tc // 2)
                    tvc = None
                    for hs in range(32):
                        tvc = E("T", lambda t, hs=hs, tc=tc, PX=PX, col=col, w3=wcv3: t.matmul(
                            PX[:, col:col + 512], kcv[:, hs, 128 * tc:128 * (tc + 1)],
                            w3[:, hs, :], start=(hs == 0), stop=(hs == 31)),
                            waits=([t_wcv, t_kc, t_sv.get((dh - 1, tc)),
                                    t_sv.get((dh, tc - 2))] +
                                   ([tok_PB_free] if (dh == 0 and tc == 1) else []))
                            if hs == 0 else (),
                            inc=(hs == 31))
                    tep = E("G", lambda g, tc=tc, dh=dh: g.tensor_single_scalar(
                        SLh(6, 0), ECR[:, 1024 * tc + 512 * dh:1024 * tc + 512 * dh + 512],
                        1.0, ALU.add),
                        waits=[tecr[tc], tvc, tat_prev] +
                              ([tok_sl_free[6]] if (dh == 0 and tc == 0) else []),
                        inc=True)
                    trc2 = E("V", lambda v: v.reciprocal(SLh(7, 0), SLh(6, 0)),
                             waits=[tep, tsv_prev] +
                                   ([tok_sl_free[7]] if (dh == 0 and tc == 0) else []),
                             inc=True)
                    tsv = E("V", lambda v, PX=PX, col=col: v.tensor_tensor(
                        SLh(6, 0), PX[:, col:col + 512], SLh(7, 0), ALU.mult),
                        waits=[trc2], inc=True)
                    tsv_prev = tsv
                    tat = E("G", lambda g, tc=tc, dh=dh: g.tensor_tensor(
                        H1[:, tc, 512 * dh:512 * dh + 512], H1[:, tc, 512 * dh:512 * dh + 512],
                        SLh(6, 0), ALU.add), waits=[tsv], inc=True)
                    tat_prev = tat
                    t_sv[(dh, tc)] = tsv
                    tep_last = tat
                    if dbg:
                        E("G", lambda g, n=n, tc=tc: g.dma_start(
                            out=dbg_r["attn"][n][tc], in_=H1[:, tc, :]),
                          waits=[tat], inc=True, dma=True)
            tok_PB_free = tep_last; tok_PA_free = tep_last
            t_attn = K.tok("V")
            tok_sl_free[4] = tep_last; tok_sl_free[5] = tep_last
            tok_sl_free[6] = tep_last; tok_sl_free[7] = tep_last
            tok_poolw_free = K.tok("T")

            if stop_stage == 'attn':
                stage_dump([t_attn, K.tok("S")])
                tok_x0_free = K.tok("G"); tok_h1_free = K.tok("G")
                tok_trf_free = t_trf
                continue

            prev_c = None
            for tc in range(4):
                ac = H1[:, tc, :]
                t_x = E("D", lambda d, n=n, tc=tc: d.dma_start(out=XC[:], in_=x_r2[n][tc]),
                        waits=[t_xc, prev_c], inc=True, dma=True)
                st, rstd, nmr = ln_stats(ac, 3, [t_attn, prev_c])
                tnm = E("S", lambda s, a=ac, r=rstd, nm=nmr: s.activation(
                    SL(10), a, AF.Identity, scale=r, bias=nm), waits=st, inc=True)
                tx2 = E("V", lambda v, tc=tc: v.tensor_tensor(H1[:, tc, :], XC[:], SL(10),
                                                              ALU.add),
                        waits=[tnm, t_x], inc=True)
                t_xc = tx2
                st2, rstd2, nmr2 = ln_stats(ac, 4, [tx2])
                th2 = E("S", lambda s, x=ac, r=rstd2, nm=nmr2: s.activation(
                    TMS[:], x, AF.Identity, scale=r, bias=nm), waits=st2, inc=True)
                te = transpose_to_fm(TMS[:], MIX[:, 0, :, 128 * tc:128 * (tc + 1)],
                                     [th2, K.tok("T")], tok_PT_free)
                tok_PT_free = te
                prev_c = tx2
                if dbg:
                    E("G", lambda g, n=n, tc=tc: g.dma_start(
                        out=dbg_r["x2"][n][tc], in_=H1[:, tc, :]),
                      waits=[tx2], inc=True, dma=True)
            t_h2f = tok_PT_free
            t_x2 = K.tok("V")

            if stop_stage == 'x2':
                stage_dump([t_x2, K.tok("S"), K.tok("T")])
                tok_x0_free = K.tok("G"); tok_h1_free = K.tok("G")
                tok_trf_free = t_trf
                continue

            u1_sb = POOLW[:, 0:2048]; v1_sb = POOLW[:, 2048:10240]
            u2_sb = POOLW[:, 10240:18432]; v2_sb = POOLW[:, 18432:20480]
            t_u1 = load_w(u1_sb, "U1T", [tok_poolw_free])
            t_v1 = load_w(v1_sb, "V1T", [])
            t_u2 = load_w(u2_sb, "U2T", [])
            t_v2 = load_w(v2_sb, "V2T", [])
            PBUF = PQB[:, :].rearrange("p (a k) -> p a k", a=2)
            QBUF = PBUF
            u13 = u1_sb.rearrange("p (a k) -> p a k", a=8)
            v13 = v1_sb.rearrange("p (a k) -> p a k", a=2)
            u23 = u2_sb.rearrange("p (a k) -> p a k", a=32)
            v23 = v2_sb.rearrange("p (a k) -> p a k", a=2)
            for rs in range(2):
                for a in range(8):
                    E("T", lambda t, rs=rs, a=a: t.matmul(
                        PA[:, 512 * rs:512 * (rs + 1)], u13[:, a, 128 * rs:128 * (rs + 1)],
                        MIX[:, 0, a, :], start=(a == 0), stop=(a == 7)),
                        waits=[t_u1, t_h2f, tok_PA_free] if (rs == 0 and a == 0) else (),
                        inc=(rs == 1 and a == 7))
            tpb = E("S", lambda s: s.copy(PQB[:, :], PA[:]),
                    waits=[K.tok("T")], inc=True)
            tok_PA_free = tpb
            if stop_stage == 'ffp':
                stage_dump([tpb, K.tok("S"), K.tok("T")])
                tok_x0_free = K.tok("G"); tok_h1_free = K.tok("G")
                tok_trf_free = t_trf; tok_nu_guard = K.tok("T")
                continue
            HB = KC
            t_hh = None
            tsg_p = None
            tr2 = None
            for fs in range(32):
                th = None
                for rs in range(2):
                    th = E("T", lambda t, fs=fs, rs=rs: t.matmul(
                        PB[:, 0:512], v13[:, rs, 128 * fs:128 * (fs + 1)], PBUF[:, rs, :],
                        start=(rs == 0), stop=(rs == 1)),
                        waits=([t_v1, tpb, tok_PB_free] if fs == 0 else [tok_PB_free])
                        if rs == 0 else (),
                        inc=(rs == 1))
                tug = E("V", lambda v, fs=fs: v.tensor_scalar(
                    SLh(10, 0), PB[:, 0:512], FB1[:, fs:fs + 1], None, ALU.add, ALU.bypass),
                    waits=[th], inc=True)
                tok_PB_free = tug
                E("V", lambda v: v.tensor_tensor(SLh(10, 1), SLh(10, 0), SLh(10, 0), ALU.mult))
                E("V", lambda v: v.tensor_scalar(SLh(10, 1), SLh(10, 1), GC1, GC2,
                                                 ALU.mult, ALU.add))
                t3 = E("V", lambda v: v.tensor_tensor(SLh(10, 1), SLh(10, 1), SLh(10, 0),
                                                      ALU.mult), inc=True)
                tsg = E("S", lambda s: s.activation(SLh(9, 0), SLh(10, 1), AF.Exp,
                                                    scale=-2.0),
                        waits=[t3] + ([tok_sl_free[9]] if fs == 0 else []), inc=True)
                te1 = E("V", lambda v: v.tensor_single_scalar(SLh(9, 0), SLh(9, 0), 1.0,
                                                              ALU.add), waits=[tsg], inc=True)
                tr2 = E("V", lambda v: v.reciprocal(SLh(9, 1), SLh(9, 0)), inc=True)
                t_hh = E("V", lambda v, fs=fs: v.tensor_tensor(
                    HB[:, 512 * fs:512 * (fs + 1)], SLh(10, 0), SLh(9, 1), ALU.mult),
                    waits=[t_kc] if fs == 0 else [], inc=True)
            if stop_stage == 'ffh':
                stage_dump([t_hh, K.tok("S"), K.tok("T")])
                tok_x0_free = K.tok("G"); tok_h1_free = K.tok("G")
                tok_trf_free = t_trf; tok_nu_guard = K.tok("T")
                continue
            for rs in range(2):
                for fs in range(32):
                    E("T", lambda t, rs=rs, fs=fs: t.matmul(
                        PA[:, 512 * rs:512 * (rs + 1)], u23[:, fs, 128 * rs:128 * (rs + 1)],
                        HB[:, 512 * fs:512 * (fs + 1)], start=(fs == 0), stop=(fs == 31)),
                        waits=[t_u2, t_hh, tok_PA_free] if (rs == 0 and fs == 0) else (),
                        inc=(rs == 1 and fs == 31))
            tqb = E("S", lambda s: s.copy(PQB[:, :], PA[:]),
                    waits=[K.tok("T")], inc=True)
            tok_PA_free = tqb
            if stop_stage == 'ffq':
                stage_dump([tqb, K.tok("S"), K.tok("T")])
                tok_x0_free = K.tok("G"); tok_h1_free = K.tok("G")
                tok_trf_free = t_trf; tok_nu_guard = K.tok("T")
                continue
            t_fb2 = E("D", lambda d: d.dma_start(out=r_fb2, in_=fb2_d),
                      waits=[tok_mix_free, tok_state], inc=True, dma=True)
            prev_e = None
            for tc in range(4):
                tf = None
                for h in range(2):
                    for rs in range(2):
                        E("T", lambda t, h=h, rs=rs, tc=tc: t.matmul(
                            PB[:, 512 * h:512 * (h + 1)],
                            QBUF[:, rs, 128 * tc:128 * (tc + 1)],
                            v23[:, rs, 512 * h:512 * (h + 1)], start=(rs == 0), stop=False),
                            waits=[t_v2, tqb, tok_PB_free, t_fb2]
                            if (h == 0 and rs == 0) else ())
                    tf = E("T", lambda t, h=h: t.matmul(
                        PB[:, 512 * h:512 * (h + 1)], ONES1[:],
                        r_fb2[:, 512 * h:512 * (h + 1)], start=False, stop=True),
                        inc=(h == 1))
                if stop_stage == 'ffe1':
                    tok_PB_free = E("V", lambda v: v.tensor_copy(SL(10), PB[:]),
                                    waits=[tf], inc=True)
                    prev_e = tok_PB_free
                    continue
                st, rstd, nmr = ln_stats(PB[:], 5, [tf, prev_e], psum_src=True)
                if stop_stage == 'ffe2':
                    tok_PB_free = st[1]
                    prev_e = st[1]
                    continue
                tnm = E("S", lambda s, r=rstd, nm=nmr: s.activation(
                    SL(10), PB[:], AF.Identity, scale=r, bias=nm), waits=st, inc=True)
                tok_PB_free = tnm
                if stop_stage == 'ffe3':
                    prev_e = tnm
                    continue
                tout = E("G", lambda g, tc=tc: g.tensor_tensor(
                    H1[:, tc, :], H1[:, tc, :], SL(10), ALU.add),
                    waits=[tnm], inc=True)
                prev_e = tout
                E("G", lambda g, n=n, tc=tc: g.dma_start(out=out_r[n][tc], in_=H1[:, tc, :]),
                  inc=True, dma=True)
            if stop_stage in ('ffe1', 'ffe2', 'ffe3'):
                stage_dump([K.tok("S"), K.tok("V"), K.tok("T")])
            tok_x0_free = K.tok("G")
            tok_h1_free = K.tok("G")
            tok_nu_guard = K.tok("T")
            for i in range(10):
                tok_sl_free[i] = tqb
            tok_poolw_free = K.tok("T")
            t_cast_done = None

        E("G", lambda g: g.wait_ge(sem["G"], K.cnt["G"]))
        K.check()

        waited = {e: {} for e in Sched.ENGS}

        def replay(e, eng):
            for waits, fn, inc, amt, sm in K.q[e]:
                for we, wv in waits:
                    if waited[e].get(we, 0) < wv:
                        eng.wait_ge(sem[we], wv)
                        waited[e][we] = wv
                inst = fn(eng)
                if inc:
                    inst.then_inc(sem[sm], amt)

        @blk.tensor
        def _(tensor):
            replay("T", tensor)

        @blk.vector
        def _(vector):
            replay("V", vector)

        @blk.scalar
        def _(scalar):
            replay("S", scalar)

        @blk.sync
        def _(sync):
            replay("D", sync)

        @blk.gpsimd
        def _(gpsimd):
            replay("G", gpsimd)

    return nc


def make_inputs(inp, b, T):
    p = {k[8:]: np.asarray(v, np.float32) for k, v in inp.items()
         if k.startswith("L0_attn_") and k != "L0_attn_x"}
    f = {k[7:]: np.asarray(v, np.float32) for k, v in inp.items() if k.startswith("L0_ffn_")}
    x = np.asarray(inp["L0_attn_x"][b], np.float32)[:T]
    wck = np.ascontiguousarray(
        p["Wck"].T.reshape(8, 128, 32, 128).transpose(1, 2, 0, 3).reshape(128, 32768))
    d = {
        "x": np.ascontiguousarray(x),
        "WkT": _lhsT(p["Wk"].T, 8).reshape(128, -1),
        "WvT": _lhsT(p["Wv"].T, 8).reshape(128, -1),
        "WrT": _lhsT(p["Wr"].T, 8).reshape(128, -1),
        "WcrT": _lhsT(p["Wcr"].T, 8).reshape(128, -1),
        "WckT": wck,
        "WcvT": np.ascontiguousarray(
            p["Wcv"].T.astype(np.float32).reshape(32, 128, 2, 512)
            .transpose(1, 2, 0, 3).reshape(128, 32768)),
        "U1T": _lhsT(f["U1"], 8).reshape(128, -1),
        "V1T": _lhsT(f["V1"], 2).reshape(128, -1),
        "U2T": _lhsT(f["U2"], 32).reshape(128, -1),
        "V2T": _lhsT(f["V2"], 2).reshape(128, -1),
        "fmrows": np.ascontiguousarray(np.stack(
            [_fm(p["rk"]), _fm(p["rv"]), _fm(p["rr"]), _fm(p["crk"]), _fm(p["crr"]),
             _fm(p["ln1_g"]), _fm(p["ln1_b"]), _fm(p["ln2_g"]), _fm(p["ln2_b"])], axis=2)),
        "fb1fm": np.ascontiguousarray(np.asarray(f["b1"], np.float32).reshape(32, 128).T),
        "wrow": np.asarray(p["w"], np.float32).reshape(1, -1).copy(),
        "urow": np.asarray(p["u"], np.float32).reshape(1, -1).copy(),
        "fb2row": np.asarray(f["b2"], np.float32).reshape(1, -1).copy(),
        "Rf32": np.triu(np.ones((128, 128), np.float32)),
        "Rbf": np.triu(np.ones((128, 128))).astype(BF),
        "identbf": np.eye(128).astype(BF),
        "iotacol": np.arange(128, dtype=np.float32).reshape(-1, 1),
        "ones1": np.ones((1, 128), np.float32),
        "onescb": np.ones((128, 1)).astype(BF),
        "onescf": np.ones((128, 1), np.float32),
    }
    return d


def kernel(**inputs):
    T = 8192
    nc = build_program(T)
    in_maps = [make_inputs(inputs, b, T) for b in range(8)]
    res = run_bass_kernel_spmd(nc, in_maps, core_ids=list(range(8)))
    return np.stack([res.results[b]["out"] for b in range(8)], axis=0)


# version 08
# speedup vs baseline: 1.0665x; 1.0639x over previous
import numpy as np
import ml_dtypes
import concourse.bass as bass
import concourse.mybir as mybir
from concourse.bass_utils import run_bass_kernel_spmd
from contextlib import ExitStack

F32 = mybir.dt.float32
BF16 = mybir.dt.bfloat16
AF = mybir.ActivationFunctionType
ALU = mybir.AluOpType
BF = ml_dtypes.bfloat16

C = 1024
L = 128
TT_TILE = 512
NCHUNK = TT_TILE // L
PC_CLAMP = -15.0
GC1 = 0.044715 * 0.7978845608028654
GC2 = 0.7978845608028654


def _lhsT(Wcd, ng):
    return np.ascontiguousarray(
        np.asarray(Wcd, np.float32).reshape(ng, 128, -1).transpose(1, 0, 2))


def _fm(v):
    return np.ascontiguousarray(np.asarray(v, np.float32).reshape(8, 128).T)


class Sched:
    ENGS = "TVSDG"
    SEMS = "TVSDGP"

    def __init__(self, serialize=False):
        self.q = {e: [] for e in self.ENGS}
        self.cnt = {e: 0 for e in self.SEMS}
        self.serialize = serialize
        self.last = None

    def emit(self, e, fn, waits=(), inc=False, dma=False):
        w = [t for t in waits if t is not None]
        if self.serialize:
            inc = True
            if self.last is not None:
                w = w + [self.last]
        sem = e
        if e == "G" and not dma:
            sem = "P"
        amt = 16 if dma else 1
        self.q[e].append((w, fn, inc, amt, sem))
        if inc:
            self.cnt[sem] += amt
            tok = (sem, self.cnt[sem])
            if self.serialize:
                self.last = tok
            return tok
        return None

    def tok(self, e):
        return (e, self.cnt[e])

    def check(self):
        ptr = {e: 0 for e in self.ENGS}
        done = {s: 0 for s in self.SEMS}
        progress = True
        while progress:
            progress = False
            for e in self.ENGS:
                while ptr[e] < len(self.q[e]):
                    waits, _, inc, amt, sem = self.q[e][ptr[e]]
                    if any(done[we] < wv for we, wv in waits):
                        break
                    if inc:
                        done[sem] += amt
                    ptr[e] += 1
                    progress = True
        stuck = {e: (ptr[e], len(self.q[e])) for e in self.ENGS if ptr[e] < len(self.q[e])}
        if stuck:
            msgs = []
            for e in stuck:
                waits, _, _, _, _ = self.q[e][ptr[e]]
                unmet = [(we, wv, done[we]) for we, wv in waits if done[we] < wv]
                msgs.append(f"{e}@{ptr[e]}/{len(self.q[e])} needs {unmet}")
            raise RuntimeError("deadlock: " + "; ".join(msgs))


def build_program(T, dbg=False, serialize=False, stop_stage='out'):
    nt = T // TT_TILE
    nc = bass.Bass("TRN2", target_bir_lowering=False, detect_race_conditions=False)

    def din(name, shape, dtype=mybir.dt.float32):
        return nc.dram_tensor(name, list(shape), dtype, kind="ExternalInput").ap()

    x_d = din("x", [T, C])
    out_d = nc.dram_tensor("out", [T, C], mybir.dt.float32, kind="ExternalOutput").ap()
    wk_d = din("WkT", [128, 8192]); wv_d = din("WvT", [128, 8192])
    wr_d = din("WrT", [128, 8192]); wcr_d = din("WcrT", [128, 8192])
    wck_d = din("WckT", [128, 32768]); wcv_d = din("WcvT", [128, 32768])
    u1_d = din("U1T", [128, 2048]); v1_d = din("V1T", [128, 8192])
    u2_d = din("U2T", [128, 8192]); v2_d = din("V2T", [128, 2048])
    fmr_d = din("fmrows", [128, 8, 9])
    fb1_d = din("fb1fm", [128, 32])
    wrow_d = din("wrow", [1, 1024]); urow_d = din("urow", [1, 1024])
    fb2_d = din("fb2row", [1, 1024])
    Rf_d = din("Rf32", [128, 128]); Rb_d = din("Rbf", [128, 128], BF16)
    id_d = din("identbf", [128, 128], BF16)
    iota_d = din("iotacol", [128, 1]); ones1_d = din("ones1", [1, 128])
    onescb_d = din("onescb", [128, 1], BF16); onescf_d = din("onescf", [128, 1])
    wscr = {}
    for nm, w in [("WkT", 8192), ("WvT", 8192), ("WrT", 8192), ("WcrT", 8192),
                  ("WckT", 32768), ("WcvT", 32768), ("U1T", 2048), ("V1T", 8192),
                  ("U2T", 8192), ("V2T", 2048)]:
        wscr[nm] = nc.dram_tensor("s_" + nm, [128, w], BF16).ap()
    wraw = {"WkT": wk_d, "WvT": wv_d, "WrT": wr_d, "WcrT": wcr_d, "WckT": wck_d,
            "WcvT": wcv_d, "U1T": u1_d, "V1T": v1_d, "U2T": u2_d, "V2T": v2_d}
    dbg_d = {}
    if dbg:
        for nm in ["h1", "x1", "attn", "x2"]:
            dbg_d[nm] = nc.dram_tensor("dbg_" + nm, [T, C], mybir.dt.float32,
                                       kind="ExternalOutput").ap()

    es = ExitStack()
    with es:
        sb = lambda n, s, d=F32: es.enter_context(nc.sbuf_tensor(n, list(s), d))
        pst = lambda n, s, d=F32: es.enter_context(nc.psum_tensor(n, list(s), d))
        POOLW = sb("poolw", [128, 24576], BF16)
        SA = sb("sarena", [128, 11, 1024])
        SAb = SA.bitcast(BF16)
        GB = sb("gbpair", [128, 2048], BF16)
        MIX = sb("mix", [128, 3, 8, 512], BF16)
        TRF = sb("trfm", [128, 8, 513], BF16)
        TMS = sb("tmstage", [128, 1024], BF16)
        XC = sb("xc", [128, 1024])
        H1 = sb("h1t", [128, 4, 1024])
        KCA = sb("kcarena", [128, 16384], BF16)
        PQB = sb("pqbuf", [128, 1024], BF16)
        ROWA = sb("rowa", [128, 1024]); ROWB = sb("rowb", [128, 1024])
        ROWC = sb("rowc", [128, 1024])
        STAT = sb("stat", [128, 64])
        IOTW = sb("iotw", [128, 1024]); EU = sb("eu", [128, 1024])
        RF = sb("rf", [128, 128]); RB = sb("rb", [128, 128], BF16)
        IDB = sb("idb", [128, 128], BF16)
        ONES1 = sb("onesone", [1, 128]); IOTAC = sb("iotac", [128, 1])
        ONESCB = sb("onescbs", [128, 1], BF16); ONESCF = sb("onescfs", [128, 1])
        MIXCO = sb("mixco", [128, 8, 17])
        FB1 = sb("fb1", [128, 32])
        CAR1 = sb("car1", [128, 8, 1]); CAR2 = sb("car2", [128, 8, 1])
        CONSTS = sb("consts", [128, 3])
        nc.const_aps.register(F32, 1e-5, CONSTS[:, 0:1])
        nc.const_aps.register(F32, 1e-38, CONSTS[:, 1:2])
        nc.const_aps.register(F32, 1.0, CONSTS[:, 2:3])
        PA = pst("PA", [128, 1024]); PB = pst("PB", [128, 1024])
        PT = pst("PT", [128, 1024], BF16)
        PTOT = pst("PTOT", [1, 1024])
        sem = {e: es.enter_context(nc.semaphore("sem" + e)) for e in Sched.SEMS}
        blk = es.enter_context(nc.Block())

        K = Sched(serialize=serialize)
        E = K.emit
        FMR = SA[:, 10, 0:72].rearrange("p (a k) -> p a k", a=8)
        r_phc = ROWA[0:1, :]; r_phat = ROWA[32:33, :]; r_s0 = ROWA[64:65, :]
        r_mu = ROWB[0:1, :]; r_nhat = ROWB[32:33, :]; r_s1 = ROWB[64:65, :]
        r_nu = ROWC[0:1, :]; r_Lw = ROWC[32:33, :]; r_s2 = ROWC[64:65, :]
        r_wu = ROWA[0:1, :]; r_fb2 = ROWC[0:1, :]; r_s3 = ROWB[64:65, :]
        r_phat32 = ROWA[32:33, :]; r_lnt2 = ROWB[32:33, :]
        SL = lambda i: SA[:, i, :]
        SLh = lambda i, h: SA[:, i, 512 * h:512 * (h + 1)]

        ldt = []
        for dst, src in [(RF[:], Rf_d), (RB[:], Rb_d), (IDB[:], id_d),
                         (ONES1[:], ones1_d), (IOTAC[:], iota_d),
                         (ONESCB[:], onescb_d), (ONESCF[:], onescf_d),
                         (FMR, fmr_d), (FB1[:], fb1_d),
                         (r_wu, wrow_d), (r_s3, urow_d)]:
            ldt.append(E("D", lambda d, o=dst, s=src: d.dma_start(out=o, in_=s),
                         inc=True, dma=True))
        tld = ldt[-1]
        t_lw = E("S", lambda s: s.activation(r_Lw, r_wu, AF.Identity, scale=float(L)),
                 waits=[tld], inc=True)
        t_eur = E("S", lambda s: s.activation(r_mu, r_s3, AF.Exp), inc=True)
        E("T", lambda t: t.matmul(PA[:, 0:512], ONES1[:], r_mu[:, 0:512],
                                  start=True, stop=True), waits=[t_eur])
        t_mm1 = E("T", lambda t: t.matmul(PA[:, 512:1024], ONES1[:], r_mu[:, 512:1024],
                                          start=True, stop=True), inc=True)
        E("T", lambda t: t.matmul(PB[:, 0:512], ONES1[:], r_wu[:, 0:512],
                                  start=True, stop=True), waits=[tld])
        t_mm2 = E("T", lambda t: t.matmul(PB[:, 512:1024], ONES1[:], r_wu[:, 512:1024],
                                          start=True, stop=True), inc=True)
        t_eu = E("V", lambda v: v.tensor_copy(EU[:], PA[:]), waits=[t_mm1], inc=True)
        t_iotw = E("V", lambda v: v.tensor_scalar(IOTW[:], PB[:], IOTAC[:, 0:1], None,
                                                  ALU.mult, ALU.bypass),
                   waits=[t_mm2], inc=True)
        fv = lambda i: FMR[:, :, i]
        mco = lambda i: MIXCO[:, :, i]
        tpm = None
        for dsti, ai, bi in [(0, 0, 5), (2, 1, 5), (4, 2, 5), (7, 3, 7), (9, 4, 7)]:
            tpm = E("V", lambda v, d=dsti, a=ai, b=bi: v.tensor_tensor(mco(d), fv(a), fv(b),
                                                                       ALU.mult),
                    waits=[tld], inc=True)
        for dsti, gi, pi in [(1, 5, 0), (3, 5, 2), (5, 5, 4), (8, 7, 7), (10, 7, 9)]:
            tpm = E("V", lambda v, d=dsti, g=gi, p=pi: v.tensor_tensor(mco(d), fv(g), mco(p),
                                                                       ALU.subtract),
                    waits=[tpm], inc=True)
        E("V", lambda v: v.tensor_copy(mco(6), fv(6)))
        E("V", lambda v: v.tensor_copy(mco(11), fv(8)))
        for dsti, ri, bi in [(12, 0, 6), (13, 1, 6), (14, 2, 6), (15, 3, 8), (16, 4, 8)]:
            tpm = E("V", lambda v, d=dsti, r=ri, b=bi: v.tensor_tensor(mco(d), fv(r), fv(b),
                                                                       ALU.mult),
                    waits=[tpm], inc=True)
        E("V", lambda v: v.memset(CONSTS[:, 0:1], 1e-5))
        E("V", lambda v: v.memset(CONSTS[:, 1:2], 1e-38))
        E("V", lambda v: v.memset(CONSTS[:, 2:3], 1.0))
        E("V", lambda v: v.memset(r_phat, -10000.0))
        E("V", lambda v: v.memset(r_nhat, 0.0))
        E("V", lambda v: v.memset(CAR1[:], 0.0))
        t_init = E("V", lambda v: v.memset(CAR2[:], 0.0), inc=True)

        jobs = []
        for nm in ["WkT", "WvT", "WrT", "WcrT", "WckT", "WcvT", "U1T", "V1T", "U2T", "V2T"]:
            tot = wraw[nm].shape[1]
            for c0 in range(0, tot, 2048):
                jobs.append((wraw[nm][:, c0:c0 + 2048], wscr[nm][:, c0:c0 + 2048]))
        prev_out = [None, None]
        t_cast_done = None
        for i, (srcap, dstap) in enumerate(jobs):
            b = i % 2
            stg = SA[:, 2 * b:2 * b + 2, :].rearrange("p a k -> p (a k)")
            tin = E("D", lambda d, s=srcap, g=stg: d.dma_start(out=g, in_=s),
                    waits=[prev_out[b]], inc=True, dma=True)
            tout = E("G", lambda g, s=stg, dd=dstap: g.dma_start(out=dd, in_=s),
                     waits=[tin], inc=True, dma=True)
            prev_out[b] = tout
            t_cast_done = tout

        last_ln = [t_init]

        def ln_stats(src_ap, idx, tok_in, psum_src=False):
            sm = STAT[:, 4 * idx:4 * idx + 1]
            sq = STAT[:, 4 * idx + 2:4 * idx + 3]
            tsq = E("S", lambda s, a=src_ap: s.activation(SL(10), a, AF.Square),
                    waits=list(tok_in) + [last_ln[0]], inc=True)
            t1 = E("V", lambda v, a=src_ap, o=sm: v.reduce_sum(o, a, axis=mybir.AxisListType.X),
                   waits=list(tok_in) + [last_ln[0]] + ([tsq] if psum_src else []), inc=True)
            t2 = E("V", lambda v, o=sq: v.reduce_sum(o, SL(10), axis=mybir.AxisListType.X),
                   waits=[tsq], inc=True)
            m = STAT[:, 32:33]; v2 = STAT[:, 36:37]; rstd = STAT[:, 40:41]
            nmr = STAT[:, 44:45]; tmp = STAT[:, 48:49]; tmp2 = STAT[:, 52:53]
            ta = E("V", lambda v: v.tensor_single_scalar(m, sm, 1.0 / 1024.0, ALU.mult),
                   waits=[t1, t2], inc=True)
            tb = E("V", lambda v: v.tensor_single_scalar(v2, sq, 1.0 / 1024.0, ALU.mult),
                   waits=[ta], inc=True)
            tc2 = E("V", lambda v: v.tensor_tensor(tmp, m, m, ALU.mult), waits=[tb], inc=True)
            td2 = E("V", lambda v: v.tensor_tensor(v2, v2, tmp, ALU.subtract),
                    waits=[tc2], inc=True)
            tv = E("V", lambda v: v.tensor_single_scalar(v2, v2, 1e-5, ALU.add),
                   waits=[td2], inc=True)
            E("S", lambda s: s.activation(tmp2, v2, AF.Ln), waits=[tv])
            tr = E("S", lambda s: s.activation(rstd, tmp2, AF.Exp, scale=-0.5), inc=True)
            te2 = E("V", lambda v: v.tensor_tensor(tmp, m, rstd, ALU.mult), waits=[tr], inc=True)
            tn = E("V", lambda v: v.tensor_single_scalar(nmr, tmp, -1.0, ALU.mult),
                   waits=[te2], inc=True)
            last_ln[0] = tn
            return [tr, tn], rstd, nmr

        def transpose_to_fm(tm_bf_ap, dst_fm_ap, waits, pt_free):
            tt = None
            for a in range(8):
                tt = E("T", lambda t, a=a, s=tm_bf_ap: t.transpose(
                    PT[:, 128 * a:128 * (a + 1)], s[:, 128 * a:128 * (a + 1)], IDB[:]),
                    waits=(list(waits) + [pt_free]) if a == 0 else (), inc=(a == 7))
            return E("S", lambda s, d=dst_fm_ap: s.copy(
                d, PT[:, :].rearrange("p (a t) -> p a t", a=8)), waits=[tt], inc=True)

        tok_PA_free = t_eu; tok_PB_free = t_iotw
        tok_PT_free = K.tok("V")
        tok_PTOT_free = t_init
        tok_x0_free = None
        tok_h1_free = None
        tok_nu_guard = None
        tok_prol_guard = t_eur
        tok_mix_free = None
        tok_state = t_init
        tok_trf_free = None
        tok_gb_free = None
        tok_gb_free2 = None
        tok_sl_free = {i: None for i in range(11)}
        tok_poolw_free = None

        x_r2 = x_d.rearrange("(n tc p) c -> n tc p c", p=128, tc=4)
        out_r = out_d.rearrange("(n tc p) c -> n tc p c", p=128, tc=4)
        dbg_r = {k2: v2.rearrange("(n tc p) c -> n tc p c", p=128, tc=4)
                 for k2, v2 in dbg_d.items()}

        def load_w(dst2d, nm, waits):
            return E("D", lambda d, o=dst2d, s=wscr[nm]: d.dma_start(out=o, in_=s),
                     waits=waits, inc=True, dma=True)

        for n in range(nt):
            wk_sb = POOLW[:, 0:8192]; wv_sb = POOLW[:, 8192:16384]; wr_sb = POOLW[:, 16384:24576]
            t_wk = load_w(wk_sb, "WkT", [tok_poolw_free, t_cast_done])
            t_wv = load_w(wv_sb, "WvT", [])
            t_wr = load_w(wr_sb, "WrT", [])
            wk3 = wk_sb.rearrange("p (a k) -> p a k", a=8)
            wv3 = wv_sb.rearrange("p (a k) -> p a k", a=8)
            wr3 = wr_sb.rearrange("p (a k) -> p a k", a=8)

            tcar = E("V", lambda v: v.tensor_copy(TRF[:, :, 0:1], CAR1[:]),
                     waits=[tok_trf_free, t_init], inc=True)
            t_h1 = []
            t_xc = None
            for tc in range(4):
                x0c = XC[:, :]; h1c = H1[:, tc, :]
                t_x = E("D", lambda d, n=n, tc=tc: d.dma_start(out=XC[:], in_=x_r2[n][tc]),
                        waits=[t_xc, tok_x0_free] + ([t_cast_done] if tc == 0 else []),
                        inc=True, dma=True)
                st, rstd, nmr = ln_stats(x0c, 0, [t_x])
                th = E("S", lambda s, a=x0c, o=h1c, r=rstd, nm=nmr: s.activation(
                    o, a, AF.Identity, scale=r, bias=nm), waits=st + [tok_h1_free], inc=True)
                t_xc = th
                if stop_stage == 'h1x':
                    t_h1.append(th)
                    t_xc = th
                    continue
                st2, rstd2, nmr2 = ln_stats(h1c, 1, [th])
                tx = E("S", lambda s, a=h1c, r=rstd2, nm=nmr2: s.activation(
                    TMS[:], a, AF.Identity, scale=r, bias=nm), waits=st2, inc=True)
                te = transpose_to_fm(TMS[:], TRF[:, :, 1 + 128 * tc:1 + 128 * (tc + 1)],
                                     [tx, tcar], tok_PT_free)
                tok_PT_free = te
                t_h1.append(th)
                if dbg:
                    E("G", lambda g, n=n, tc=tc, a=h1c: g.dma_start(
                        out=dbg_r["h1"][n][tc], in_=a), waits=[th], inc=True, dma=True)
            t_trf = tok_PT_free

            def stage_dump(waits):
                tt = None
                for tc2 in range(4):
                    tt = E("G", lambda g, n=n, tc2=tc2: g.dma_start(out=out_r[n][tc2],
                                                                   in_=H1[:, tc2, :]),
                           waits=list(waits), inc=True, dma=True)
                return tt

            if stop_stage in ('h1', 'h1x'):
                stage_dump([K.tok("S"), K.tok("V")])
                tok_x0_free = K.tok("G"); tok_h1_free = K.tok("G")
                tok_poolw_free = K.tok("T"); tok_mix_free = K.tok("T")
                tok_trf_free = t_trf
                continue

            tmix = None
            for mi, (cg, cg1) in enumerate([(0, 1), (2, 3), (4, 5)]):
                for a in range(8):
                    cur = TRF[:, a, 1:513]; prv = TRF[:, a, 0:512]
                    o = MIX[:, mi, a, :]
                    E("V", lambda v, c=cur, g=cg, a=a: v.tensor_scalar(
                        SLh(10, 0), c, MIXCO[:, a, g:g + 1], MIXCO[:, a, 6:7],
                        ALU.mult, ALU.add),
                      waits=[t_trf, tok_mix_free] if (mi == 0 and a == 0) else ())
                    E("V", lambda v, p=prv, g=cg1, a=a: v.tensor_scalar(
                        SLh(10, 1), p, MIXCO[:, a, g:g + 1], None, ALU.mult, ALU.bypass))
                    tmix = E("V", lambda v, o=o: v.tensor_tensor(o, SLh(10, 0), SLh(10, 1),
                                                                 ALU.add), inc=True)
                    if n == 0:
                        tmix = E("V", lambda v, o=o, c=cur, g=cg, a=a, mi=mi: v.tensor_scalar(
                            o[:, 0:1], c[:, 0:1], MIXCO[:, a, g:g + 1],
                            MIXCO[:, a, 12 + mi:13 + mi], ALU.mult, ALU.add), inc=True)
            tcarS = E("V", lambda v: v.tensor_copy(CAR1[:], TRF[:, :, 512:513]), inc=True)
            t_mixA = tmix

            for tc in range(NCHUNK):
                def ddump(name, ap, tok, dt2=F32, _n=n, _tc=tc):
                    if not (dbg and _n == 0 and _tc == 0):
                        return None
                    dd = nc.dram_tensor("dbgw_" + name, list(ap.shape), dt2,
                                        kind="ExternalOutput").ap()
                    return E("G", lambda g, a=ap, o=dd: g.dma_start(out=o, in_=a),
                             waits=[tok], inc=True, dma=True)
                ts1 = E("V", lambda v: v.tensor_single_scalar(r_phc, r_phat, PC_CLAMP, ALU.max),
                        waits=[tok_state, tok_prol_guard], inc=True)
                tok_prol_guard = None
                ts2 = E("V", lambda v: v.tensor_scalar(r_s0, r_phat, -PC_CLAMP, 0.0,
                                                       ALU.add, ALU.min), waits=[ts1], inc=True)
                ts3a = E("S", lambda s: s.activation(r_mu, r_s0, AF.Exp), waits=[ts2], inc=True)
                ts3 = E("V", lambda v: v.tensor_single_scalar(r_mu, r_mu, 1e-38, ALU.add),
                       waits=[ts3a], inc=True)
                ts4a = E("V", lambda v: v.tensor_copy(r_nu, r_nhat),
                         waits=[tok_nu_guard, ts3], inc=True)
                ts4 = E("V", lambda v: v.tensor_tensor(r_nu, r_mu, r_nu, ALU.mult),
                        waits=[ts4a], inc=True)
                tok_nu_guard = None
                tbc = None
                for h in range(2):
                    tbc = E("T", lambda t, h=h: t.matmul(
                        PA[:, 512 * h:512 * (h + 1)], ONES1[:],
                        r_phc[:, 512 * h:512 * (h + 1)], start=True, stop=True),
                        waits=[ts2, tok_PA_free] if h == 0 else (), inc=(h == 1))
                tk = None
                for h in range(2):
                    for a in range(8):
                        tk = E("T", lambda t, h=h, a=a, tc=tc: t.matmul(
                            PB[:, 512 * h:512 * (h + 1)],
                            MIX[:, 0, a, 128 * tc:128 * (tc + 1)],
                            wk3[:, a, 512 * h:512 * (h + 1)],
                            start=(a == 0), stop=(a == 7)),
                            waits=[t_mixA, t_wk, tok_PB_free] if (h == 0 and a == 0) else (),
                            inc=(h == 1 and a == 7))
                tv1 = E("V", lambda v: v.tensor_tensor(SL(0), PB[:], IOTW[:], ALU.subtract),
                        waits=[tk, tok_sl_free[0]], inc=True)
                tz = E("V", lambda v: v.tensor_tensor(SL(1), SL(0), PA[:], ALU.subtract),
                       waits=[tbc, tok_sl_free[1]], inc=True)
                tok_PB_free = tz; tok_PA_free = tz
                tvp = None
                for h in range(2):
                    for a in range(8):
                        tvp = E("T", lambda t, h=h, a=a, tc=tc: t.matmul(
                            PA[:, 512 * h:512 * (h + 1)],
                            MIX[:, 1, a, 128 * tc:128 * (tc + 1)],
                            wv3[:, a, 512 * h:512 * (h + 1)],
                            start=(a == 0), stop=(a == 7)),
                            waits=[tz, t_wv] if (h == 0 and a == 0) else (),
                            inc=(h == 1 and a == 7))
                tg = E("S", lambda s: s.activation(SL(2), SL(1), AF.Exp),
                       waits=[tz, tok_sl_free[2]], inc=True)
                tgb = E("S", lambda s: s.copy(GB[:, 0:1024], SL(2)),
                        waits=[tg, tok_gb_free, tok_gb_free2], inc=True)
                tvf = E("S", lambda s: s.copy(SL(3), PA[:]), waits=[tvp, tok_sl_free[3]],
                        inc=True)
                tok_PA_free = tvf
                trp = None
                for h in range(2):
                    for a in range(8):
                        trp = E("T", lambda t, h=h, a=a, tc=tc: t.matmul(
                            PA[:, 512 * h:512 * (h + 1)],
                            MIX[:, 2, a, 128 * tc:128 * (tc + 1)],
                            wr3[:, a, 512 * h:512 * (h + 1)],
                            start=(a == 0), stop=(a == 7)),
                            waits=[tvf, t_wr] if (h == 0 and a == 0) else (),
                            inc=(h == 1 and a == 7))
                tF = None
                for h in range(2):
                    E("T", lambda t, h=h: t.matmul(
                        PB[:, 512 * h:512 * (h + 1)], RB[:], GB[:, 512 * h:512 * (h + 1)],
                        start=True, stop=False), waits=[tgb] if h == 0 else ())
                    tF = E("T", lambda t, h=h: t.matmul(
                        PB[:, 512 * h:512 * (h + 1)], ONES1[:],
                        r_mu[:, 512 * h:512 * (h + 1)], start=False, stop=True),
                        waits=[ts3] if h == 0 else (), inc=(h == 1))
                tFt = None
                for h in range(2):
                    tFt = E("T", lambda t, h=h: t.matmul(
                        PTOT[0:1, 512 * h:512 * (h + 1)], ONESCB[:],
                        GB[:, 512 * h:512 * (h + 1)], start=True, stop=True),
                        waits=[tok_PTOT_free] if h == 0 else (), inc=(h == 1))
                ter = E("S", lambda s: s.activation(SL(4), PA[:], AF.Exp, scale=-1.0),
                        waits=[trp, tok_sl_free[4]], inc=True)
                tok_PA_free = ter
                tFE = E("V", lambda v: v.tensor_tensor(SL(5), PB[:], GB[:, 0:1024],
                                                       ALU.subtract),
                        waits=[tF, tok_sl_free[5]], inc=True)
                tok_PB_free = tFE
                tt3 = E("S", lambda s: s.activation(SL(6), SL(5), AF.Ln),
                        waits=[tFE, tok_sl_free[6]], inc=True)
                td = E("V", lambda v: v.tensor_tensor(SL(7), SL(6), SL(1), ALU.subtract),
                       waits=[tt3, tok_sl_free[7]], inc=True)
                tad = E("S", lambda s: s.activation(SL(6), SL(7), AF.Abs),
                        waits=[td], inc=True)
                ted = E("S", lambda s: s.activation(SL(7), SL(6), AF.Exp, scale=-1.0),
                        waits=[tad], inc=True)
                tp1 = E("V", lambda v: v.tensor_single_scalar(SL(7), SL(7), 1.0, ALU.add),
                        waits=[ted], inc=True)
                tgam = E("S", lambda s: s.activation(SL(6), SL(7), AF.Ln), waits=[tp1], inc=True)
                tst5 = E("V", lambda v: v.tensor_tensor(r_s0, r_mu, PTOT[0:1, :], ALU.add),
                         waits=[tFt], inc=True)
                tst5b = E("V", lambda v: v.reciprocal(r_s1, r_s0), waits=[tst5], inc=True)
                tst6 = E("S", lambda s: s.activation(r_lnt2, r_s0, AF.Ln), waits=[tst5b],
                         inc=True)
                tst7a = E("V", lambda v: v.tensor_copy(r_phat32, r_phc), waits=[tst6], inc=True)
                tst7b = E("V", lambda v: v.tensor_tensor(r_phat32, r_phat32, r_Lw, ALU.add),
                         waits=[tst7a], inc=True)
                tst8 = E("V", lambda v: v.tensor_tensor(r_phat, r_phat32, r_lnt2, ALU.add),
                        waits=[tst7b], inc=True)
                tQ = None
                for h in range(2):
                    tQ = E("T", lambda t, h=h: t.matmul(
                        PB[:, 512 * h:512 * (h + 1)], RF[:], SLh(6, h), start=True, stop=True),
                        waits=[tgam, tFE] if h == 0 else (), inc=(h == 1))
                tQt = None
                for h in range(2):
                    tQt = E("T", lambda t, h=h: t.matmul(
                        PTOT[0:1, 512 * h:512 * (h + 1)], ONESCF[:], SLh(6, h),
                        start=True, stop=True),
                        waits=[tst5, tgam] if h == 0 else (), inc=(h == 1))
                tqp = E("V", lambda v: v.tensor_tensor(SL(8), PB[:], SL(6), ALU.subtract),
                        waits=[tQ, tok_sl_free[8]], inc=True)
                tok_PB_free = tqp
                tEQ = E("S", lambda s: s.activation(SL(9), SL(8), AF.Exp),
                        waits=[tqp, tok_sl_free[9]], inc=True)
                dz = ddump("z", SL(1), tz)
                dg = ddump("g", SL(2), tg)
                dfe = ddump("fe", SL(5), tFE)
                dgam = ddump("gam", SL(6), tgam)
                dqp = ddump("qp", SL(8), tqp)
                tzq = E("V", lambda v: v.tensor_tensor(SL(1), SL(1), SL(8), ALU.subtract),
                        waits=[tqp, dz], inc=True)
                tgt = E("S", lambda s: s.activation(SL(0), SL(1), AF.Exp), waits=[tzq],
                        inc=True)
                tgtv = E("V", lambda v: v.tensor_tensor(GB[:, 1024:2048], SL(0), SL(3),
                                                        ALU.mult),
                         waits=[tgt, tvf, tok_gb_free, tok_gb_free2], inc=True)
                tP = None
                for h in range(2):
                    E("T", lambda t, h=h: t.matmul(
                        PA[:, 512 * h:512 * (h + 1)], RB[:],
                        GB[:, 1024 + 512 * h:1024 + 512 * (h + 1)], start=True, stop=False),
                      waits=[tgtv, ter] if h == 0 else ())
                    tP = E("T", lambda t, h=h: t.matmul(
                        PA[:, 512 * h:512 * (h + 1)], ONES1[:],
                        r_nu[:, 512 * h:512 * (h + 1)], start=False, stop=True),
                        waits=[ts4] if h == 0 else (), inc=(h == 1))
                tst10 = E("S", lambda s: s.activation(r_s2, PTOT[0:1, :], AF.Exp),
                          waits=[tQt], inc=True)
                tPt = None
                for h in range(2):
                    tPt = E("T", lambda t, h=h: t.matmul(
                        PTOT[0:1, 512 * h:512 * (h + 1)], ONESCB[:],
                        GB[:, 1024 + 512 * h:1024 + 512 * (h + 1)], start=True, stop=True),
                        waits=[tst10, tgtv] if h == 0 else (), inc=(h == 1))
                deq = ddump("eq", SL(9), tEQ)
                dgtv = ddump("gtv", GB[:, 1024:2048], tgtv, BF16)
                ten = E("V", lambda v: v.tensor_tensor(SL(7), EU[:], SL(2), ALU.mult),
                        waits=[tgam], inc=True)
                tden = E("V", lambda v: v.tensor_tensor(SL(5), SL(5), SL(7), ALU.add),
                         waits=[ten, tt3, dfe], inc=True)
                te1p = E("V", lambda v: v.tensor_single_scalar(SL(4), SL(4), 1.0, ALU.add),
                         waits=[ter], inc=True)
                tden2 = E("V", lambda v: v.tensor_tensor(SL(5), SL(5), SL(4), ALU.mult),
                          inc=True)
                dden2 = ddump("den2", SL(5), tden2)
                tln5 = E("S", lambda s: s.activation(SL(4), SL(5), AF.Ln),
                         waits=[tden2, dqp, dden2], inc=True)
                trec = E("S", lambda s: s.activation(SL(8), SL(4), AF.Exp, scale=-1.0),
                         waits=[tln5, tEQ], inc=True)
                tn1 = E("V", lambda v: v.tensor_tensor(SL(6), PA[:], GB[:, 1024:2048],
                                                       ALU.subtract),
                       waits=[tP, tQt, dgam], inc=True)
                tok_PA_free = tn1
                tn2 = E("V", lambda v: v.tensor_tensor(SL(6), SL(6), SL(9), ALU.mult),
                        inc=True)
                tdn = E("V", lambda v: v.tensor_tensor(SL(7), SL(7), SL(3), ALU.mult),
                       waits=[tden], inc=True)
                tnum = E("V", lambda v: v.tensor_tensor(SL(6), SL(6), SL(7), ALU.add),
                         inc=True)
                dnum = ddump("num", SL(6), tnum)
                tyz = E("V", lambda v: v.tensor_tensor(SL(6), SL(6), SL(8), ALU.mult),
                        waits=[trec, dnum], inc=True)
                tx1 = E("V", lambda v, tc=tc: v.tensor_tensor(
                    H1[:, tc, :], H1[:, tc, :], SL(6), ALU.add),
                    waits=[t_h1[tc]], inc=True)
                tst11 = E("V", lambda v: v.tensor_tensor(r_s0, r_nu, PTOT[0:1, :], ALU.add),
                          waits=[tPt, tst8], inc=True)
                tok_PTOT_free = tst11
                tst12 = E("V", lambda v: v.tensor_tensor(r_s0, r_s0, r_s2, ALU.mult),
                         waits=[tst11], inc=True)
                tst13 = E("V", lambda v: v.tensor_tensor(r_nhat, r_s0, r_s1, ALU.mult),
                         waits=[tst12], inc=True)
                tok_state = tst13
                tok_sl_free[0] = tgtv; tok_sl_free[1] = tgt; tok_sl_free[2] = tdn
                tok_sl_free[3] = tdn; tok_sl_free[4] = trec; tok_sl_free[5] = tln5
                tok_sl_free[6] = tyz; tok_sl_free[7] = tnum; tok_sl_free[8] = tyz
                tok_sl_free[9] = tn2
                tok_gb_free = tn1; tok_gb_free2 = tPt
                if dbg:
                    E("G", lambda g, n=n, tc=tc: g.dma_start(
                        out=dbg_r["x1"][n][tc], in_=H1[:, tc, :]),
                      waits=[tx1], inc=True, dma=True)
            t_x1 = K.tok("V")
            tok_mix_free = K.tok("T")
            tok_poolw_free = K.tok("T")

            if stop_stage == 'x1':
                stage_dump([t_x1, K.tok("S")])
                tok_x0_free = K.tok("G"); tok_h1_free = K.tok("G")
                tok_trf_free = t_trf
                continue

            tcar2 = E("V", lambda v: v.tensor_copy(TRF[:, :, 0:1], CAR2[:]),
                      waits=[tcarS], inc=True)
            for tc in range(4):
                x1c = H1[:, tc, :]
                st, rstd, nmr = ln_stats(x1c, 2, [t_x1])
                tx = E("S", lambda s, a=x1c, r=rstd, nm=nmr: s.activation(
                    TMS[:], a, AF.Identity, scale=r, bias=nm), waits=st, inc=True)
                te = transpose_to_fm(TMS[:], TRF[:, :, 1 + 128 * tc:1 + 128 * (tc + 1)],
                                     [tx, tcar2], tok_PT_free)
                tok_PT_free = te
            t_trf2 = tok_PT_free
            tmixb = None
            for mi, (cg, cg1) in enumerate([(7, 8), (9, 10)]):
                for a in range(8):
                    cur = TRF[:, a, 1:513]; prv = TRF[:, a, 0:512]
                    o = MIX[:, mi, a, :]
                    E("V", lambda v, c=cur, g=cg, a=a: v.tensor_scalar(
                        SLh(10, 0), c, MIXCO[:, a, g:g + 1], MIXCO[:, a, 11:12],
                        ALU.mult, ALU.add),
                      waits=[t_trf2, tok_mix_free] if (mi == 0 and a == 0) else ())
                    E("V", lambda v, p=prv, g=cg1, a=a: v.tensor_scalar(
                        SLh(10, 1), p, MIXCO[:, a, g:g + 1], None, ALU.mult, ALU.bypass))
                    tmixb = E("V", lambda v, o=o: v.tensor_tensor(o, SLh(10, 0), SLh(10, 1),
                                                                  ALU.add), inc=True)
                    if n == 0:
                        tmixb = E("V", lambda v, o=o, c=cur, g=cg, a=a, mi=mi: v.tensor_scalar(
                            o[:, 0:1], c[:, 0:1], MIXCO[:, a, g:g + 1],
                            MIXCO[:, a, 15 + mi:16 + mi], ALU.mult, ALU.add), inc=True)
            tcarS2 = E("V", lambda v: v.tensor_copy(CAR2[:], TRF[:, :, 512:513]), inc=True)
            t_mixB = tmixb

            wcr_sb = POOLW[:, 0:8192]
            t_wcr = load_w(wcr_sb, "WcrT", [tok_poolw_free])
            wcr3 = wcr_sb.rearrange("p (a k) -> p a k", a=8)
            wckh_sb = POOLW[:, 8192:24576]
            ECR = SAb[:, 4:6, :].rearrange("p a k -> p (a k)")
            tecr = []
            for tc in range(4):
                trc = None
                for h in range(2):
                    for a in range(8):
                        trc = E("T", lambda t, h=h, a=a, tc=tc: t.matmul(
                            PB[:, 512 * h:512 * (h + 1)],
                            MIX[:, 1, a, 128 * tc:128 * (tc + 1)],
                            wcr3[:, a, 512 * h:512 * (h + 1)],
                            start=(a == 0), stop=(a == 7)),
                            waits=[t_mixB, t_wcr, tok_PB_free] if (h == 0 and a == 0) else (),
                            inc=(h == 1 and a == 7))
                t_e = E("S", lambda s, tc=tc: s.activation(
                    ECR[:, 1024 * tc:1024 * (tc + 1)], PB[:], AF.Exp, scale=-1.0),
                    waits=[trc] + ([tok_sl_free[4], tok_sl_free[5]] if tc == 0 else []),
                    inc=True)
                tok_PB_free = t_e
                tecr.append(t_e)
            KC = KCA[:, :]
            t_kc = None
            for wh in range(2):
                twl = E("D", lambda d, wh=wh: d.dma_start(
                    out=wckh_sb, in_=wscr["WckT"][:, 16384 * wh:16384 * (wh + 1)]),
                    waits=[tok_poolw_free if wh == 0 else t_kc], inc=True, dma=True)
                wck4 = wckh_sb.rearrange("p (hs a k) -> p hs a k", hs=16, a=8)
                for hs in range(16):
                    hsg = 16 * wh + hs
                    tkp = None
                    for a in range(8):
                        tkp = E("T", lambda t, hs=hs, a=a, w4=wck4: t.matmul(
                            PA[:, 0:512], w4[:, hs, a, :], MIX[:, 0, a, :],
                            start=(a == 0), stop=(a == 7)),
                            waits=([twl, t_mixB, tok_PA_free] +
                                   ([tok_sl_free[0], tok_sl_free[1], tok_sl_free[2],
                                     tok_sl_free[3], tok_sl_free[6], tok_sl_free[7]]
                                    if (wh == 0 and hs == 0) else []))
                            if a == 0 else (), inc=(a == 7))
                    tr1 = E("V", lambda v: v.tensor_single_scalar(
                        SLh(10, 0), PA[:, 0:512], 0.0, ALU.max),
                        waits=[tkp, t_kc], inc=True)
                    t_kc = E("G", lambda g, hsg=hsg: g.tensor_tensor(
                        KC[:, 512 * hsg:512 * (hsg + 1)], SLh(10, 0), SLh(10, 0), ALU.mult),
                        waits=[tr1], inc=True)
                    tok_PA_free = t_kc

            if dbg and n == 0:
                for _nm, _ap, _tok, _dt in [("xck", MIX[:, 0, :, :], t_mixB, BF16),
                                            ("xcr", MIX[:, 1, :, :], t_mixB, BF16),
                                            ("kc", KC, t_kc, BF16),
                                            ("ecr", ECR, tecr[3], BF16)]:
                    _dd = nc.dram_tensor("dbgw_" + _nm, list(_ap.shape), _dt,
                                         kind="ExternalOutput").ap()
                    E("G", lambda g, a=_ap, o=_dd: g.dma_start(out=o, in_=a),
                      waits=[_tok], inc=True, dma=True)
            wcv_sb = POOLW[:, 0:16384]
            kcv = KC.rearrange("p (g k) -> p g k", g=32)
            tep_last = None
            tat_prev = None
            tsv_prev = None
            t_sv = {}
            for dh in range(2):
                t_wcv = E("D", lambda d, dh=dh: d.dma_start(
                    out=wcv_sb, in_=wscr["WcvT"][:, 16384 * dh:16384 * (dh + 1)]),
                    waits=[K.tok("T")], inc=True, dma=True)
                wcv3 = wcv_sb.rearrange("p (a k) -> p a k", a=32)
                for tc in range(4):
                    PX = (PA if tc % 2 == 0 else PB)
                    col = 512 * (tc // 2)
                    tvc = None
                    for hs in range(32):
                        tvc = E("T", lambda t, hs=hs, tc=tc, PX=PX, col=col, w3=wcv3: t.matmul(
                            PX[:, col:col + 512], kcv[:, hs, 128 * tc:128 * (tc + 1)],
                            w3[:, hs, :], start=(hs == 0), stop=(hs == 31)),
                            waits=([t_wcv, t_kc, t_sv.get((dh - 1, tc)),
                                    t_sv.get((dh, tc - 2))] +
                                   ([tok_PB_free] if (dh == 0 and tc == 1) else []))
                            if hs == 0 else (),
                            inc=(hs == 31))
                    tep = E("G", lambda g, tc=tc, dh=dh: g.tensor_single_scalar(
                        SLh(6, 0), ECR[:, 1024 * tc + 512 * dh:1024 * tc + 512 * dh + 512],
                        1.0, ALU.add),
                        waits=[tecr[tc], tvc, tat_prev] +
                              ([tok_sl_free[6]] if (dh == 0 and tc == 0) else []),
                        inc=True)
                    tln2 = E("S", lambda s: s.activation(SLh(7, 0), SLh(6, 0), AF.Ln),
                             waits=[tep, tsv_prev] +
                                   ([tok_sl_free[7]] if (dh == 0 and tc == 0) else []),
                             inc=True)
                    trc2 = E("S", lambda s: s.activation(SLh(7, 1), SLh(7, 0), AF.Exp,
                                                         scale=-1.0),
                             waits=[tln2], inc=True)
                    tsv = E("V", lambda v, PX=PX, col=col: v.tensor_tensor(
                        SLh(6, 0), PX[:, col:col + 512], SLh(7, 1), ALU.mult),
                        waits=[trc2], inc=True)
                    tsv_prev = tsv
                    tat = E("G", lambda g, tc=tc, dh=dh: g.tensor_tensor(
                        H1[:, tc, 512 * dh:512 * dh + 512], H1[:, tc, 512 * dh:512 * dh + 512],
                        SLh(6, 0), ALU.add), waits=[tsv], inc=True)
                    tat_prev = tat
                    t_sv[(dh, tc)] = tsv
                    tep_last = tat
                    if dbg:
                        E("G", lambda g, n=n, tc=tc: g.dma_start(
                            out=dbg_r["attn"][n][tc], in_=H1[:, tc, :]),
                          waits=[tat], inc=True, dma=True)
            tok_PB_free = tep_last; tok_PA_free = tep_last
            t_attn = K.tok("V")
            tok_sl_free[4] = tep_last; tok_sl_free[5] = tep_last
            tok_sl_free[6] = tep_last; tok_sl_free[7] = tep_last
            tok_poolw_free = K.tok("T")

            if stop_stage == 'attn':
                stage_dump([t_attn, K.tok("S")])
                tok_x0_free = K.tok("G"); tok_h1_free = K.tok("G")
                tok_trf_free = t_trf
                continue

            prev_c = None
            for tc in range(4):
                ac = H1[:, tc, :]
                t_x = E("D", lambda d, n=n, tc=tc: d.dma_start(out=XC[:], in_=x_r2[n][tc]),
                        waits=[t_xc, prev_c], inc=True, dma=True)
                st, rstd, nmr = ln_stats(ac, 3, [t_attn, prev_c])
                tnm = E("S", lambda s, a=ac, r=rstd, nm=nmr: s.activation(
                    SL(10), a, AF.Identity, scale=r, bias=nm), waits=st, inc=True)
                tx2 = E("V", lambda v, tc=tc: v.tensor_tensor(H1[:, tc, :], XC[:], SL(10),
                                                              ALU.add),
                        waits=[tnm, t_x], inc=True)
                t_xc = tx2
                st2, rstd2, nmr2 = ln_stats(ac, 4, [tx2])
                th2 = E("S", lambda s, x=ac, r=rstd2, nm=nmr2: s.activation(
                    TMS[:], x, AF.Identity, scale=r, bias=nm), waits=st2, inc=True)
                te = transpose_to_fm(TMS[:], MIX[:, 0, :, 128 * tc:128 * (tc + 1)],
                                     [th2, K.tok("T")], tok_PT_free)
                tok_PT_free = te
                prev_c = tx2
                if dbg:
                    E("G", lambda g, n=n, tc=tc: g.dma_start(
                        out=dbg_r["x2"][n][tc], in_=H1[:, tc, :]),
                      waits=[tx2], inc=True, dma=True)
            t_h2f = tok_PT_free
            t_x2 = K.tok("V")

            if stop_stage == 'x2':
                stage_dump([t_x2, K.tok("S"), K.tok("T")])
                tok_x0_free = K.tok("G"); tok_h1_free = K.tok("G")
                tok_trf_free = t_trf
                continue

            u1_sb = POOLW[:, 0:2048]; v1_sb = POOLW[:, 2048:10240]
            u2_sb = POOLW[:, 10240:18432]; v2_sb = POOLW[:, 18432:20480]
            t_u1 = load_w(u1_sb, "U1T", [tok_poolw_free])
            t_v1 = load_w(v1_sb, "V1T", [])
            t_u2 = load_w(u2_sb, "U2T", [])
            t_v2 = load_w(v2_sb, "V2T", [])
            PBUF = PQB[:, :].rearrange("p (a k) -> p a k", a=2)
            QBUF = PBUF
            u13 = u1_sb.rearrange("p (a k) -> p a k", a=8)
            v13 = v1_sb.rearrange("p (a k) -> p a k", a=2)
            u23 = u2_sb.rearrange("p (a k) -> p a k", a=32)
            v23 = v2_sb.rearrange("p (a k) -> p a k", a=2)
            for rs in range(2):
                for a in range(8):
                    E("T", lambda t, rs=rs, a=a: t.matmul(
                        PA[:, 512 * rs:512 * (rs + 1)], u13[:, a, 128 * rs:128 * (rs + 1)],
                        MIX[:, 0, a, :], start=(a == 0), stop=(a == 7)),
                        waits=[t_u1, t_h2f, tok_PA_free] if (rs == 0 and a == 0) else (),
                        inc=(rs == 1 and a == 7))
            tpb = E("S", lambda s: s.copy(PQB[:, :], PA[:]),
                    waits=[K.tok("T")], inc=True)
            tok_PA_free = tpb
            if stop_stage == 'ffp':
                stage_dump([tpb, K.tok("S"), K.tok("T")])
                tok_x0_free = K.tok("G"); tok_h1_free = K.tok("G")
                tok_trf_free = t_trf; tok_nu_guard = K.tok("T")
                continue
            HB = KC
            t_hh = None
            tsg_p = None
            tr2 = None
            for fs in range(32):
                th = None
                for rs in range(2):
                    th = E("T", lambda t, fs=fs, rs=rs: t.matmul(
                        PB[:, 0:512], v13[:, rs, 128 * fs:128 * (fs + 1)], PBUF[:, rs, :],
                        start=(rs == 0), stop=(rs == 1)),
                        waits=([t_v1, tpb, tok_PB_free] if fs == 0 else [tok_PB_free])
                        if rs == 0 else (),
                        inc=(rs == 1))
                tug = E("V", lambda v, fs=fs: v.tensor_scalar(
                    SLh(10, 0), PB[:, 0:512], FB1[:, fs:fs + 1], None, ALU.add, ALU.bypass),
                    waits=[th], inc=True)
                tok_PB_free = tug
                E("V", lambda v: v.tensor_tensor(SLh(10, 1), SLh(10, 0), SLh(10, 0), ALU.mult))
                E("V", lambda v: v.tensor_scalar(SLh(10, 1), SLh(10, 1), GC1, GC2,
                                                 ALU.mult, ALU.add))
                t3 = E("V", lambda v: v.tensor_tensor(SLh(10, 1), SLh(10, 1), SLh(10, 0),
                                                      ALU.mult), inc=True)
                tsg = E("S", lambda s: s.activation(SLh(9, 0), SLh(10, 1), AF.Exp,
                                                    scale=-2.0),
                        waits=[t3] + ([tok_sl_free[9]] if fs == 0 else []), inc=True)
                te1 = E("V", lambda v: v.tensor_single_scalar(SLh(9, 0), SLh(9, 0), 1.0,
                                                              ALU.add), waits=[tsg], inc=True)
                tln9 = E("S", lambda s: s.activation(SLh(9, 1), SLh(9, 0), AF.Ln),
                         waits=[te1], inc=True)
                tr2 = E("S", lambda s: s.activation(SLh(9, 0), SLh(9, 1), AF.Exp,
                                                    scale=-1.0), waits=[tln9], inc=True)
                t_hh = E("V", lambda v, fs=fs: v.tensor_tensor(
                    HB[:, 512 * fs:512 * (fs + 1)], SLh(10, 0), SLh(9, 0), ALU.mult),
                    waits=([t_kc] if fs == 0 else []) + [tr2], inc=True)
            if stop_stage == 'ffh':
                stage_dump([t_hh, K.tok("S"), K.tok("T")])
                tok_x0_free = K.tok("G"); tok_h1_free = K.tok("G")
                tok_trf_free = t_trf; tok_nu_guard = K.tok("T")
                continue
            for rs in range(2):
                for fs in range(32):
                    E("T", lambda t, rs=rs, fs=fs: t.matmul(
                        PA[:, 512 * rs:512 * (rs + 1)], u23[:, fs, 128 * rs:128 * (rs + 1)],
                        HB[:, 512 * fs:512 * (fs + 1)], start=(fs == 0), stop=(fs == 31)),
                        waits=[t_u2, t_hh, tok_PA_free] if (rs == 0 and fs == 0) else (),
                        inc=(rs == 1 and fs == 31))
            tqb = E("S", lambda s: s.copy(PQB[:, :], PA[:]),
                    waits=[K.tok("T")], inc=True)
            tok_PA_free = tqb
            if stop_stage == 'ffq':
                stage_dump([tqb, K.tok("S"), K.tok("T")])
                tok_x0_free = K.tok("G"); tok_h1_free = K.tok("G")
                tok_trf_free = t_trf; tok_nu_guard = K.tok("T")
                continue
            t_fb2 = E("D", lambda d: d.dma_start(out=r_fb2, in_=fb2_d),
                      waits=[tok_mix_free, tok_state], inc=True, dma=True)
            prev_e = None
            for tc in range(4):
                tf = None
                for h in range(2):
                    for rs in range(2):
                        E("T", lambda t, h=h, rs=rs, tc=tc: t.matmul(
                            PB[:, 512 * h:512 * (h + 1)],
                            QBUF[:, rs, 128 * tc:128 * (tc + 1)],
                            v23[:, rs, 512 * h:512 * (h + 1)], start=(rs == 0), stop=False),
                            waits=[t_v2, tqb, tok_PB_free, t_fb2]
                            if (h == 0 and rs == 0) else ())
                    tf = E("T", lambda t, h=h: t.matmul(
                        PB[:, 512 * h:512 * (h + 1)], ONES1[:],
                        r_fb2[:, 512 * h:512 * (h + 1)], start=False, stop=True),
                        inc=(h == 1))
                if stop_stage == 'ffe1':
                    tok_PB_free = E("V", lambda v: v.tensor_copy(SL(10), PB[:]),
                                    waits=[tf], inc=True)
                    prev_e = tok_PB_free
                    continue
                st, rstd, nmr = ln_stats(PB[:], 5, [tf, prev_e], psum_src=True)
                if stop_stage == 'ffe2':
                    tok_PB_free = st[1]
                    prev_e = st[1]
                    continue
                tnm = E("S", lambda s, r=rstd, nm=nmr: s.activation(
                    SL(10), PB[:], AF.Identity, scale=r, bias=nm), waits=st, inc=True)
                tok_PB_free = tnm
                if stop_stage == 'ffe3':
                    prev_e = tnm
                    continue
                tout = E("G", lambda g, tc=tc: g.tensor_tensor(
                    H1[:, tc, :], H1[:, tc, :], SL(10), ALU.add),
                    waits=[tnm], inc=True)
                prev_e = tout
                E("G", lambda g, n=n, tc=tc: g.dma_start(out=out_r[n][tc], in_=H1[:, tc, :]),
                  inc=True, dma=True)
            if stop_stage in ('ffe1', 'ffe2', 'ffe3'):
                stage_dump([K.tok("S"), K.tok("V"), K.tok("T")])
            tok_x0_free = K.tok("G")
            tok_h1_free = K.tok("G")
            tok_nu_guard = K.tok("T")
            for i in range(10):
                tok_sl_free[i] = tqb
            tok_poolw_free = K.tok("T")
            t_cast_done = None

        E("G", lambda g: g.wait_ge(sem["G"], K.cnt["G"]))
        K.check()

        waited = {e: {} for e in Sched.ENGS}

        def replay(e, eng):
            for waits, fn, inc, amt, sm in K.q[e]:
                for we, wv in waits:
                    if waited[e].get(we, 0) < wv:
                        eng.wait_ge(sem[we], wv)
                        waited[e][we] = wv
                inst = fn(eng)
                if inc:
                    inst.then_inc(sem[sm], amt)

        @blk.tensor
        def _(tensor):
            replay("T", tensor)

        @blk.vector
        def _(vector):
            replay("V", vector)

        @blk.scalar
        def _(scalar):
            replay("S", scalar)

        @blk.sync
        def _(sync):
            replay("D", sync)

        @blk.gpsimd
        def _(gpsimd):
            replay("G", gpsimd)

    return nc


def make_inputs(inp, b, T):
    p = {k[8:]: np.asarray(v, np.float32) for k, v in inp.items()
         if k.startswith("L0_attn_") and k != "L0_attn_x"}
    f = {k[7:]: np.asarray(v, np.float32) for k, v in inp.items() if k.startswith("L0_ffn_")}
    x = np.asarray(inp["L0_attn_x"][b], np.float32)[:T]
    wck = np.ascontiguousarray(
        p["Wck"].T.reshape(8, 128, 32, 128).transpose(1, 2, 0, 3).reshape(128, 32768))
    d = {
        "x": np.ascontiguousarray(x),
        "WkT": _lhsT(p["Wk"].T, 8).reshape(128, -1),
        "WvT": _lhsT(p["Wv"].T, 8).reshape(128, -1),
        "WrT": _lhsT(p["Wr"].T, 8).reshape(128, -1),
        "WcrT": _lhsT(p["Wcr"].T, 8).reshape(128, -1),
        "WckT": wck,
        "WcvT": np.ascontiguousarray(
            p["Wcv"].T.astype(np.float32).reshape(32, 128, 2, 512)
            .transpose(1, 2, 0, 3).reshape(128, 32768)),
        "U1T": _lhsT(f["U1"], 8).reshape(128, -1),
        "V1T": _lhsT(f["V1"], 2).reshape(128, -1),
        "U2T": _lhsT(f["U2"], 32).reshape(128, -1),
        "V2T": _lhsT(f["V2"], 2).reshape(128, -1),
        "fmrows": np.ascontiguousarray(np.stack(
            [_fm(p["rk"]), _fm(p["rv"]), _fm(p["rr"]), _fm(p["crk"]), _fm(p["crr"]),
             _fm(p["ln1_g"]), _fm(p["ln1_b"]), _fm(p["ln2_g"]), _fm(p["ln2_b"])], axis=2)),
        "fb1fm": np.ascontiguousarray(np.asarray(f["b1"], np.float32).reshape(32, 128).T),
        "wrow": np.asarray(p["w"], np.float32).reshape(1, -1).copy(),
        "urow": np.asarray(p["u"], np.float32).reshape(1, -1).copy(),
        "fb2row": np.asarray(f["b2"], np.float32).reshape(1, -1).copy(),
        "Rf32": np.triu(np.ones((128, 128), np.float32)),
        "Rbf": np.triu(np.ones((128, 128))).astype(BF),
        "identbf": np.eye(128).astype(BF),
        "iotacol": np.arange(128, dtype=np.float32).reshape(-1, 1),
        "ones1": np.ones((1, 128), np.float32),
        "onescb": np.ones((128, 1)).astype(BF),
        "onescf": np.ones((128, 1), np.float32),
    }
    return d


def kernel(**inputs):
    T = 8192
    nc = build_program(T)
    in_maps = [make_inputs(inputs, b, T) for b in range(8)]
    res = run_bass_kernel_spmd(nc, in_maps, core_ids=list(range(8)))
    return np.stack([res.results[b]["out"] for b in range(8)], axis=0)


# version 09
# speedup vs baseline: 1.0977x; 1.0293x over previous
import numpy as np
import ml_dtypes
import concourse.bass as bass
import concourse.mybir as mybir
from concourse.bass_utils import run_bass_kernel_spmd
from contextlib import ExitStack

F32 = mybir.dt.float32
BF16 = mybir.dt.bfloat16
AF = mybir.ActivationFunctionType
ALU = mybir.AluOpType
BF = ml_dtypes.bfloat16

C = 1024
L = 128
TT_TILE = 512
NCHUNK = TT_TILE // L
PC_CLAMP = -15.0
GC1 = 0.044715 * 0.7978845608028654
GC2 = 0.7978845608028654


def _lhsT(Wcd, ng):
    return np.ascontiguousarray(
        np.asarray(Wcd, np.float32).reshape(ng, 128, -1).transpose(1, 0, 2))


def _fm(v):
    return np.ascontiguousarray(np.asarray(v, np.float32).reshape(8, 128).T)


class Sched:
    ENGS = "TVSDG"
    SEMS = "TVSDGP"

    def __init__(self, serialize=False):
        self.q = {e: [] for e in self.ENGS}
        self.cnt = {e: 0 for e in self.SEMS}
        self.serialize = serialize
        self.last = None

    def emit(self, e, fn, waits=(), inc=False, dma=False):
        w = [t for t in waits if t is not None]
        if self.serialize:
            inc = True
            if self.last is not None:
                w = w + [self.last]
        sem = e
        if e == "G" and not dma:
            sem = "P"
        amt = 16 if dma else 1
        self.q[e].append((w, fn, inc, amt, sem))
        if inc:
            self.cnt[sem] += amt
            tok = (sem, self.cnt[sem])
            if self.serialize:
                self.last = tok
            return tok
        return None

    def tok(self, e):
        return (e, self.cnt[e])

    def check(self):
        ptr = {e: 0 for e in self.ENGS}
        done = {s: 0 for s in self.SEMS}
        progress = True
        while progress:
            progress = False
            for e in self.ENGS:
                while ptr[e] < len(self.q[e]):
                    waits, _, inc, amt, sem = self.q[e][ptr[e]]
                    if any(done[we] < wv for we, wv in waits):
                        break
                    if inc:
                        done[sem] += amt
                    ptr[e] += 1
                    progress = True
        stuck = {e: (ptr[e], len(self.q[e])) for e in self.ENGS if ptr[e] < len(self.q[e])}
        if stuck:
            msgs = []
            for e in stuck:
                waits, _, _, _, _ = self.q[e][ptr[e]]
                unmet = [(we, wv, done[we]) for we, wv in waits if done[we] < wv]
                msgs.append(f"{e}@{ptr[e]}/{len(self.q[e])} needs {unmet}")
            raise RuntimeError("deadlock: " + "; ".join(msgs))


def build_program(T, dbg=False, serialize=False, stop_stage='out'):
    nt = T // TT_TILE
    nc = bass.Bass("TRN2", target_bir_lowering=False, detect_race_conditions=False)

    def din(name, shape, dtype=mybir.dt.float32):
        return nc.dram_tensor(name, list(shape), dtype, kind="ExternalInput").ap()

    x_d = din("x", [T, C])
    out_d = nc.dram_tensor("out", [T, C], mybir.dt.float32, kind="ExternalOutput").ap()
    wk_d = din("WkT", [128, 8192]); wv_d = din("WvT", [128, 8192])
    wr_d = din("WrT", [128, 8192]); wcr_d = din("WcrT", [128, 8192])
    wck_d = din("WckT", [128, 32768]); wcv_d = din("WcvT", [128, 32768])
    u1_d = din("U1T", [128, 2048]); v1_d = din("V1T", [128, 8192])
    u2_d = din("U2T", [128, 8192]); v2_d = din("V2T", [128, 2048])
    fmr_d = din("fmrows", [128, 8, 9])
    fb1_d = din("fb1fm", [128, 32])
    wrow_d = din("wrow", [1, 1024]); urow_d = din("urow", [1, 1024])
    fb2_d = din("fb2row", [1, 1024])
    Rf_d = din("Rf32", [128, 128]); Rb_d = din("Rbf", [128, 128], BF16)
    id_d = din("identbf", [128, 128], BF16)
    iota_d = din("iotacol", [128, 1]); ones1_d = din("ones1", [1, 128])
    onescb_d = din("onescb", [128, 1], BF16); onescf_d = din("onescf", [128, 1])
    wscr = {}
    for nm, w in [("WkT", 8192), ("WvT", 8192), ("WrT", 8192), ("WcrT", 8192),
                  ("WckT", 32768), ("WcvT", 32768), ("U1T", 2048), ("V1T", 8192),
                  ("U2T", 8192), ("V2T", 2048)]:
        wscr[nm] = nc.dram_tensor("s_" + nm, [128, w], BF16).ap()
    wraw = {"WkT": wk_d, "WvT": wv_d, "WrT": wr_d, "WcrT": wcr_d, "WckT": wck_d,
            "WcvT": wcv_d, "U1T": u1_d, "V1T": v1_d, "U2T": u2_d, "V2T": v2_d}
    dbg_d = {}
    if dbg:
        for nm in ["h1", "x1", "attn", "x2"]:
            dbg_d[nm] = nc.dram_tensor("dbg_" + nm, [T, C], mybir.dt.float32,
                                       kind="ExternalOutput").ap()

    es = ExitStack()
    with es:
        sb = lambda n, s, d=F32: es.enter_context(nc.sbuf_tensor(n, list(s), d))
        pst = lambda n, s, d=F32: es.enter_context(nc.psum_tensor(n, list(s), d))
        POOLW = sb("poolw", [128, 24576], BF16)
        SA = sb("sarena", [128, 11, 1024])
        SAb = SA.bitcast(BF16)
        GB = sb("gbpair", [128, 2048], BF16)
        MIX = sb("mix", [128, 3, 8, 512], BF16)
        TRF = sb("trfm", [128, 8, 513], BF16)
        TMS = sb("tmstage", [128, 1024], BF16)
        XC = sb("xc", [128, 1024])
        H1 = sb("h1t", [128, 4, 1024])
        KCA = sb("kcarena", [128, 16384], BF16)
        PQB = sb("pqbuf", [128, 1024], BF16)
        ROWA = sb("rowa", [128, 1024]); ROWB = sb("rowb", [128, 1024])
        ROWC = sb("rowc", [128, 1024])
        STAT = sb("stat", [128, 64])
        IOTW = sb("iotw", [128, 1024]); EU = sb("eu", [128, 1024])
        RF = sb("rf", [128, 128]); RB = sb("rb", [128, 128], BF16)
        IDB = sb("idb", [128, 128], BF16)
        ONES1 = sb("onesone", [1, 128]); IOTAC = sb("iotac", [128, 1])
        ONESCB = sb("onescbs", [128, 1], BF16); ONESCF = sb("onescfs", [128, 1])
        MIXCO = sb("mixco", [128, 8, 17])
        FB1 = sb("fb1", [128, 32])
        CAR1 = sb("car1", [128, 8, 1]); CAR2 = sb("car2", [128, 8, 1])
        CONSTS = sb("consts", [128, 3])
        nc.const_aps.register(F32, 1e-5, CONSTS[:, 0:1])
        nc.const_aps.register(F32, 1e-38, CONSTS[:, 1:2])
        nc.const_aps.register(F32, 1.0, CONSTS[:, 2:3])
        PA = pst("PA", [128, 1024]); PB = pst("PB", [128, 1024])
        PT = pst("PT", [128, 1024], BF16)
        PTOT = pst("PTOT", [1, 1024])
        sem = {e: es.enter_context(nc.semaphore("sem" + e)) for e in Sched.SEMS}
        blk = es.enter_context(nc.Block())

        K = Sched(serialize=serialize)
        E = K.emit
        FMR = SA[:, 10, 0:72].rearrange("p (a k) -> p a k", a=8)
        r_phc = ROWA[0:1, :]; r_phat = ROWA[32:33, :]; r_s0 = ROWA[64:65, :]
        r_mu = ROWB[0:1, :]; r_nhat = ROWB[32:33, :]; r_s1 = ROWB[64:65, :]
        r_nu = ROWC[0:1, :]; r_Lw = ROWC[32:33, :]; r_s2 = ROWC[64:65, :]
        r_wu = ROWA[0:1, :]; r_fb2 = ROWC[0:1, :]; r_s3 = ROWB[64:65, :]
        r_phat32 = ROWA[32:33, :]; r_lnt2 = ROWB[32:33, :]
        SL = lambda i: SA[:, i, :]
        SLh = lambda i, h: SA[:, i, 512 * h:512 * (h + 1)]

        ldt = []
        for dst, src in [(RF[:], Rf_d), (RB[:], Rb_d), (IDB[:], id_d),
                         (ONES1[:], ones1_d), (IOTAC[:], iota_d),
                         (ONESCB[:], onescb_d), (ONESCF[:], onescf_d),
                         (FMR, fmr_d), (FB1[:], fb1_d),
                         (r_wu, wrow_d), (r_s3, urow_d)]:
            ldt.append(E("D", lambda d, o=dst, s=src: d.dma_start(out=o, in_=s),
                         inc=True, dma=True))
        tld = ldt[-1]
        t_lw = E("S", lambda s: s.activation(r_Lw, r_wu, AF.Identity, scale=float(L)),
                 waits=[tld], inc=True)
        t_eur = E("S", lambda s: s.activation(r_mu, r_s3, AF.Exp), inc=True)
        E("T", lambda t: t.matmul(PA[:, 0:512], ONES1[:], r_mu[:, 0:512],
                                  start=True, stop=True), waits=[t_eur])
        t_mm1 = E("T", lambda t: t.matmul(PA[:, 512:1024], ONES1[:], r_mu[:, 512:1024],
                                          start=True, stop=True), inc=True)
        E("T", lambda t: t.matmul(PB[:, 0:512], ONES1[:], r_wu[:, 0:512],
                                  start=True, stop=True), waits=[tld])
        t_mm2 = E("T", lambda t: t.matmul(PB[:, 512:1024], ONES1[:], r_wu[:, 512:1024],
                                          start=True, stop=True), inc=True)
        t_eu = E("V", lambda v: v.tensor_copy(EU[:], PA[:]), waits=[t_mm1], inc=True)
        t_iotw = E("V", lambda v: v.tensor_scalar(IOTW[:], PB[:], IOTAC[:, 0:1], None,
                                                  ALU.mult, ALU.bypass),
                   waits=[t_mm2], inc=True)
        fv = lambda i: FMR[:, :, i]
        mco = lambda i: MIXCO[:, :, i]
        tpm = None
        for dsti, ai, bi in [(0, 0, 5), (2, 1, 5), (4, 2, 5), (7, 3, 7), (9, 4, 7)]:
            tpm = E("V", lambda v, d=dsti, a=ai, b=bi: v.tensor_tensor(mco(d), fv(a), fv(b),
                                                                       ALU.mult),
                    waits=[tld], inc=True)
        for dsti, gi, pi in [(1, 5, 0), (3, 5, 2), (5, 5, 4), (8, 7, 7), (10, 7, 9)]:
            tpm = E("V", lambda v, d=dsti, g=gi, p=pi: v.tensor_tensor(mco(d), fv(g), mco(p),
                                                                       ALU.subtract),
                    waits=[tpm], inc=True)
        E("V", lambda v: v.tensor_copy(mco(6), fv(6)))
        E("V", lambda v: v.tensor_copy(mco(11), fv(8)))
        for dsti, ri, bi in [(12, 0, 6), (13, 1, 6), (14, 2, 6), (15, 3, 8), (16, 4, 8)]:
            tpm = E("V", lambda v, d=dsti, r=ri, b=bi: v.tensor_tensor(mco(d), fv(r), fv(b),
                                                                       ALU.mult),
                    waits=[tpm], inc=True)
        E("V", lambda v: v.memset(CONSTS[:, 0:1], 1e-5))
        E("V", lambda v: v.memset(CONSTS[:, 1:2], 1e-38))
        E("V", lambda v: v.memset(CONSTS[:, 2:3], 1.0))
        E("V", lambda v: v.memset(r_phat, -10000.0))
        E("V", lambda v: v.memset(r_nhat, 0.0))
        E("V", lambda v: v.memset(CAR1[:], 0.0))
        t_init = E("V", lambda v: v.memset(CAR2[:], 0.0), inc=True)

        jobs = []
        for nm in ["WkT", "WvT", "WrT", "WcrT", "WckT", "WcvT", "U1T", "V1T", "U2T", "V2T"]:
            tot = wraw[nm].shape[1]
            for c0 in range(0, tot, 2048):
                jobs.append((wraw[nm][:, c0:c0 + 2048], wscr[nm][:, c0:c0 + 2048]))
        prev_out = [None, None]
        t_cast_done = None
        for i, (srcap, dstap) in enumerate(jobs):
            b = i % 2
            stg = SA[:, 2 * b:2 * b + 2, :].rearrange("p a k -> p (a k)")
            tin = E("D", lambda d, s=srcap, g=stg: d.dma_start(out=g, in_=s),
                    waits=[prev_out[b]], inc=True, dma=True)
            tout = E("G", lambda g, s=stg, dd=dstap: g.dma_start(out=dd, in_=s),
                     waits=[tin], inc=True, dma=True)
            prev_out[b] = tout
            t_cast_done = tout

        last_ln = [t_init]

        def ln_stats(src_ap, idx, tok_in, psum_src=False):
            sm = STAT[:, 4 * idx:4 * idx + 1]
            sq = STAT[:, 4 * idx + 2:4 * idx + 3]
            tsq = E("S", lambda s, a=src_ap: s.activation(SL(10), a, AF.Square),
                    waits=list(tok_in) + [last_ln[0]], inc=True)
            t1 = E("V", lambda v, a=src_ap, o=sm: v.reduce_sum(o, a, axis=mybir.AxisListType.X),
                   waits=list(tok_in) + [last_ln[0]] + ([tsq] if psum_src else []), inc=True)
            t2 = E("V", lambda v, o=sq: v.reduce_sum(o, SL(10), axis=mybir.AxisListType.X),
                   waits=[tsq], inc=True)
            m = STAT[:, 32:33]; v2 = STAT[:, 36:37]; rstd = STAT[:, 40:41]
            nmr = STAT[:, 44:45]; tmp = STAT[:, 48:49]; tmp2 = STAT[:, 52:53]
            ta = E("V", lambda v: v.tensor_single_scalar(m, sm, 1.0 / 1024.0, ALU.mult),
                   waits=[t1, t2], inc=True)
            tb = E("V", lambda v: v.tensor_single_scalar(v2, sq, 1.0 / 1024.0, ALU.mult),
                   waits=[ta], inc=True)
            tc2 = E("V", lambda v: v.tensor_tensor(tmp, m, m, ALU.mult), waits=[tb], inc=True)
            td2 = E("V", lambda v: v.tensor_tensor(v2, v2, tmp, ALU.subtract),
                    waits=[tc2], inc=True)
            tv = E("V", lambda v: v.tensor_single_scalar(v2, v2, 1e-5, ALU.add),
                   waits=[td2], inc=True)
            E("S", lambda s: s.activation(tmp2, v2, AF.Ln), waits=[tv])
            tr = E("S", lambda s: s.activation(rstd, tmp2, AF.Exp, scale=-0.5), inc=True)
            te2 = E("V", lambda v: v.tensor_tensor(tmp, m, rstd, ALU.mult), waits=[tr], inc=True)
            tn = E("V", lambda v: v.tensor_single_scalar(nmr, tmp, -1.0, ALU.mult),
                   waits=[te2], inc=True)
            last_ln[0] = tn
            return [tr, tn], rstd, nmr

        def transpose_to_fm(tm_bf_ap, dst_fm_ap, waits, pt_free):
            tt = None
            for a in range(8):
                tt = E("T", lambda t, a=a, s=tm_bf_ap: t.transpose(
                    PT[:, 128 * a:128 * (a + 1)], s[:, 128 * a:128 * (a + 1)], IDB[:]),
                    waits=(list(waits) + [pt_free]) if a == 0 else (), inc=(a == 7))
            return E("S", lambda s, d=dst_fm_ap: s.copy(
                d, PT[:, :].rearrange("p (a t) -> p a t", a=8)), waits=[tt], inc=True)

        tok_PA_free = t_eu; tok_PB_free = t_iotw
        tok_PT_free = K.tok("V")
        tok_PTOT_free = t_init
        tok_x0_free = None
        tok_h1_free = None
        tok_nu_guard = None
        tok_prol_guard = t_eur
        tok_mix_free = None
        tok_state = t_init
        tok_trf_free = None
        tok_gb_free = None
        tok_gb_free2 = None
        tok_sl_free = {i: None for i in range(11)}
        tok_poolw_free = None

        x_r2 = x_d.rearrange("(n tc p) c -> n tc p c", p=128, tc=4)
        out_r = out_d.rearrange("(n tc p) c -> n tc p c", p=128, tc=4)
        dbg_r = {k2: v2.rearrange("(n tc p) c -> n tc p c", p=128, tc=4)
                 for k2, v2 in dbg_d.items()}

        def load_w(dst2d, nm, waits):
            return E("D", lambda d, o=dst2d, s=wscr[nm]: d.dma_start(out=o, in_=s),
                     waits=waits, inc=True, dma=True)

        for n in range(nt):
            wk_sb = POOLW[:, 0:8192]; wv_sb = POOLW[:, 8192:16384]; wr_sb = POOLW[:, 16384:24576]
            t_wk = load_w(wk_sb, "WkT", [tok_poolw_free, t_cast_done])
            t_wv = load_w(wv_sb, "WvT", [])
            t_wr = load_w(wr_sb, "WrT", [])
            wk3 = wk_sb.rearrange("p (a k) -> p a k", a=8)
            wv3 = wv_sb.rearrange("p (a k) -> p a k", a=8)
            wr3 = wr_sb.rearrange("p (a k) -> p a k", a=8)

            tcar = E("V", lambda v: v.tensor_copy(TRF[:, :, 0:1], CAR1[:]),
                     waits=[tok_trf_free, t_init], inc=True)
            t_h1 = []
            t_xc = None
            for tc in range(4):
                x0c = XC[:, :]; h1c = H1[:, tc, :]
                t_x = E("D", lambda d, n=n, tc=tc: d.dma_start(out=XC[:], in_=x_r2[n][tc]),
                        waits=[t_xc, tok_x0_free] + ([t_cast_done] if tc == 0 else []),
                        inc=True, dma=True)
                st, rstd, nmr = ln_stats(x0c, 0, [t_x])
                th = E("S", lambda s, a=x0c, o=h1c, r=rstd, nm=nmr: s.activation(
                    o, a, AF.Identity, scale=r, bias=nm), waits=st + [tok_h1_free], inc=True)
                t_xc = th
                if stop_stage == 'h1x':
                    t_h1.append(th)
                    t_xc = th
                    continue
                st2, rstd2, nmr2 = ln_stats(h1c, 1, [th])
                tx = E("S", lambda s, a=h1c, r=rstd2, nm=nmr2: s.activation(
                    TMS[:], a, AF.Identity, scale=r, bias=nm), waits=st2, inc=True)
                te = transpose_to_fm(TMS[:], TRF[:, :, 1 + 128 * tc:1 + 128 * (tc + 1)],
                                     [tx, tcar], tok_PT_free)
                tok_PT_free = te
                t_h1.append(th)
                if dbg:
                    E("G", lambda g, n=n, tc=tc, a=h1c: g.dma_start(
                        out=dbg_r["h1"][n][tc], in_=a), waits=[th], inc=True, dma=True)
            t_trf = tok_PT_free

            def stage_dump(waits):
                tt = None
                for tc2 in range(4):
                    tt = E("G", lambda g, n=n, tc2=tc2: g.dma_start(out=out_r[n][tc2],
                                                                   in_=H1[:, tc2, :]),
                           waits=list(waits), inc=True, dma=True)
                return tt

            if stop_stage in ('h1', 'h1x'):
                stage_dump([K.tok("S"), K.tok("V")])
                tok_x0_free = K.tok("G"); tok_h1_free = K.tok("G")
                tok_poolw_free = K.tok("T"); tok_mix_free = K.tok("T")
                tok_trf_free = t_trf
                continue

            tmix = None
            for mi, (cg, cg1) in enumerate([(0, 1), (2, 3), (4, 5)]):
                for a in range(8):
                    cur = TRF[:, a, 1:513]; prv = TRF[:, a, 0:512]
                    o = MIX[:, mi, a, :]
                    E("V", lambda v, c=cur, g=cg, a=a: v.tensor_scalar(
                        SLh(10, 0), c, MIXCO[:, a, g:g + 1], MIXCO[:, a, 6:7],
                        ALU.mult, ALU.add),
                      waits=[t_trf, tok_mix_free] if (mi == 0 and a == 0) else ())
                    E("V", lambda v, p=prv, g=cg1, a=a: v.tensor_scalar(
                        SLh(10, 1), p, MIXCO[:, a, g:g + 1], None, ALU.mult, ALU.bypass))
                    tmix = E("V", lambda v, o=o: v.tensor_tensor(o, SLh(10, 0), SLh(10, 1),
                                                                 ALU.add), inc=True)
                    if n == 0:
                        tmix = E("V", lambda v, o=o, c=cur, g=cg, a=a, mi=mi: v.tensor_scalar(
                            o[:, 0:1], c[:, 0:1], MIXCO[:, a, g:g + 1],
                            MIXCO[:, a, 12 + mi:13 + mi], ALU.mult, ALU.add), inc=True)
            tcarS = E("V", lambda v: v.tensor_copy(CAR1[:], TRF[:, :, 512:513]), inc=True)
            t_mixA = tmix

            for tc in range(NCHUNK):
                def ddump(name, ap, tok, dt2=F32, _n=n, _tc=tc):
                    if not (dbg and _n == 0 and _tc == 0):
                        return None
                    dd = nc.dram_tensor("dbgw_" + name, list(ap.shape), dt2,
                                        kind="ExternalOutput").ap()
                    return E("G", lambda g, a=ap, o=dd: g.dma_start(out=o, in_=a),
                             waits=[tok], inc=True, dma=True)
                ts1 = E("V", lambda v: v.tensor_single_scalar(r_phc, r_phat, PC_CLAMP, ALU.max),
                        waits=[tok_state, tok_prol_guard], inc=True)
                tok_prol_guard = None
                ts2 = E("V", lambda v: v.tensor_scalar(r_s0, r_phat, -PC_CLAMP, 0.0,
                                                       ALU.add, ALU.min), waits=[ts1], inc=True)
                ts3a = E("S", lambda s: s.activation(r_mu, r_s0, AF.Exp), waits=[ts2], inc=True)
                ts3 = E("V", lambda v: v.tensor_single_scalar(r_mu, r_mu, 1e-38, ALU.add),
                       waits=[ts3a], inc=True)
                ts4a = E("V", lambda v: v.tensor_copy(r_nu, r_nhat),
                         waits=[tok_nu_guard, ts3], inc=True)
                ts4 = E("V", lambda v: v.tensor_tensor(r_nu, r_mu, r_nu, ALU.mult),
                        waits=[ts4a], inc=True)
                tok_nu_guard = None
                tbc = None
                for h in range(2):
                    tbc = E("T", lambda t, h=h: t.matmul(
                        PA[:, 512 * h:512 * (h + 1)], ONES1[:],
                        r_phc[:, 512 * h:512 * (h + 1)], start=True, stop=True),
                        waits=[ts2, tok_PA_free] if h == 0 else (), inc=(h == 1))
                tk = None
                for h in range(2):
                    for a in range(8):
                        tk = E("T", lambda t, h=h, a=a, tc=tc: t.matmul(
                            PB[:, 512 * h:512 * (h + 1)],
                            MIX[:, 0, a, 128 * tc:128 * (tc + 1)],
                            wk3[:, a, 512 * h:512 * (h + 1)],
                            start=(a == 0), stop=(a == 7)),
                            waits=[t_mixA, t_wk, tok_PB_free] if (h == 0 and a == 0) else (),
                            inc=(h == 1 and a == 7))
                tv1 = E("V", lambda v: v.tensor_tensor(SL(0), PB[:], IOTW[:], ALU.subtract),
                        waits=[tk, tok_sl_free[0]], inc=True)
                tz = E("V", lambda v: v.tensor_tensor(SL(1), SL(0), PA[:], ALU.subtract),
                       waits=[tbc, tok_sl_free[1]], inc=True)
                tok_PB_free = tz; tok_PA_free = tz
                tvp = None
                for h in range(2):
                    for a in range(8):
                        tvp = E("T", lambda t, h=h, a=a, tc=tc: t.matmul(
                            PA[:, 512 * h:512 * (h + 1)],
                            MIX[:, 1, a, 128 * tc:128 * (tc + 1)],
                            wv3[:, a, 512 * h:512 * (h + 1)],
                            start=(a == 0), stop=(a == 7)),
                            waits=[tz, t_wv] if (h == 0 and a == 0) else (),
                            inc=(h == 1 and a == 7))
                tg = E("S", lambda s: s.activation(SL(2), SL(1), AF.Exp),
                       waits=[tz, tok_sl_free[2]], inc=True)
                tgb = E("S", lambda s: s.copy(GB[:, 0:1024], SL(2)),
                        waits=[tg, tok_gb_free, tok_gb_free2], inc=True)
                tvf = E("S", lambda s: s.copy(SL(3), PA[:]), waits=[tvp, tok_sl_free[3]],
                        inc=True)
                tok_PA_free = tvf
                trp = None
                for h in range(2):
                    for a in range(8):
                        trp = E("T", lambda t, h=h, a=a, tc=tc: t.matmul(
                            PA[:, 512 * h:512 * (h + 1)],
                            MIX[:, 2, a, 128 * tc:128 * (tc + 1)],
                            wr3[:, a, 512 * h:512 * (h + 1)],
                            start=(a == 0), stop=(a == 7)),
                            waits=[tvf, t_wr] if (h == 0 and a == 0) else (),
                            inc=(h == 1 and a == 7))
                tF = None
                for h in range(2):
                    E("T", lambda t, h=h: t.matmul(
                        PB[:, 512 * h:512 * (h + 1)], RB[:], GB[:, 512 * h:512 * (h + 1)],
                        start=True, stop=False), waits=[tgb] if h == 0 else ())
                    tF = E("T", lambda t, h=h: t.matmul(
                        PB[:, 512 * h:512 * (h + 1)], ONES1[:],
                        r_mu[:, 512 * h:512 * (h + 1)], start=False, stop=True),
                        waits=[ts3] if h == 0 else (), inc=(h == 1))
                tFt = None
                for h in range(2):
                    tFt = E("T", lambda t, h=h: t.matmul(
                        PTOT[0:1, 512 * h:512 * (h + 1)], ONESCB[:],
                        GB[:, 512 * h:512 * (h + 1)], start=True, stop=True),
                        waits=[tok_PTOT_free] if h == 0 else (), inc=(h == 1))
                ter = E("S", lambda s: s.activation(SL(4), PA[:], AF.Exp, scale=-1.0),
                        waits=[trp, tok_sl_free[4]], inc=True)
                tok_PA_free = ter
                tFE = E("V", lambda v: v.tensor_tensor(SL(5), PB[:], GB[:, 0:1024],
                                                       ALU.subtract),
                        waits=[tF, tok_sl_free[5]], inc=True)
                tok_PB_free = tFE
                tt3 = E("S", lambda s: s.activation(SL(6), SL(5), AF.Ln),
                        waits=[tFE, tok_sl_free[6]], inc=True)
                td = E("V", lambda v: v.tensor_tensor(SL(7), SL(6), SL(1), ALU.subtract),
                       waits=[tt3, tok_sl_free[7]], inc=True)
                tad = E("S", lambda s: s.activation(SL(6), SL(7), AF.Abs),
                        waits=[td], inc=True)
                ted = E("S", lambda s: s.activation(SL(7), SL(6), AF.Exp, scale=-1.0),
                        waits=[tad], inc=True)
                tp1 = E("V", lambda v: v.tensor_single_scalar(SL(7), SL(7), 1.0, ALU.add),
                        waits=[ted], inc=True)
                tgam = E("S", lambda s: s.activation(SL(6), SL(7), AF.Ln), waits=[tp1], inc=True)
                tst5 = E("V", lambda v: v.tensor_tensor(r_s0, r_mu, PTOT[0:1, :], ALU.add),
                         waits=[tFt], inc=True)
                tst5b = E("V", lambda v: v.reciprocal(r_s1, r_s0), waits=[tst5], inc=True)
                tst6 = E("S", lambda s: s.activation(r_lnt2, r_s0, AF.Ln), waits=[tst5b],
                         inc=True)
                tst7a = E("V", lambda v: v.tensor_copy(r_phat32, r_phc), waits=[tst6], inc=True)
                tst7b = E("V", lambda v: v.tensor_tensor(r_phat32, r_phat32, r_Lw, ALU.add),
                         waits=[tst7a], inc=True)
                tst8 = E("V", lambda v: v.tensor_tensor(r_phat, r_phat32, r_lnt2, ALU.add),
                        waits=[tst7b], inc=True)
                tQ = None
                for h in range(2):
                    tQ = E("T", lambda t, h=h: t.matmul(
                        PB[:, 512 * h:512 * (h + 1)], RF[:], SLh(6, h), start=True, stop=True),
                        waits=[tgam, tFE] if h == 0 else (), inc=(h == 1))
                tQt = None
                for h in range(2):
                    tQt = E("T", lambda t, h=h: t.matmul(
                        PTOT[0:1, 512 * h:512 * (h + 1)], ONESCF[:], SLh(6, h),
                        start=True, stop=True),
                        waits=[tst5, tgam] if h == 0 else (), inc=(h == 1))
                tqp = E("V", lambda v: v.tensor_tensor(SL(8), PB[:], SL(6), ALU.subtract),
                        waits=[tQ, tok_sl_free[8]], inc=True)
                tok_PB_free = tqp
                tEQ = E("S", lambda s: s.activation(SL(9), SL(8), AF.Exp),
                        waits=[tqp, tok_sl_free[9]], inc=True)
                dz = ddump("z", SL(1), tz)
                dg = ddump("g", SL(2), tg)
                dfe = ddump("fe", SL(5), tFE)
                dgam = ddump("gam", SL(6), tgam)
                dqp = ddump("qp", SL(8), tqp)
                tzq = E("V", lambda v: v.tensor_tensor(SL(1), SL(1), SL(8), ALU.subtract),
                        waits=[tqp, dz], inc=True)
                tgt = E("S", lambda s: s.activation(SL(0), SL(1), AF.Exp), waits=[tzq],
                        inc=True)
                tgtv = E("V", lambda v: v.tensor_tensor(GB[:, 1024:2048], SL(0), SL(3),
                                                        ALU.mult),
                         waits=[tgt, tvf, tok_gb_free, tok_gb_free2], inc=True)
                tP = None
                for h in range(2):
                    E("T", lambda t, h=h: t.matmul(
                        PA[:, 512 * h:512 * (h + 1)], RB[:],
                        GB[:, 1024 + 512 * h:1024 + 512 * (h + 1)], start=True, stop=False),
                      waits=[tgtv, ter] if h == 0 else ())
                    tP = E("T", lambda t, h=h: t.matmul(
                        PA[:, 512 * h:512 * (h + 1)], ONES1[:],
                        r_nu[:, 512 * h:512 * (h + 1)], start=False, stop=True),
                        waits=[ts4] if h == 0 else (), inc=(h == 1))
                tst10 = E("S", lambda s: s.activation(r_s2, PTOT[0:1, :], AF.Exp),
                          waits=[tQt], inc=True)
                tPt = None
                for h in range(2):
                    tPt = E("T", lambda t, h=h: t.matmul(
                        PTOT[0:1, 512 * h:512 * (h + 1)], ONESCB[:],
                        GB[:, 1024 + 512 * h:1024 + 512 * (h + 1)], start=True, stop=True),
                        waits=[tst10, tgtv] if h == 0 else (), inc=(h == 1))
                deq = ddump("eq", SL(9), tEQ)
                dgtv = ddump("gtv", GB[:, 1024:2048], tgtv, BF16)
                ten = E("V", lambda v: v.tensor_tensor(SL(7), EU[:], SL(2), ALU.mult),
                        waits=[tgam], inc=True)
                tden = E("V", lambda v: v.tensor_tensor(SL(5), SL(5), SL(7), ALU.add),
                         waits=[ten, tt3, dfe], inc=True)
                te1p = E("V", lambda v: v.tensor_single_scalar(SL(4), SL(4), 1.0, ALU.add),
                         waits=[ter], inc=True)
                tden2 = E("V", lambda v: v.tensor_tensor(SL(5), SL(5), SL(4), ALU.mult),
                          inc=True)
                dden2 = ddump("den2", SL(5), tden2)
                tln5 = E("S", lambda s: s.activation(SL(4), SL(5), AF.Ln),
                         waits=[tden2, dqp, dden2], inc=True)
                trec = E("S", lambda s: s.activation(SL(8), SL(4), AF.Exp, scale=-1.0),
                         waits=[tln5, tEQ], inc=True)
                tn1 = E("V", lambda v: v.tensor_tensor(SL(6), PA[:], GB[:, 1024:2048],
                                                       ALU.subtract),
                       waits=[tP, tQt, dgam], inc=True)
                tok_PA_free = tn1
                tn2 = E("V", lambda v: v.tensor_tensor(SL(6), SL(6), SL(9), ALU.mult),
                        inc=True)
                tdn = E("V", lambda v: v.tensor_tensor(SL(7), SL(7), SL(3), ALU.mult),
                       waits=[tden], inc=True)
                tnum = E("V", lambda v: v.tensor_tensor(SL(6), SL(6), SL(7), ALU.add),
                         inc=True)
                dnum = ddump("num", SL(6), tnum)
                tyz = E("V", lambda v: v.tensor_tensor(SL(6), SL(6), SL(8), ALU.mult),
                        waits=[trec, dnum], inc=True)
                tx1 = E("V", lambda v, tc=tc: v.tensor_tensor(
                    H1[:, tc, :], H1[:, tc, :], SL(6), ALU.add),
                    waits=[t_h1[tc]], inc=True)
                tst11 = E("V", lambda v: v.tensor_tensor(r_s0, r_nu, PTOT[0:1, :], ALU.add),
                          waits=[tPt, tst8], inc=True)
                tok_PTOT_free = tst11
                tst12 = E("V", lambda v: v.tensor_tensor(r_s0, r_s0, r_s2, ALU.mult),
                         waits=[tst11], inc=True)
                tst13 = E("V", lambda v: v.tensor_tensor(r_nhat, r_s0, r_s1, ALU.mult),
                         waits=[tst12], inc=True)
                tok_state = tst13
                tok_sl_free[0] = tgtv; tok_sl_free[1] = tgt; tok_sl_free[2] = tdn
                tok_sl_free[3] = tdn; tok_sl_free[4] = trec; tok_sl_free[5] = tln5
                tok_sl_free[6] = tyz; tok_sl_free[7] = tnum; tok_sl_free[8] = tyz
                tok_sl_free[9] = tn2
                tok_gb_free = tn1; tok_gb_free2 = tPt
                if dbg:
                    E("G", lambda g, n=n, tc=tc: g.dma_start(
                        out=dbg_r["x1"][n][tc], in_=H1[:, tc, :]),
                      waits=[tx1], inc=True, dma=True)
            t_x1 = K.tok("V")
            tok_mix_free = K.tok("T")
            tok_poolw_free = K.tok("T")

            if stop_stage == 'x1':
                stage_dump([t_x1, K.tok("S")])
                tok_x0_free = K.tok("G"); tok_h1_free = K.tok("G")
                tok_trf_free = t_trf
                continue

            tcar2 = E("V", lambda v: v.tensor_copy(TRF[:, :, 0:1], CAR2[:]),
                      waits=[tcarS], inc=True)
            for tc in range(4):
                x1c = H1[:, tc, :]
                st, rstd, nmr = ln_stats(x1c, 2, [t_x1])
                tx = E("S", lambda s, a=x1c, r=rstd, nm=nmr: s.activation(
                    TMS[:], a, AF.Identity, scale=r, bias=nm), waits=st, inc=True)
                te = transpose_to_fm(TMS[:], TRF[:, :, 1 + 128 * tc:1 + 128 * (tc + 1)],
                                     [tx, tcar2], tok_PT_free)
                tok_PT_free = te
            t_trf2 = tok_PT_free
            tmixb = None
            for mi, (cg, cg1) in enumerate([(7, 8), (9, 10)]):
                for a in range(8):
                    cur = TRF[:, a, 1:513]; prv = TRF[:, a, 0:512]
                    o = MIX[:, mi, a, :]
                    E("V", lambda v, c=cur, g=cg, a=a: v.tensor_scalar(
                        SLh(10, 0), c, MIXCO[:, a, g:g + 1], MIXCO[:, a, 11:12],
                        ALU.mult, ALU.add),
                      waits=[t_trf2, tok_mix_free] if (mi == 0 and a == 0) else ())
                    E("V", lambda v, p=prv, g=cg1, a=a: v.tensor_scalar(
                        SLh(10, 1), p, MIXCO[:, a, g:g + 1], None, ALU.mult, ALU.bypass))
                    tmixb = E("V", lambda v, o=o: v.tensor_tensor(o, SLh(10, 0), SLh(10, 1),
                                                                  ALU.add), inc=True)
                    if n == 0:
                        tmixb = E("V", lambda v, o=o, c=cur, g=cg, a=a, mi=mi: v.tensor_scalar(
                            o[:, 0:1], c[:, 0:1], MIXCO[:, a, g:g + 1],
                            MIXCO[:, a, 15 + mi:16 + mi], ALU.mult, ALU.add), inc=True)
            tcarS2 = E("V", lambda v: v.tensor_copy(CAR2[:], TRF[:, :, 512:513]), inc=True)
            t_mixB = tmixb

            wcr_sb = POOLW[:, 0:8192]
            t_wcr = load_w(wcr_sb, "WcrT", [tok_poolw_free])
            wcr3 = wcr_sb.rearrange("p (a k) -> p a k", a=8)
            wckh_sb = POOLW[:, 8192:24576]
            ECR = SAb[:, 4:6, :].rearrange("p a k -> p (a k)")
            tecr = []
            for tc in range(4):
                trc = None
                for h in range(2):
                    for a in range(8):
                        trc = E("T", lambda t, h=h, a=a, tc=tc: t.matmul(
                            PB[:, 512 * h:512 * (h + 1)],
                            MIX[:, 1, a, 128 * tc:128 * (tc + 1)],
                            wcr3[:, a, 512 * h:512 * (h + 1)],
                            start=(a == 0), stop=(a == 7)),
                            waits=[t_mixB, t_wcr, tok_PB_free] if (h == 0 and a == 0) else (),
                            inc=(h == 1 and a == 7))
                t_e = E("S", lambda s, tc=tc: s.activation(
                    ECR[:, 1024 * tc:1024 * (tc + 1)], PB[:], AF.Exp, scale=-1.0),
                    waits=[trc] + ([tok_sl_free[4], tok_sl_free[5]] if tc == 0 else []),
                    inc=True)
                tok_PB_free = t_e
                tecr.append(t_e)
            KC = KCA[:, :]
            t_kc = None
            for wh in range(2):
                twl = E("D", lambda d, wh=wh: d.dma_start(
                    out=wckh_sb, in_=wscr["WckT"][:, 16384 * wh:16384 * (wh + 1)]),
                    waits=[tok_poolw_free if wh == 0 else t_kc], inc=True, dma=True)
                wck4 = wckh_sb.rearrange("p (hs a k) -> p hs a k", hs=16, a=8)
                for hs in range(16):
                    hsg = 16 * wh + hs
                    tkp = None
                    for a in range(8):
                        tkp = E("T", lambda t, hs=hs, a=a, w4=wck4: t.matmul(
                            PA[:, 0:512], w4[:, hs, a, :], MIX[:, 0, a, :],
                            start=(a == 0), stop=(a == 7)),
                            waits=([twl, t_mixB, tok_PA_free] +
                                   ([tok_sl_free[0], tok_sl_free[1], tok_sl_free[2],
                                     tok_sl_free[3], tok_sl_free[6], tok_sl_free[7]]
                                    if (wh == 0 and hs == 0) else []))
                            if a == 0 else (), inc=(a == 7))
                    tr1 = E("V", lambda v: v.tensor_single_scalar(
                        SLh(10, 0), PA[:, 0:512], 0.0, ALU.max),
                        waits=[tkp, t_kc], inc=True)
                    t_kc = E("G", lambda g, hsg=hsg: g.tensor_tensor(
                        KC[:, 512 * hsg:512 * (hsg + 1)], SLh(10, 0), SLh(10, 0), ALU.mult),
                        waits=[tr1], inc=True)
                    tok_PA_free = t_kc

            if dbg and n == 0:
                for _nm, _ap, _tok, _dt in [("xck", MIX[:, 0, :, :], t_mixB, BF16),
                                            ("xcr", MIX[:, 1, :, :], t_mixB, BF16),
                                            ("kc", KC, t_kc, BF16),
                                            ("ecr", ECR, tecr[3], BF16)]:
                    _dd = nc.dram_tensor("dbgw_" + _nm, list(_ap.shape), _dt,
                                         kind="ExternalOutput").ap()
                    E("G", lambda g, a=_ap, o=_dd: g.dma_start(out=o, in_=a),
                      waits=[_tok], inc=True, dma=True)
            wcv_sb = POOLW[:, 0:16384]
            kcv = KC.rearrange("p (g k) -> p g k", g=32)
            tep_last = None
            tat_prev = None
            tsv_prev = None
            t_sv = {}
            for dh in range(2):
                t_wcv = E("D", lambda d, dh=dh: d.dma_start(
                    out=wcv_sb, in_=wscr["WcvT"][:, 16384 * dh:16384 * (dh + 1)]),
                    waits=[K.tok("T")], inc=True, dma=True)
                wcv3 = wcv_sb.rearrange("p (a k) -> p a k", a=32)
                for tc in range(4):
                    PX = (PA if tc % 2 == 0 else PB)
                    col = 512 * (tc // 2)
                    tvc = None
                    for hs in range(32):
                        tvc = E("T", lambda t, hs=hs, tc=tc, PX=PX, col=col, w3=wcv3: t.matmul(
                            PX[:, col:col + 512], kcv[:, hs, 128 * tc:128 * (tc + 1)],
                            w3[:, hs, :], start=(hs == 0), stop=(hs == 31)),
                            waits=([t_wcv, t_kc, t_sv.get((dh - 1, tc)),
                                    t_sv.get((dh, tc - 2))] +
                                   ([tok_PB_free] if (dh == 0 and tc == 1) else []))
                            if hs == 0 else (),
                            inc=(hs == 31))
                    tep = E("V", lambda v, tc=tc, dh=dh: v.tensor_single_scalar(
                        SLh(6, 0), ECR[:, 1024 * tc + 512 * dh:1024 * tc + 512 * dh + 512],
                        1.0, ALU.add),
                        waits=[tecr[tc], tvc, tat_prev] +
                              ([tok_sl_free[6]] if (dh == 0 and tc == 0) else []),
                        inc=True)
                    tln2 = E("S", lambda s: s.activation(SLh(7, 0), SLh(6, 0), AF.Ln),
                             waits=[tep, tsv_prev] +
                                   ([tok_sl_free[7]] if (dh == 0 and tc == 0) else []),
                             inc=True)
                    trc2 = E("S", lambda s: s.activation(SLh(7, 1), SLh(7, 0), AF.Exp,
                                                         scale=-1.0),
                             waits=[tln2], inc=True)
                    tsv = E("V", lambda v, PX=PX, col=col: v.tensor_tensor(
                        SLh(6, 0), PX[:, col:col + 512], SLh(7, 1), ALU.mult),
                        waits=[trc2], inc=True)
                    tsv_prev = tsv
                    tat = E("V", lambda v, tc=tc, dh=dh: v.tensor_tensor(
                        H1[:, tc, 512 * dh:512 * dh + 512], H1[:, tc, 512 * dh:512 * dh + 512],
                        SLh(6, 0), ALU.add), waits=[tsv], inc=True)
                    tat_prev = tat
                    t_sv[(dh, tc)] = tsv
                    tep_last = tat
                    if dbg:
                        E("G", lambda g, n=n, tc=tc: g.dma_start(
                            out=dbg_r["attn"][n][tc], in_=H1[:, tc, :]),
                          waits=[tat], inc=True, dma=True)
            tok_PB_free = tep_last; tok_PA_free = tep_last
            t_attn = K.tok("V")
            tok_sl_free[4] = tep_last; tok_sl_free[5] = tep_last
            tok_sl_free[6] = tep_last; tok_sl_free[7] = tep_last
            tok_poolw_free = K.tok("T")

            if stop_stage == 'attn':
                stage_dump([t_attn, K.tok("S")])
                tok_x0_free = K.tok("G"); tok_h1_free = K.tok("G")
                tok_trf_free = t_trf
                continue

            prev_c = None
            for tc in range(4):
                ac = H1[:, tc, :]
                t_x = E("D", lambda d, n=n, tc=tc: d.dma_start(out=XC[:], in_=x_r2[n][tc]),
                        waits=[t_xc, prev_c], inc=True, dma=True)
                st, rstd, nmr = ln_stats(ac, 3, [t_attn, prev_c])
                tnm = E("S", lambda s, a=ac, r=rstd, nm=nmr: s.activation(
                    SL(10), a, AF.Identity, scale=r, bias=nm), waits=st, inc=True)
                tx2 = E("V", lambda v, tc=tc: v.tensor_tensor(H1[:, tc, :], XC[:], SL(10),
                                                              ALU.add),
                        waits=[tnm, t_x], inc=True)
                t_xc = tx2
                st2, rstd2, nmr2 = ln_stats(ac, 4, [tx2])
                th2 = E("S", lambda s, x=ac, r=rstd2, nm=nmr2: s.activation(
                    TMS[:], x, AF.Identity, scale=r, bias=nm), waits=st2, inc=True)
                te = transpose_to_fm(TMS[:], MIX[:, 0, :, 128 * tc:128 * (tc + 1)],
                                     [th2, K.tok("T")], tok_PT_free)
                tok_PT_free = te
                prev_c = tx2
                if dbg:
                    E("G", lambda g, n=n, tc=tc: g.dma_start(
                        out=dbg_r["x2"][n][tc], in_=H1[:, tc, :]),
                      waits=[tx2], inc=True, dma=True)
            t_h2f = tok_PT_free
            t_x2 = K.tok("V")

            if stop_stage == 'x2':
                stage_dump([t_x2, K.tok("S"), K.tok("T")])
                tok_x0_free = K.tok("G"); tok_h1_free = K.tok("G")
                tok_trf_free = t_trf
                continue

            u1_sb = POOLW[:, 0:2048]; v1_sb = POOLW[:, 2048:10240]
            u2_sb = POOLW[:, 10240:18432]; v2_sb = POOLW[:, 18432:20480]
            t_u1 = load_w(u1_sb, "U1T", [tok_poolw_free])
            t_v1 = load_w(v1_sb, "V1T", [])
            t_u2 = load_w(u2_sb, "U2T", [])
            t_v2 = load_w(v2_sb, "V2T", [])
            PBUF = PQB[:, :].rearrange("p (a k) -> p a k", a=2)
            QBUF = PBUF
            u13 = u1_sb.rearrange("p (a k) -> p a k", a=8)
            v13 = v1_sb.rearrange("p (a k) -> p a k", a=2)
            u23 = u2_sb.rearrange("p (a k) -> p a k", a=32)
            v23 = v2_sb.rearrange("p (a k) -> p a k", a=2)
            for rs in range(2):
                for a in range(8):
                    E("T", lambda t, rs=rs, a=a: t.matmul(
                        PA[:, 512 * rs:512 * (rs + 1)], u13[:, a, 128 * rs:128 * (rs + 1)],
                        MIX[:, 0, a, :], start=(a == 0), stop=(a == 7)),
                        waits=[t_u1, t_h2f, tok_PA_free] if (rs == 0 and a == 0) else (),
                        inc=(rs == 1 and a == 7))
            tpb = E("S", lambda s: s.copy(PQB[:, :], PA[:]),
                    waits=[K.tok("T")], inc=True)
            tok_PA_free = tpb
            if stop_stage == 'ffp':
                stage_dump([tpb, K.tok("S"), K.tok("T")])
                tok_x0_free = K.tok("G"); tok_h1_free = K.tok("G")
                tok_trf_free = t_trf; tok_nu_guard = K.tok("T")
                continue
            HB = KC
            t_hh = None
            tsg_p = None
            tr2 = None
            for fs in range(32):
                th = None
                for rs in range(2):
                    th = E("T", lambda t, fs=fs, rs=rs: t.matmul(
                        PB[:, 0:512], v13[:, rs, 128 * fs:128 * (fs + 1)], PBUF[:, rs, :],
                        start=(rs == 0), stop=(rs == 1)),
                        waits=([t_v1, tpb, tok_PB_free] if fs == 0 else [tok_PB_free])
                        if rs == 0 else (),
                        inc=(rs == 1))
                tug = E("V", lambda v, fs=fs: v.tensor_scalar(
                    SLh(10, 0), PB[:, 0:512], FB1[:, fs:fs + 1], None, ALU.add, ALU.bypass),
                    waits=[th], inc=True)
                tok_PB_free = tug
                E("V", lambda v: v.tensor_tensor(SLh(10, 1), SLh(10, 0), SLh(10, 0), ALU.mult))
                E("V", lambda v: v.tensor_scalar(SLh(10, 1), SLh(10, 1), GC1, GC2,
                                                 ALU.mult, ALU.add))
                t3 = E("V", lambda v: v.tensor_tensor(SLh(10, 1), SLh(10, 1), SLh(10, 0),
                                                      ALU.mult), inc=True)
                tsg = E("S", lambda s: s.activation(SLh(9, 0), SLh(10, 1), AF.Exp,
                                                    scale=-2.0),
                        waits=[t3] + ([tok_sl_free[9]] if fs == 0 else []), inc=True)
                te1 = E("V", lambda v: v.tensor_single_scalar(SLh(9, 0), SLh(9, 0), 1.0,
                                                              ALU.add), waits=[tsg], inc=True)
                tln9 = E("S", lambda s: s.activation(SLh(9, 1), SLh(9, 0), AF.Ln),
                         waits=[te1], inc=True)
                tr2 = E("S", lambda s: s.activation(SLh(9, 0), SLh(9, 1), AF.Exp,
                                                    scale=-1.0), waits=[tln9], inc=True)
                t_hh = E("V", lambda v, fs=fs: v.tensor_tensor(
                    HB[:, 512 * fs:512 * (fs + 1)], SLh(10, 0), SLh(9, 0), ALU.mult),
                    waits=([t_kc] if fs == 0 else []) + [tr2], inc=True)
            if stop_stage == 'ffh':
                stage_dump([t_hh, K.tok("S"), K.tok("T")])
                tok_x0_free = K.tok("G"); tok_h1_free = K.tok("G")
                tok_trf_free = t_trf; tok_nu_guard = K.tok("T")
                continue
            for rs in range(2):
                for fs in range(32):
                    E("T", lambda t, rs=rs, fs=fs: t.matmul(
                        PA[:, 512 * rs:512 * (rs + 1)], u23[:, fs, 128 * rs:128 * (rs + 1)],
                        HB[:, 512 * fs:512 * (fs + 1)], start=(fs == 0), stop=(fs == 31)),
                        waits=[t_u2, t_hh, tok_PA_free] if (rs == 0 and fs == 0) else (),
                        inc=(rs == 1 and fs == 31))
            tqb = E("S", lambda s: s.copy(PQB[:, :], PA[:]),
                    waits=[K.tok("T")], inc=True)
            tok_PA_free = tqb
            if stop_stage == 'ffq':
                stage_dump([tqb, K.tok("S"), K.tok("T")])
                tok_x0_free = K.tok("G"); tok_h1_free = K.tok("G")
                tok_trf_free = t_trf; tok_nu_guard = K.tok("T")
                continue
            t_fb2 = E("D", lambda d: d.dma_start(out=r_fb2, in_=fb2_d),
                      waits=[tok_mix_free, tok_state], inc=True, dma=True)
            prev_e = None
            for tc in range(4):
                tf = None
                for h in range(2):
                    for rs in range(2):
                        E("T", lambda t, h=h, rs=rs, tc=tc: t.matmul(
                            PB[:, 512 * h:512 * (h + 1)],
                            QBUF[:, rs, 128 * tc:128 * (tc + 1)],
                            v23[:, rs, 512 * h:512 * (h + 1)], start=(rs == 0), stop=False),
                            waits=[t_v2, tqb, tok_PB_free, t_fb2]
                            if (h == 0 and rs == 0) else ())
                    tf = E("T", lambda t, h=h: t.matmul(
                        PB[:, 512 * h:512 * (h + 1)], ONES1[:],
                        r_fb2[:, 512 * h:512 * (h + 1)], start=False, stop=True),
                        inc=(h == 1))
                if stop_stage == 'ffe1':
                    tok_PB_free = E("V", lambda v: v.tensor_copy(SL(10), PB[:]),
                                    waits=[tf], inc=True)
                    prev_e = tok_PB_free
                    continue
                st, rstd, nmr = ln_stats(PB[:], 5, [tf, prev_e], psum_src=True)
                if stop_stage == 'ffe2':
                    tok_PB_free = st[1]
                    prev_e = st[1]
                    continue
                tnm = E("S", lambda s, r=rstd, nm=nmr: s.activation(
                    SL(10), PB[:], AF.Identity, scale=r, bias=nm), waits=st, inc=True)
                tok_PB_free = tnm
                if stop_stage == 'ffe3':
                    prev_e = tnm
                    continue
                tout = E("G", lambda g, tc=tc: g.tensor_tensor(
                    H1[:, tc, :], H1[:, tc, :], SL(10), ALU.add),
                    waits=[tnm], inc=True)
                prev_e = tout
                E("G", lambda g, n=n, tc=tc: g.dma_start(out=out_r[n][tc], in_=H1[:, tc, :]),
                  inc=True, dma=True)
            if stop_stage in ('ffe1', 'ffe2', 'ffe3'):
                stage_dump([K.tok("S"), K.tok("V"), K.tok("T")])
            tok_x0_free = K.tok("G")
            tok_h1_free = K.tok("G")
            tok_nu_guard = K.tok("T")
            for i in range(10):
                tok_sl_free[i] = tqb
            tok_poolw_free = K.tok("T")
            t_cast_done = None

        E("G", lambda g: g.wait_ge(sem["G"], K.cnt["G"]))
        K.check()

        waited = {e: {} for e in Sched.ENGS}

        def replay(e, eng):
            for waits, fn, inc, amt, sm in K.q[e]:
                for we, wv in waits:
                    if waited[e].get(we, 0) < wv:
                        eng.wait_ge(sem[we], wv)
                        waited[e][we] = wv
                inst = fn(eng)
                if inc:
                    inst.then_inc(sem[sm], amt)

        @blk.tensor
        def _(tensor):
            replay("T", tensor)

        @blk.vector
        def _(vector):
            replay("V", vector)

        @blk.scalar
        def _(scalar):
            replay("S", scalar)

        @blk.sync
        def _(sync):
            replay("D", sync)

        @blk.gpsimd
        def _(gpsimd):
            replay("G", gpsimd)

    return nc


def make_inputs(inp, b, T):
    p = {k[8:]: np.asarray(v, np.float32) for k, v in inp.items()
         if k.startswith("L0_attn_") and k != "L0_attn_x"}
    f = {k[7:]: np.asarray(v, np.float32) for k, v in inp.items() if k.startswith("L0_ffn_")}
    x = np.asarray(inp["L0_attn_x"][b], np.float32)[:T]
    wck = np.ascontiguousarray(
        p["Wck"].T.reshape(8, 128, 32, 128).transpose(1, 2, 0, 3).reshape(128, 32768))
    d = {
        "x": np.ascontiguousarray(x),
        "WkT": _lhsT(p["Wk"].T, 8).reshape(128, -1),
        "WvT": _lhsT(p["Wv"].T, 8).reshape(128, -1),
        "WrT": _lhsT(p["Wr"].T, 8).reshape(128, -1),
        "WcrT": _lhsT(p["Wcr"].T, 8).reshape(128, -1),
        "WckT": wck,
        "WcvT": np.ascontiguousarray(
            p["Wcv"].T.astype(np.float32).reshape(32, 128, 2, 512)
            .transpose(1, 2, 0, 3).reshape(128, 32768)),
        "U1T": _lhsT(f["U1"], 8).reshape(128, -1),
        "V1T": _lhsT(f["V1"], 2).reshape(128, -1),
        "U2T": _lhsT(f["U2"], 32).reshape(128, -1),
        "V2T": _lhsT(f["V2"], 2).reshape(128, -1),
        "fmrows": np.ascontiguousarray(np.stack(
            [_fm(p["rk"]), _fm(p["rv"]), _fm(p["rr"]), _fm(p["crk"]), _fm(p["crr"]),
             _fm(p["ln1_g"]), _fm(p["ln1_b"]), _fm(p["ln2_g"]), _fm(p["ln2_b"])], axis=2)),
        "fb1fm": np.ascontiguousarray(np.asarray(f["b1"], np.float32).reshape(32, 128).T),
        "wrow": np.asarray(p["w"], np.float32).reshape(1, -1).copy(),
        "urow": np.asarray(p["u"], np.float32).reshape(1, -1).copy(),
        "fb2row": np.asarray(f["b2"], np.float32).reshape(1, -1).copy(),
        "Rf32": np.triu(np.ones((128, 128), np.float32)),
        "Rbf": np.triu(np.ones((128, 128))).astype(BF),
        "identbf": np.eye(128).astype(BF),
        "iotacol": np.arange(128, dtype=np.float32).reshape(-1, 1),
        "ones1": np.ones((1, 128), np.float32),
        "onescb": np.ones((128, 1)).astype(BF),
        "onescf": np.ones((128, 1), np.float32),
    }
    return d


def kernel(**inputs):
    T = 8192
    nc = build_program(T)
    in_maps = [make_inputs(inputs, b, T) for b in range(8)]
    res = run_bass_kernel_spmd(nc, in_maps, core_ids=list(range(8)))
    return np.stack([res.results[b]["out"] for b in range(8)], axis=0)


# version 11
# speedup vs baseline: 1.1058x; 1.0074x over previous
import numpy as np
import ml_dtypes
import concourse.bass as bass
import concourse.mybir as mybir
from concourse.bass_utils import run_bass_kernel_spmd
from contextlib import ExitStack

F32 = mybir.dt.float32
BF16 = mybir.dt.bfloat16
AF = mybir.ActivationFunctionType
ALU = mybir.AluOpType
BF = ml_dtypes.bfloat16

C = 1024
L = 128
TT_TILE = 512
NCHUNK = TT_TILE // L
PC_CLAMP = -15.0
GC1 = 0.044715 * 0.7978845608028654
GC2 = 0.7978845608028654


def _lhsT(Wcd, ng):
    return np.ascontiguousarray(
        np.asarray(Wcd, np.float32).reshape(ng, 128, -1).transpose(1, 0, 2))


def _fm(v):
    return np.ascontiguousarray(np.asarray(v, np.float32).reshape(8, 128).T)


class Sched:
    ENGS = "TVSDG"
    SEMS = "TVSDGP"

    def __init__(self, serialize=False):
        self.q = {e: [] for e in self.ENGS}
        self.cnt = {e: 0 for e in self.SEMS}
        self.serialize = serialize
        self.last = None

    def emit(self, e, fn, waits=(), inc=False, dma=False):
        w = [t for t in waits if t is not None]
        if self.serialize:
            inc = True
            if self.last is not None:
                w = w + [self.last]
        sem = e
        if e == "G" and not dma:
            sem = "P"
        amt = 16 if dma else 1
        self.q[e].append((w, fn, inc, amt, sem))
        if inc:
            self.cnt[sem] += amt
            tok = (sem, self.cnt[sem])
            if self.serialize:
                self.last = tok
            return tok
        return None

    def tok(self, e):
        return (e, self.cnt[e])

    def check(self):
        ptr = {e: 0 for e in self.ENGS}
        done = {s: 0 for s in self.SEMS}
        progress = True
        while progress:
            progress = False
            for e in self.ENGS:
                while ptr[e] < len(self.q[e]):
                    waits, _, inc, amt, sem = self.q[e][ptr[e]]
                    if any(done[we] < wv for we, wv in waits):
                        break
                    if inc:
                        done[sem] += amt
                    ptr[e] += 1
                    progress = True
        stuck = {e: (ptr[e], len(self.q[e])) for e in self.ENGS if ptr[e] < len(self.q[e])}
        if stuck:
            msgs = []
            for e in stuck:
                waits, _, _, _, _ = self.q[e][ptr[e]]
                unmet = [(we, wv, done[we]) for we, wv in waits if done[we] < wv]
                msgs.append(f"{e}@{ptr[e]}/{len(self.q[e])} needs {unmet}")
            raise RuntimeError("deadlock: " + "; ".join(msgs))


def build_program(T, dbg=False, serialize=False, stop_stage='out'):
    nt = T // TT_TILE
    nc = bass.Bass("TRN2", target_bir_lowering=False, detect_race_conditions=False)

    def din(name, shape, dtype=mybir.dt.float32):
        return nc.dram_tensor(name, list(shape), dtype, kind="ExternalInput").ap()

    x_d = din("x", [T, C])
    out_d = nc.dram_tensor("out", [T, C], mybir.dt.float32, kind="ExternalOutput").ap()
    wk_d = din("WkT", [128, 8192]); wv_d = din("WvT", [128, 8192])
    wr_d = din("WrT", [128, 8192]); wcr_d = din("WcrT", [128, 8192])
    wck_d = din("WckT", [128, 32768]); wcv_d = din("WcvT", [128, 32768])
    u1_d = din("U1T", [128, 2048]); v1_d = din("V1T", [128, 8192])
    u2_d = din("U2T", [128, 8192]); v2_d = din("V2T", [128, 2048])
    fmr_d = din("fmrows", [128, 8, 9])
    fb1_d = din("fb1fm", [128, 32])
    wrow_d = din("wrow", [1, 1024]); urow_d = din("urow", [1, 1024])
    fb2_d = din("fb2row", [1, 1024])
    Rf_d = din("Rf32", [128, 128]); Rb_d = din("Rbf", [128, 128], BF16)
    id_d = din("identbf", [128, 128], BF16)
    iota_d = din("iotacol", [128, 1]); ones1_d = din("ones1", [1, 128])
    onescb_d = din("onescb", [128, 1], BF16); onescf_d = din("onescf", [128, 1])
    wscr = {}
    for nm, w in [("WkT", 8192), ("WvT", 8192), ("WrT", 8192), ("WcrT", 8192),
                  ("WckT", 32768), ("WcvT", 32768), ("U1T", 2048), ("V1T", 8192),
                  ("U2T", 8192), ("V2T", 2048)]:
        wscr[nm] = nc.dram_tensor("s_" + nm, [128, w], BF16).ap()
    wraw = {"WkT": wk_d, "WvT": wv_d, "WrT": wr_d, "WcrT": wcr_d, "WckT": wck_d,
            "WcvT": wcv_d, "U1T": u1_d, "V1T": v1_d, "U2T": u2_d, "V2T": v2_d}
    dbg_d = {}
    if dbg:
        for nm in ["h1", "x1", "attn", "x2"]:
            dbg_d[nm] = nc.dram_tensor("dbg_" + nm, [T, C], mybir.dt.float32,
                                       kind="ExternalOutput").ap()

    es = ExitStack()
    with es:
        sb = lambda n, s, d=F32: es.enter_context(nc.sbuf_tensor(n, list(s), d))
        pst = lambda n, s, d=F32: es.enter_context(nc.psum_tensor(n, list(s), d))
        POOLW = sb("poolw", [128, 24576], BF16)
        SA = sb("sarena", [128, 11, 1024])
        SAb = SA.bitcast(BF16)
        GB = sb("gbpair", [128, 2048], BF16)
        MIX = sb("mix", [128, 3, 8, 512], BF16)
        TRF = sb("trfm", [128, 8, 513], BF16)
        TMS = sb("tmstage", [128, 1024], BF16)
        XC = sb("xc", [128, 1024])
        H1 = sb("h1t", [128, 4, 1024])
        KCA = sb("kcarena", [128, 16384], BF16)
        PQB = sb("pqbuf", [128, 1024], BF16)
        ROWA = sb("rowa", [128, 1024]); ROWB = sb("rowb", [128, 1024])
        ROWC = sb("rowc", [128, 1024])
        STAT = sb("stat", [128, 64])
        IOTW = sb("iotw", [128, 1024]); EU = sb("eu", [128, 1024])
        RF = sb("rf", [128, 128]); RB = sb("rb", [128, 128], BF16)
        IDB = sb("idb", [128, 128], BF16)
        ONES1 = sb("onesone", [1, 128]); IOTAC = sb("iotac", [128, 1])
        ONESCB = sb("onescbs", [128, 1], BF16); ONESCF = sb("onescfs", [128, 1])
        MIXCO = sb("mixco", [128, 8, 17])
        FB1 = sb("fb1", [128, 32])
        CAR1 = sb("car1", [128, 8, 1]); CAR2 = sb("car2", [128, 8, 1])
        CONSTS = sb("consts", [128, 3])
        nc.const_aps.register(F32, 1e-5, CONSTS[:, 0:1])
        nc.const_aps.register(F32, 1e-38, CONSTS[:, 1:2])
        nc.const_aps.register(F32, 1.0, CONSTS[:, 2:3])
        PA = pst("PA", [128, 1024]); PB = pst("PB", [128, 1024])
        PT = pst("PT", [128, 1024], BF16)
        PC = pst("PC", [128, 1024])
        sem = {e: es.enter_context(nc.semaphore("sem" + e)) for e in Sched.SEMS}
        blk = es.enter_context(nc.Block())

        K = Sched(serialize=serialize)
        E = K.emit
        FMR = SA[:, 10, 0:72].rearrange("p (a k) -> p a k", a=8)
        r_phc = ROWA[0:1, :]; r_phat = ROWA[32:33, :]; r_s0 = ROWA[64:65, :]
        r_mu = ROWB[0:1, :]; r_nhat = ROWB[32:33, :]; r_s1 = ROWB[64:65, :]
        r_nu = ROWC[0:1, :]; r_Lw = ROWC[32:33, :]; r_s2 = ROWC[64:65, :]
        r_wu = ROWA[0:1, :]; r_fb2 = ROWC[0:1, :]; r_s3 = ROWB[64:65, :]
        r_phat32 = ROWA[32:33, :]; r_lnt2 = ROWB[32:33, :]
        SL = lambda i: SA[:, i, :]
        SL10B = SA[96:128, 10, :]
        SL10R = SA[127:128, 10, :]
        SLh = lambda i, h: SA[:, i, 512 * h:512 * (h + 1)]

        ldt = []
        for dst, src in [(RF[:], Rf_d), (RB[:], Rb_d), (IDB[:], id_d),
                         (ONES1[:], ones1_d), (IOTAC[:], iota_d),
                         (ONESCB[:], onescb_d), (ONESCF[:], onescf_d),
                         (FMR, fmr_d), (FB1[:], fb1_d),
                         (r_wu, wrow_d), (r_s3, urow_d)]:
            ldt.append(E("D", lambda d, o=dst, s=src: d.dma_start(out=o, in_=s),
                         inc=True, dma=True))
        tld = ldt[-1]
        t_lw = E("S", lambda s: s.activation(r_Lw, r_wu, AF.Identity, scale=float(L)),
                 waits=[tld], inc=True)
        t_eur = E("S", lambda s: s.activation(r_mu, r_s3, AF.Exp), inc=True)
        E("T", lambda t: t.matmul(PA[:, 0:512], ONES1[:], r_mu[:, 0:512],
                                  start=True, stop=True), waits=[t_eur])
        t_mm1 = E("T", lambda t: t.matmul(PA[:, 512:1024], ONES1[:], r_mu[:, 512:1024],
                                          start=True, stop=True), inc=True)
        E("T", lambda t: t.matmul(PB[:, 0:512], ONES1[:], r_wu[:, 0:512],
                                  start=True, stop=True), waits=[tld])
        t_mm2 = E("T", lambda t: t.matmul(PB[:, 512:1024], ONES1[:], r_wu[:, 512:1024],
                                          start=True, stop=True), inc=True)
        t_eu = E("V", lambda v: v.tensor_copy(EU[:], PA[:]), waits=[t_mm1], inc=True)
        t_iotw = E("V", lambda v: v.tensor_scalar(IOTW[:], PB[:], IOTAC[:, 0:1], None,
                                                  ALU.mult, ALU.bypass),
                   waits=[t_mm2], inc=True)
        fv = lambda i: FMR[:, :, i]
        mco = lambda i: MIXCO[:, :, i]
        tpm = None
        for dsti, ai, bi in [(0, 0, 5), (2, 1, 5), (4, 2, 5), (7, 3, 7), (9, 4, 7)]:
            tpm = E("V", lambda v, d=dsti, a=ai, b=bi: v.tensor_tensor(mco(d), fv(a), fv(b),
                                                                       ALU.mult),
                    waits=[tld], inc=True)
        for dsti, gi, pi in [(1, 5, 0), (3, 5, 2), (5, 5, 4), (8, 7, 7), (10, 7, 9)]:
            tpm = E("V", lambda v, d=dsti, g=gi, p=pi: v.tensor_tensor(mco(d), fv(g), mco(p),
                                                                       ALU.subtract),
                    waits=[tpm], inc=True)
        E("V", lambda v: v.tensor_copy(mco(6), fv(6)))
        E("V", lambda v: v.tensor_copy(mco(11), fv(8)))
        for dsti, ri, bi in [(12, 0, 6), (13, 1, 6), (14, 2, 6), (15, 3, 8), (16, 4, 8)]:
            tpm = E("V", lambda v, d=dsti, r=ri, b=bi: v.tensor_tensor(mco(d), fv(r), fv(b),
                                                                       ALU.mult),
                    waits=[tpm], inc=True)
        E("V", lambda v: v.memset(CONSTS[:, 0:1], 1e-5))
        E("V", lambda v: v.memset(CONSTS[:, 1:2], 1e-38))
        E("V", lambda v: v.memset(CONSTS[:, 2:3], 1.0))
        E("V", lambda v: v.memset(r_phat, -10000.0))
        E("V", lambda v: v.memset(r_nhat, 0.0))
        E("V", lambda v: v.memset(CAR1[:], 0.0))
        t_init = E("V", lambda v: v.memset(CAR2[:], 0.0), inc=True)

        jobs = []
        for nm in ["WkT", "WvT", "WrT", "WcrT", "WckT", "WcvT", "U1T", "V1T", "U2T", "V2T"]:
            tot = wraw[nm].shape[1]
            for c0 in range(0, tot, 2048):
                jobs.append((wraw[nm][:, c0:c0 + 2048], wscr[nm][:, c0:c0 + 2048]))
        prev_out = [None, None]
        t_cast_done = None
        for i, (srcap, dstap) in enumerate(jobs):
            b = i % 2
            stg = SA[:, 2 * b:2 * b + 2, :].rearrange("p a k -> p (a k)")
            tin = E("D", lambda d, s=srcap, g=stg: d.dma_start(out=g, in_=s),
                    waits=[prev_out[b]], inc=True, dma=True)
            tout = E("G", lambda g, s=stg, dd=dstap: g.dma_start(out=dd, in_=s),
                     waits=[tin], inc=True, dma=True)
            prev_out[b] = tout
            t_cast_done = tout

        last_ln = [t_init]

        def ln_stats(src_ap, idx, tok_in, psum_src=False):
            sm = STAT[:, 4 * idx:4 * idx + 1]
            sq = STAT[:, 4 * idx + 2:4 * idx + 3]
            tsq = E("S", lambda s, a=src_ap: s.activation(SL(10), a, AF.Square),
                    waits=list(tok_in) + [last_ln[0]], inc=True)
            t1 = E("V", lambda v, a=src_ap, o=sm: v.reduce_sum(o, a, axis=mybir.AxisListType.X),
                   waits=list(tok_in) + [last_ln[0]] + ([tsq] if psum_src else []), inc=True)
            t2 = E("V", lambda v, o=sq: v.reduce_sum(o, SL(10), axis=mybir.AxisListType.X),
                   waits=[tsq], inc=True)
            m = STAT[:, 32:33]; v2 = STAT[:, 36:37]; rstd = STAT[:, 40:41]
            nmr = STAT[:, 44:45]; tmp = STAT[:, 48:49]; tmp2 = STAT[:, 52:53]
            ta = E("V", lambda v: v.tensor_single_scalar(m, sm, 1.0 / 1024.0, ALU.mult),
                   waits=[t1, t2], inc=True)
            tb = E("V", lambda v: v.tensor_single_scalar(v2, sq, 1.0 / 1024.0, ALU.mult),
                   waits=[ta], inc=True)
            tc2 = E("V", lambda v: v.tensor_tensor(tmp, m, m, ALU.mult), waits=[tb], inc=True)
            td2 = E("V", lambda v: v.tensor_tensor(v2, v2, tmp, ALU.subtract),
                    waits=[tc2], inc=True)
            tv = E("V", lambda v: v.tensor_single_scalar(v2, v2, 1e-5, ALU.add),
                   waits=[td2], inc=True)
            E("S", lambda s: s.activation(tmp2, v2, AF.Ln), waits=[tv])
            tr = E("S", lambda s: s.activation(rstd, tmp2, AF.Exp, scale=-0.5), inc=True)
            te2 = E("V", lambda v: v.tensor_tensor(tmp, m, rstd, ALU.mult), waits=[tr], inc=True)
            tn = E("V", lambda v: v.tensor_single_scalar(nmr, tmp, -1.0, ALU.mult),
                   waits=[te2], inc=True)
            last_ln[0] = tn
            return [tr, tn], rstd, nmr

        def transpose_to_fm(tm_bf_ap, dst_fm_ap, waits, pt_free):
            tt = None
            for a in range(8):
                tt = E("T", lambda t, a=a, s=tm_bf_ap: t.transpose(
                    PT[:, 128 * a:128 * (a + 1)], s[:, 128 * a:128 * (a + 1)], IDB[:]),
                    waits=(list(waits) + [pt_free]) if a == 0 else (), inc=(a == 7))
            return E("S", lambda s, d=dst_fm_ap: s.copy(
                d, PT[:, :].rearrange("p (a t) -> p a t", a=8)), waits=[tt], inc=True)

        tok_PA_free = t_eu; tok_PB_free = t_iotw
        tok_PT_free = K.tok("V")
        tok_PC_free = None
        tok_PC_free2 = None
        tok_sl10_free = None
        tok_x0_free = None
        tok_h1_free = None
        tok_nu_guard = None
        tok_prol_guard = t_eur
        tok_mix_free = None
        tok_state = t_init
        tok_trf_free = None
        tok_gb_free = None
        tok_gb_free2 = None
        tok_sl_free = {i: None for i in range(11)}
        tok_poolw_free = None

        x_r2 = x_d.rearrange("(n tc p) c -> n tc p c", p=128, tc=4)
        out_r = out_d.rearrange("(n tc p) c -> n tc p c", p=128, tc=4)
        dbg_r = {k2: v2.rearrange("(n tc p) c -> n tc p c", p=128, tc=4)
                 for k2, v2 in dbg_d.items()}

        def load_w(dst2d, nm, waits):
            return E("D", lambda d, o=dst2d, s=wscr[nm]: d.dma_start(out=o, in_=s),
                     waits=waits, inc=True, dma=True)

        for n in range(nt):
            wk_sb = POOLW[:, 0:8192]; wv_sb = POOLW[:, 8192:16384]; wr_sb = POOLW[:, 16384:24576]
            t_wk = load_w(wk_sb, "WkT", [tok_poolw_free, t_cast_done])
            t_wv = load_w(wv_sb, "WvT", [])
            t_wr = load_w(wr_sb, "WrT", [])
            wk3 = wk_sb.rearrange("p (a k) -> p a k", a=8)
            wv3 = wv_sb.rearrange("p (a k) -> p a k", a=8)
            wr3 = wr_sb.rearrange("p (a k) -> p a k", a=8)

            tcar = E("V", lambda v: v.tensor_copy(TRF[:, :, 0:1], CAR1[:]),
                     waits=[tok_trf_free, t_init], inc=True)
            t_h1 = []
            t_xc = None
            for tc in range(4):
                x0c = XC[:, :]; h1c = H1[:, tc, :]
                t_x = E("D", lambda d, n=n, tc=tc: d.dma_start(out=XC[:], in_=x_r2[n][tc]),
                        waits=[t_xc, tok_x0_free] + ([t_cast_done] if tc == 0 else []),
                        inc=True, dma=True)
                st, rstd, nmr = ln_stats(x0c, 0, [t_x])
                th = E("S", lambda s, a=x0c, o=h1c, r=rstd, nm=nmr: s.activation(
                    o, a, AF.Identity, scale=r, bias=nm), waits=st + [tok_h1_free], inc=True)
                t_xc = th
                if stop_stage == 'h1x':
                    t_h1.append(th)
                    t_xc = th
                    continue
                st2, rstd2, nmr2 = ln_stats(h1c, 1, [th])
                tx = E("S", lambda s, a=h1c, r=rstd2, nm=nmr2: s.activation(
                    TMS[:], a, AF.Identity, scale=r, bias=nm), waits=st2, inc=True)
                te = transpose_to_fm(TMS[:], TRF[:, :, 1 + 128 * tc:1 + 128 * (tc + 1)],
                                     [tx, tcar], tok_PT_free)
                tok_PT_free = te
                t_h1.append(th)
                if dbg:
                    E("G", lambda g, n=n, tc=tc, a=h1c: g.dma_start(
                        out=dbg_r["h1"][n][tc], in_=a), waits=[th], inc=True, dma=True)
            t_trf = tok_PT_free

            def stage_dump(waits):
                tt = None
                for tc2 in range(4):
                    tt = E("G", lambda g, n=n, tc2=tc2: g.dma_start(out=out_r[n][tc2],
                                                                   in_=H1[:, tc2, :]),
                           waits=list(waits), inc=True, dma=True)
                return tt

            if stop_stage in ('h1', 'h1x'):
                stage_dump([K.tok("S"), K.tok("V")])
                tok_x0_free = K.tok("G"); tok_h1_free = K.tok("G")
                tok_poolw_free = K.tok("T"); tok_mix_free = K.tok("T")
                tok_trf_free = t_trf
                continue

            tmix = None
            for mi, (cg, cg1) in enumerate([(0, 1), (2, 3), (4, 5)]):
                for a in range(8):
                    cur = TRF[:, a, 1:513]; prv = TRF[:, a, 0:512]
                    o = MIX[:, mi, a, :]
                    E("V", lambda v, c=cur, g=cg, a=a: v.tensor_scalar(
                        SLh(10, 0), c, MIXCO[:, a, g:g + 1], MIXCO[:, a, 6:7],
                        ALU.mult, ALU.add),
                      waits=[t_trf, tok_mix_free] if (mi == 0 and a == 0) else ())
                    E("V", lambda v, p=prv, g=cg1, a=a: v.tensor_scalar(
                        SLh(10, 1), p, MIXCO[:, a, g:g + 1], None, ALU.mult, ALU.bypass))
                    tmix = E("V", lambda v, o=o: v.tensor_tensor(o, SLh(10, 0), SLh(10, 1),
                                                                 ALU.add), inc=True)
                    if n == 0:
                        tmix = E("V", lambda v, o=o, c=cur, g=cg, a=a, mi=mi: v.tensor_scalar(
                            o[:, 0:1], c[:, 0:1], MIXCO[:, a, g:g + 1],
                            MIXCO[:, a, 12 + mi:13 + mi], ALU.mult, ALU.add), inc=True)
            tcarS = E("V", lambda v: v.tensor_copy(CAR1[:], TRF[:, :, 512:513]), inc=True)
            t_mixA = tmix

            for tc in range(NCHUNK):
                def ddump(name, ap, tok, dt2=F32, _n=n, _tc=tc):
                    if not (dbg and _n == 0 and _tc == 0):
                        return None
                    dd = nc.dram_tensor("dbgw_" + name, list(ap.shape), dt2,
                                        kind="ExternalOutput").ap()
                    return E("G", lambda g, a=ap, o=dd: g.dma_start(out=o, in_=a),
                             waits=[tok], inc=True, dma=True)
                ts1 = E("V", lambda v: v.tensor_scalar(r_phc, r_phat, -1.0, -PC_CLAMP,
                                                       ALU.mult, ALU.min),
                        waits=[tok_state, tok_prol_guard], inc=True)
                tok_prol_guard = None
                ts2 = E("V", lambda v: v.tensor_scalar(r_s0, r_phat, -PC_CLAMP, 0.0,
                                                       ALU.add, ALU.min), waits=[ts1], inc=True)
                ts3a = E("S", lambda s: s.activation(r_mu, r_s0, AF.Exp), waits=[ts2], inc=True)
                ts3 = E("V", lambda v: v.tensor_single_scalar(r_mu, r_mu, 1e-38, ALU.add),
                       waits=[ts3a], inc=True)
                ts4a = E("V", lambda v: v.tensor_copy(r_nu, r_nhat),
                         waits=[tok_nu_guard, ts3], inc=True)
                ts4 = E("V", lambda v: v.tensor_tensor(r_nu, r_mu, r_nu, ALU.mult),
                        waits=[ts4a], inc=True)
                tok_nu_guard = None
                tk = None
                for h in range(2):
                    for a in range(8):
                        E("T", lambda t, h=h, a=a, tc=tc: t.matmul(
                            PB[:, 512 * h:512 * (h + 1)],
                            MIX[:, 0, a, 128 * tc:128 * (tc + 1)],
                            wk3[:, a, 512 * h:512 * (h + 1)],
                            start=(a == 0), stop=False),
                            waits=[t_mixA, t_wk, tok_PB_free] if (h == 0 and a == 0) else ())
                    tk = E("T", lambda t, h=h: t.matmul(
                        PB[:, 512 * h:512 * (h + 1)], ONES1[:],
                        r_phc[:, 512 * h:512 * (h + 1)], start=False, stop=True),
                        waits=[ts1] if h == 0 else (), inc=(h == 1))
                tz = E("V", lambda v: v.tensor_tensor(SL(1), PB[:], IOTW[:], ALU.subtract),
                       waits=[tk, tok_sl_free[1]], inc=True)
                tok_PB_free = tz
                tvp = None
                for h in range(2):
                    for a in range(8):
                        tvp = E("T", lambda t, h=h, a=a, tc=tc: t.matmul(
                            PA[:, 512 * h:512 * (h + 1)],
                            MIX[:, 1, a, 128 * tc:128 * (tc + 1)],
                            wv3[:, a, 512 * h:512 * (h + 1)],
                            start=(a == 0), stop=(a == 7)),
                            waits=[t_mixA, t_wv, tok_PA_free] if (h == 0 and a == 0) else (),
                            inc=(h == 1 and a == 7))
                tg = E("S", lambda s: s.activation(SL(2), SL(1), AF.Exp),
                       waits=[tz, tok_sl_free[2]], inc=True)
                tgb = E("S", lambda s: s.copy(GB[:, 0:1024], SL(2)),
                        waits=[tg, tok_gb_free, tok_gb_free2], inc=True)
                tvf = E("S", lambda s: s.copy(SL(3), PA[:]), waits=[tvp, tok_sl_free[3]],
                        inc=True)
                tok_PA_free = tvf
                trp = None
                for h in range(2):
                    for a in range(8):
                        trp = E("T", lambda t, h=h, a=a, tc=tc: t.matmul(
                            PA[:, 512 * h:512 * (h + 1)],
                            MIX[:, 2, a, 128 * tc:128 * (tc + 1)],
                            wr3[:, a, 512 * h:512 * (h + 1)],
                            start=(a == 0), stop=(a == 7)),
                            waits=[tvf, t_wr] if (h == 0 and a == 0) else (),
                            inc=(h == 1 and a == 7))
                tF = None
                for h in range(2):
                    E("T", lambda t, h=h: t.matmul(
                        PB[:, 512 * h:512 * (h + 1)], RB[:], GB[:, 512 * h:512 * (h + 1)],
                        start=True, stop=False), waits=[tgb] if h == 0 else ())
                    tF = E("T", lambda t, h=h: t.matmul(
                        PB[:, 512 * h:512 * (h + 1)], ONES1[:],
                        r_mu[:, 512 * h:512 * (h + 1)], start=False, stop=True),
                        waits=[ts3] if h == 0 else (), inc=(h == 1))

                ter = E("S", lambda s: s.activation(SL(4), PA[:], AF.Exp, scale=-1.0),
                        waits=[trp, tok_sl_free[4]], inc=True)
                tok_PA_free = ter
                tFE = E("V", lambda v: v.tensor_tensor(SL(5), PB[:], GB[:, 0:1024],
                                                       ALU.subtract),
                        waits=[tF, tok_sl_free[5]], inc=True)
                tok_PB_free = tFE
                tt3 = E("S", lambda s: s.activation(SL(6), SL(5), AF.Ln),
                        waits=[tFE, tok_sl_free[6]], inc=True)
                tcpF = E("S", lambda s: s.copy(SL10B, PB[96:128, :]),
                         waits=[tF, tok_sl10_free] + ([t_mixA] if tc == 0 else []),
                         inc=True)
                tdmaF = E("D", lambda d: d.dma_start(out=r_s0, in_=SL10R),
                          waits=[tcpF, tok_state], inc=True, dma=True)
                tst6 = E("S", lambda s: s.activation(r_lnt2, r_s0, AF.Ln),
                         waits=[tdmaF], inc=True)
                tst5b = E("S", lambda s: s.activation(r_s1, r_lnt2, AF.Exp, scale=-1.0),
                         waits=[tst6], inc=True)
                td = E("V", lambda v: v.tensor_tensor(SL(7), SL(6), SL(1), ALU.subtract),
                       waits=[tt3, tok_sl_free[7]], inc=True)
                tad = E("S", lambda s: s.activation(SL(6), SL(7), AF.Abs),
                        waits=[td], inc=True)
                ted = E("S", lambda s: s.activation(SL(7), SL(6), AF.Exp, scale=-1.0),
                        waits=[tad], inc=True)
                tp1 = E("V", lambda v: v.tensor_single_scalar(SL(7), SL(7), 1.0, ALU.add),
                        waits=[ted], inc=True)
                tgam = E("S", lambda s: s.activation(SL(6), SL(7), AF.Ln), waits=[tp1], inc=True)
                tst7a = E("V", lambda v: v.tensor_single_scalar(r_phat32, r_phc, -1.0,
                                                                ALU.mult),
                          waits=[tst6], inc=True)
                tst7b = E("V", lambda v: v.tensor_tensor(r_phat32, r_phat32, r_Lw, ALU.add),
                         waits=[tst7a], inc=True)
                tst8 = E("V", lambda v: v.tensor_tensor(r_phat, r_phat32, r_lnt2, ALU.add),
                        waits=[tst7b], inc=True)
                tQ = None
                for h in range(2):
                    tQ = E("T", lambda t, h=h: t.matmul(
                        PB[:, 512 * h:512 * (h + 1)], RF[:], SLh(6, h), start=True, stop=True),
                        waits=[tgam, tFE, tdmaF] if h == 0 else (), inc=(h == 1))
                tqp = E("V", lambda v: v.tensor_tensor(SL(8), PB[:], SL(6), ALU.subtract),
                        waits=[tQ, tok_sl_free[8]], inc=True)
                tEQ = E("S", lambda s: s.activation(SL(9), SL(8), AF.Exp),
                        waits=[tqp, tok_sl_free[9]], inc=True)
                tcpQ = E("S", lambda s: s.copy(SL10B, PB[96:128, :]),
                         waits=[tQ, tdmaF], inc=True)
                tdmaQ = E("D", lambda d: d.dma_start(out=r_s0, in_=SL10R),
                          waits=[tcpQ, tst6], inc=True, dma=True)
                tst10 = E("S", lambda s: s.activation(r_s2, r_s0, AF.Exp),
                          waits=[tdmaQ], inc=True)
                tok_PB_free = tcpQ
                dz = ddump("z", SL(1), tz)
                dg = ddump("g", SL(2), tg)
                dfe = ddump("fe", SL(5), tFE)
                dgam = ddump("gam", SL(6), tgam)
                dqp = ddump("qp", SL(8), tqp)
                tzq = E("V", lambda v: v.tensor_tensor(SL(1), SL(1), SL(8), ALU.subtract),
                        waits=[tqp, dz], inc=True)
                tgt = E("S", lambda s: s.activation(SL(0), SL(1), AF.Exp),
                        waits=[tzq, tok_sl_free[0]], inc=True)
                tgtv = E("V", lambda v: v.tensor_tensor(GB[:, 1024:2048], SL(0), SL(3),
                                                        ALU.mult),
                         waits=[tgt, tvf, tok_gb_free, tok_gb_free2], inc=True)
                tP = None
                for h in range(2):
                    E("T", lambda t, h=h: t.matmul(
                        PC[:, 512 * h:512 * (h + 1)], RB[:],
                        GB[:, 1024 + 512 * h:1024 + 512 * (h + 1)], start=True, stop=False),
                      waits=[tgtv, tok_PC_free, tok_PC_free2] if h == 0 else ())
                    tP = E("T", lambda t, h=h: t.matmul(
                        PC[:, 512 * h:512 * (h + 1)], ONES1[:],
                        r_nu[:, 512 * h:512 * (h + 1)], start=False, stop=True),
                        waits=[ts4] if h == 0 else (), inc=(h == 1))
                deq = ddump("eq", SL(9), tEQ)
                dgtv = ddump("gtv", GB[:, 1024:2048], tgtv, BF16)
                ten = E("V", lambda v: v.tensor_tensor(SL(7), EU[:], SL(2), ALU.mult),
                        waits=[tgam], inc=True)
                tden = E("V", lambda v: v.tensor_tensor(SL(5), SL(5), SL(7), ALU.add),
                         waits=[ten, tt3, dfe], inc=True)
                te1p = E("V", lambda v: v.tensor_single_scalar(SL(4), SL(4), 1.0, ALU.add),
                         waits=[ter], inc=True)
                tden2 = E("V", lambda v: v.tensor_tensor(SL(5), SL(5), SL(4), ALU.mult),
                          inc=True)
                dden2 = ddump("den2", SL(5), tden2)
                tln5 = E("S", lambda s: s.activation(SL(4), SL(5), AF.Ln),
                         waits=[tden2, dqp, dden2], inc=True)
                trec = E("S", lambda s: s.activation(SL(8), SL(4), AF.Exp, scale=-1.0),
                         waits=[tln5, tEQ], inc=True)
                tn1 = E("V", lambda v: v.tensor_tensor(SL(6), PC[:], GB[:, 1024:2048],
                                                       ALU.subtract),
                       waits=[tP, dgam], inc=True)
                tok_PC_free2 = tn1
                tn2 = E("V", lambda v: v.tensor_tensor(SL(6), SL(6), SL(9), ALU.mult),
                        inc=True)
                tdn = E("V", lambda v: v.tensor_tensor(SL(7), SL(7), SL(3), ALU.mult),
                       waits=[tden], inc=True)
                tnum = E("V", lambda v: v.tensor_tensor(SL(6), SL(6), SL(7), ALU.add),
                         inc=True)
                dnum = ddump("num", SL(6), tnum)
                tyz = E("V", lambda v: v.tensor_tensor(SL(6), SL(6), SL(8), ALU.mult),
                        waits=[trec, dnum], inc=True)
                tx1 = E("V", lambda v, tc=tc: v.tensor_tensor(
                    H1[:, tc, :], H1[:, tc, :], SL(6), ALU.add),
                    waits=[t_h1[tc]], inc=True)
                tcpP = E("V", lambda v: v.tensor_copy(SL10B, PC[96:128, :]),
                         waits=[tP, tdmaQ], inc=True)
                tdmaP = E("D", lambda d: d.dma_start(out=r_s0, in_=SL10R),
                          waits=[tcpP, tst10], inc=True, dma=True)
                tst12 = E("V", lambda v: v.tensor_tensor(r_s0, r_s0, r_s2, ALU.mult),
                         waits=[tdmaP, tst8], inc=True)
                tok_PC_free = tdmaP
                tst13 = E("V", lambda v: v.tensor_tensor(r_nhat, r_s0, r_s1, ALU.mult),
                         waits=[tst12], inc=True)
                tok_state = tst13
                tok_sl10_free = tdmaP
                tok_sl_free[0] = tgtv; tok_sl_free[1] = tgt; tok_sl_free[2] = tdn
                tok_sl_free[3] = tdn; tok_sl_free[4] = trec; tok_sl_free[5] = tln5
                tok_sl_free[6] = tyz; tok_sl_free[7] = tnum; tok_sl_free[8] = tyz
                tok_sl_free[9] = tn2
                tok_gb_free = tn1; tok_gb_free2 = tP
                if dbg:
                    E("G", lambda g, n=n, tc=tc: g.dma_start(
                        out=dbg_r["x1"][n][tc], in_=H1[:, tc, :]),
                      waits=[tx1], inc=True, dma=True)
            t_x1 = K.tok("V")
            tok_mix_free = K.tok("T")
            tok_poolw_free = K.tok("T")

            if stop_stage == 'x1':
                stage_dump([t_x1, K.tok("S")])
                tok_x0_free = K.tok("G"); tok_h1_free = K.tok("G")
                tok_trf_free = t_trf
                continue

            tcar2 = E("V", lambda v: v.tensor_copy(TRF[:, :, 0:1], CAR2[:]),
                      waits=[tcarS], inc=True)
            for tc in range(4):
                x1c = H1[:, tc, :]
                st, rstd, nmr = ln_stats(x1c, 2, [t_x1])
                tx = E("S", lambda s, a=x1c, r=rstd, nm=nmr: s.activation(
                    TMS[:], a, AF.Identity, scale=r, bias=nm), waits=st, inc=True)
                te = transpose_to_fm(TMS[:], TRF[:, :, 1 + 128 * tc:1 + 128 * (tc + 1)],
                                     [tx, tcar2], tok_PT_free)
                tok_PT_free = te
            t_trf2 = tok_PT_free
            tmixb = None
            for mi, (cg, cg1) in enumerate([(7, 8), (9, 10)]):
                for a in range(8):
                    cur = TRF[:, a, 1:513]; prv = TRF[:, a, 0:512]
                    o = MIX[:, mi, a, :]
                    E("V", lambda v, c=cur, g=cg, a=a: v.tensor_scalar(
                        SLh(10, 0), c, MIXCO[:, a, g:g + 1], MIXCO[:, a, 11:12],
                        ALU.mult, ALU.add),
                      waits=[t_trf2, tok_mix_free, tok_sl10_free]
                      if (mi == 0 and a == 0) else ())
                    E("V", lambda v, p=prv, g=cg1, a=a: v.tensor_scalar(
                        SLh(10, 1), p, MIXCO[:, a, g:g + 1], None, ALU.mult, ALU.bypass))
                    tmixb = E("V", lambda v, o=o: v.tensor_tensor(o, SLh(10, 0), SLh(10, 1),
                                                                  ALU.add), inc=True)
                    if n == 0:
                        tmixb = E("V", lambda v, o=o, c=cur, g=cg, a=a, mi=mi: v.tensor_scalar(
                            o[:, 0:1], c[:, 0:1], MIXCO[:, a, g:g + 1],
                            MIXCO[:, a, 15 + mi:16 + mi], ALU.mult, ALU.add), inc=True)
            tcarS2 = E("V", lambda v: v.tensor_copy(CAR2[:], TRF[:, :, 512:513]), inc=True)
            t_mixB = tmixb

            wcr_sb = POOLW[:, 0:8192]
            t_wcr = load_w(wcr_sb, "WcrT", [tok_poolw_free])
            wcr3 = wcr_sb.rearrange("p (a k) -> p a k", a=8)
            wckh_sb = POOLW[:, 8192:24576]
            ECR = SAb[:, 4:6, :].rearrange("p a k -> p (a k)")
            tecr = []
            for tc in range(4):
                trc = None
                for h in range(2):
                    for a in range(8):
                        trc = E("T", lambda t, h=h, a=a, tc=tc: t.matmul(
                            PB[:, 512 * h:512 * (h + 1)],
                            MIX[:, 1, a, 128 * tc:128 * (tc + 1)],
                            wcr3[:, a, 512 * h:512 * (h + 1)],
                            start=(a == 0), stop=(a == 7)),
                            waits=[t_mixB, t_wcr, tok_PB_free] if (h == 0 and a == 0) else (),
                            inc=(h == 1 and a == 7))
                t_e = E("S", lambda s, tc=tc: s.activation(
                    ECR[:, 1024 * tc:1024 * (tc + 1)], PB[:], AF.Exp, scale=-1.0),
                    waits=[trc] + ([tok_sl_free[4], tok_sl_free[5]] if tc == 0 else []),
                    inc=True)
                tok_PB_free = t_e
                tecr.append(t_e)
            KC = KCA[:, :]
            t_kc = None
            for wh in range(2):
                twl = E("D", lambda d, wh=wh: d.dma_start(
                    out=wckh_sb, in_=wscr["WckT"][:, 16384 * wh:16384 * (wh + 1)]),
                    waits=[tok_poolw_free if wh == 0 else t_kc], inc=True, dma=True)
                wck4 = wckh_sb.rearrange("p (hs a k) -> p hs a k", hs=16, a=8)
                for hs in range(16):
                    hsg = 16 * wh + hs
                    tkp = None
                    for a in range(8):
                        tkp = E("T", lambda t, hs=hs, a=a, w4=wck4: t.matmul(
                            PA[:, 0:512], w4[:, hs, a, :], MIX[:, 0, a, :],
                            start=(a == 0), stop=(a == 7)),
                            waits=([twl, t_mixB, tok_PA_free] +
                                   ([tok_sl_free[0], tok_sl_free[1], tok_sl_free[2],
                                     tok_sl_free[3], tok_sl_free[6], tok_sl_free[7]]
                                    if (wh == 0 and hs == 0) else []))
                            if a == 0 else (), inc=(a == 7))
                    tr1 = E("V", lambda v: v.tensor_single_scalar(
                        SLh(10, 0), PA[:, 0:512], 0.0, ALU.max),
                        waits=[tkp, t_kc], inc=True)
                    t_kc = E("G", lambda g, hsg=hsg: g.tensor_tensor(
                        KC[:, 512 * hsg:512 * (hsg + 1)], SLh(10, 0), SLh(10, 0), ALU.mult),
                        waits=[tr1], inc=True)
                    tok_PA_free = t_kc

            if dbg and n == 0:
                for _nm, _ap, _tok, _dt in [("xck", MIX[:, 0, :, :], t_mixB, BF16),
                                            ("xcr", MIX[:, 1, :, :], t_mixB, BF16),
                                            ("kc", KC, t_kc, BF16),
                                            ("ecr", ECR, tecr[3], BF16)]:
                    _dd = nc.dram_tensor("dbgw_" + _nm, list(_ap.shape), _dt,
                                         kind="ExternalOutput").ap()
                    E("G", lambda g, a=_ap, o=_dd: g.dma_start(out=o, in_=a),
                      waits=[_tok], inc=True, dma=True)
            wcv_sb = POOLW[:, 0:16384]
            kcv = KC.rearrange("p (g k) -> p g k", g=32)
            tep_last = None
            tat_prev = None
            tsv_prev = None
            t_sv = {}
            for dh in range(2):
                t_wcv = E("D", lambda d, dh=dh: d.dma_start(
                    out=wcv_sb, in_=wscr["WcvT"][:, 16384 * dh:16384 * (dh + 1)]),
                    waits=[K.tok("T")], inc=True, dma=True)
                wcv3 = wcv_sb.rearrange("p (a k) -> p a k", a=32)
                for tc in range(4):
                    PX = (PA if tc % 2 == 0 else PB)
                    col = 512 * (tc // 2)
                    tvc = None
                    for hs in range(32):
                        tvc = E("T", lambda t, hs=hs, tc=tc, PX=PX, col=col, w3=wcv3: t.matmul(
                            PX[:, col:col + 512], kcv[:, hs, 128 * tc:128 * (tc + 1)],
                            w3[:, hs, :], start=(hs == 0), stop=(hs == 31)),
                            waits=([t_wcv, t_kc, t_sv.get((dh - 1, tc)),
                                    t_sv.get((dh, tc - 2))] +
                                   ([tok_PB_free] if (dh == 0 and tc == 1) else []))
                            if hs == 0 else (),
                            inc=(hs == 31))
                    tep = E("V", lambda v, tc=tc, dh=dh: v.tensor_single_scalar(
                        SLh(6, 0), ECR[:, 1024 * tc + 512 * dh:1024 * tc + 512 * dh + 512],
                        1.0, ALU.add),
                        waits=[tecr[tc], tvc, tat_prev] +
                              ([tok_sl_free[6]] if (dh == 0 and tc == 0) else []),
                        inc=True)
                    tln2 = E("S", lambda s: s.activation(SLh(7, 0), SLh(6, 0), AF.Ln),
                             waits=[tep, tsv_prev] +
                                   ([tok_sl_free[7]] if (dh == 0 and tc == 0) else []),
                             inc=True)
                    trc2 = E("S", lambda s: s.activation(SLh(7, 1), SLh(7, 0), AF.Exp,
                                                         scale=-1.0),
                             waits=[tln2], inc=True)
                    tsv = E("V", lambda v, PX=PX, col=col: v.tensor_tensor(
                        SLh(6, 0), PX[:, col:col + 512], SLh(7, 1), ALU.mult),
                        waits=[trc2], inc=True)
                    tsv_prev = tsv
                    tat = E("V", lambda v, tc=tc, dh=dh: v.tensor_tensor(
                        H1[:, tc, 512 * dh:512 * dh + 512], H1[:, tc, 512 * dh:512 * dh + 512],
                        SLh(6, 0), ALU.add), waits=[tsv], inc=True)
                    tat_prev = tat
                    t_sv[(dh, tc)] = tsv
                    tep_last = tat
                    if dbg:
                        E("G", lambda g, n=n, tc=tc: g.dma_start(
                            out=dbg_r["attn"][n][tc], in_=H1[:, tc, :]),
                          waits=[tat], inc=True, dma=True)
            tok_PB_free = tep_last; tok_PA_free = tep_last
            t_attn = K.tok("V")
            tok_sl_free[4] = tep_last; tok_sl_free[5] = tep_last
            tok_sl_free[6] = tep_last; tok_sl_free[7] = tep_last
            tok_poolw_free = K.tok("T")

            if stop_stage == 'attn':
                stage_dump([t_attn, K.tok("S")])
                tok_x0_free = K.tok("G"); tok_h1_free = K.tok("G")
                tok_trf_free = t_trf
                continue

            prev_c = None
            for tc in range(4):
                ac = H1[:, tc, :]
                t_x = E("D", lambda d, n=n, tc=tc: d.dma_start(out=XC[:], in_=x_r2[n][tc]),
                        waits=[t_xc, prev_c], inc=True, dma=True)
                st, rstd, nmr = ln_stats(ac, 3, [t_attn, prev_c])
                tnm = E("S", lambda s, a=ac, r=rstd, nm=nmr: s.activation(
                    SL(10), a, AF.Identity, scale=r, bias=nm), waits=st, inc=True)
                tx2 = E("V", lambda v, tc=tc: v.tensor_tensor(H1[:, tc, :], XC[:], SL(10),
                                                              ALU.add),
                        waits=[tnm, t_x], inc=True)
                t_xc = tx2
                st2, rstd2, nmr2 = ln_stats(ac, 4, [tx2])
                th2 = E("S", lambda s, x=ac, r=rstd2, nm=nmr2: s.activation(
                    TMS[:], x, AF.Identity, scale=r, bias=nm), waits=st2, inc=True)
                te = transpose_to_fm(TMS[:], MIX[:, 0, :, 128 * tc:128 * (tc + 1)],
                                     [th2, K.tok("T")], tok_PT_free)
                tok_PT_free = te
                prev_c = tx2
                if dbg:
                    E("G", lambda g, n=n, tc=tc: g.dma_start(
                        out=dbg_r["x2"][n][tc], in_=H1[:, tc, :]),
                      waits=[tx2], inc=True, dma=True)
            t_h2f = tok_PT_free
            t_x2 = K.tok("V")

            if stop_stage == 'x2':
                stage_dump([t_x2, K.tok("S"), K.tok("T")])
                tok_x0_free = K.tok("G"); tok_h1_free = K.tok("G")
                tok_trf_free = t_trf
                continue

            u1_sb = POOLW[:, 0:2048]; v1_sb = POOLW[:, 2048:10240]
            u2_sb = POOLW[:, 10240:18432]; v2_sb = POOLW[:, 18432:20480]
            t_u1 = load_w(u1_sb, "U1T", [tok_poolw_free])
            t_v1 = load_w(v1_sb, "V1T", [])
            t_u2 = load_w(u2_sb, "U2T", [])
            t_v2 = load_w(v2_sb, "V2T", [])
            PBUF = PQB[:, :].rearrange("p (a k) -> p a k", a=2)
            QBUF = PBUF
            u13 = u1_sb.rearrange("p (a k) -> p a k", a=8)
            v13 = v1_sb.rearrange("p (a k) -> p a k", a=2)
            u23 = u2_sb.rearrange("p (a k) -> p a k", a=32)
            v23 = v2_sb.rearrange("p (a k) -> p a k", a=2)
            for rs in range(2):
                for a in range(8):
                    E("T", lambda t, rs=rs, a=a: t.matmul(
                        PA[:, 512 * rs:512 * (rs + 1)], u13[:, a, 128 * rs:128 * (rs + 1)],
                        MIX[:, 0, a, :], start=(a == 0), stop=(a == 7)),
                        waits=[t_u1, t_h2f, tok_PA_free] if (rs == 0 and a == 0) else (),
                        inc=(rs == 1 and a == 7))
            tpb = E("S", lambda s: s.copy(PQB[:, :], PA[:]),
                    waits=[K.tok("T")], inc=True)
            tok_PA_free = tpb
            if stop_stage == 'ffp':
                stage_dump([tpb, K.tok("S"), K.tok("T")])
                tok_x0_free = K.tok("G"); tok_h1_free = K.tok("G")
                tok_trf_free = t_trf; tok_nu_guard = K.tok("T")
                continue
            HB = KC
            t_hh = None
            tsg_p = None
            tr2 = None
            for fs in range(32):
                th = None
                for rs in range(2):
                    th = E("T", lambda t, fs=fs, rs=rs: t.matmul(
                        PB[:, 0:512], v13[:, rs, 128 * fs:128 * (fs + 1)], PBUF[:, rs, :],
                        start=(rs == 0), stop=(rs == 1)),
                        waits=([t_v1, tpb, tok_PB_free] if fs == 0 else [tok_PB_free])
                        if rs == 0 else (),
                        inc=(rs == 1))
                tug = E("V", lambda v, fs=fs: v.tensor_scalar(
                    SLh(10, 0), PB[:, 0:512], FB1[:, fs:fs + 1], None, ALU.add, ALU.bypass),
                    waits=[th], inc=True)
                tok_PB_free = tug
                E("V", lambda v: v.tensor_tensor(SLh(10, 1), SLh(10, 0), SLh(10, 0), ALU.mult))
                E("V", lambda v: v.tensor_scalar(SLh(10, 1), SLh(10, 1), GC1, GC2,
                                                 ALU.mult, ALU.add))
                t3 = E("V", lambda v: v.tensor_tensor(SLh(10, 1), SLh(10, 1), SLh(10, 0),
                                                      ALU.mult), inc=True)
                tsg = E("S", lambda s: s.activation(SLh(9, 0), SLh(10, 1), AF.Exp,
                                                    scale=-2.0),
                        waits=[t3] + ([tok_sl_free[9]] if fs == 0 else []), inc=True)
                te1 = E("V", lambda v: v.tensor_single_scalar(SLh(9, 0), SLh(9, 0), 1.0,
                                                              ALU.add), waits=[tsg], inc=True)
                tln9 = E("S", lambda s: s.activation(SLh(9, 1), SLh(9, 0), AF.Ln),
                         waits=[te1], inc=True)
                tr2 = E("S", lambda s: s.activation(SLh(9, 0), SLh(9, 1), AF.Exp,
                                                    scale=-1.0), waits=[tln9], inc=True)
                t_hh = E("V", lambda v, fs=fs: v.tensor_tensor(
                    HB[:, 512 * fs:512 * (fs + 1)], SLh(10, 0), SLh(9, 0), ALU.mult),
                    waits=([t_kc] if fs == 0 else []) + [tr2], inc=True)
            if stop_stage == 'ffh':
                stage_dump([t_hh, K.tok("S"), K.tok("T")])
                tok_x0_free = K.tok("G"); tok_h1_free = K.tok("G")
                tok_trf_free = t_trf; tok_nu_guard = K.tok("T")
                continue
            for rs in range(2):
                for fs in range(32):
                    E("T", lambda t, rs=rs, fs=fs: t.matmul(
                        PA[:, 512 * rs:512 * (rs + 1)], u23[:, fs, 128 * rs:128 * (rs + 1)],
                        HB[:, 512 * fs:512 * (fs + 1)], start=(fs == 0), stop=(fs == 31)),
                        waits=[t_u2, t_hh, tok_PA_free] if (rs == 0 and fs == 0) else (),
                        inc=(rs == 1 and fs == 31))
            tqb = E("S", lambda s: s.copy(PQB[:, :], PA[:]),
                    waits=[K.tok("T")], inc=True)
            tok_PA_free = tqb
            if stop_stage == 'ffq':
                stage_dump([tqb, K.tok("S"), K.tok("T")])
                tok_x0_free = K.tok("G"); tok_h1_free = K.tok("G")
                tok_trf_free = t_trf; tok_nu_guard = K.tok("T")
                continue
            t_fb2 = E("D", lambda d: d.dma_start(out=r_fb2, in_=fb2_d),
                      waits=[tok_mix_free, tok_state], inc=True, dma=True)
            prev_e = None
            for tc in range(4):
                tf = None
                for h in range(2):
                    for rs in range(2):
                        E("T", lambda t, h=h, rs=rs, tc=tc: t.matmul(
                            PB[:, 512 * h:512 * (h + 1)],
                            QBUF[:, rs, 128 * tc:128 * (tc + 1)],
                            v23[:, rs, 512 * h:512 * (h + 1)], start=(rs == 0), stop=False),
                            waits=[t_v2, tqb, tok_PB_free, t_fb2]
                            if (h == 0 and rs == 0) else ())
                    tf = E("T", lambda t, h=h: t.matmul(
                        PB[:, 512 * h:512 * (h + 1)], ONES1[:],
                        r_fb2[:, 512 * h:512 * (h + 1)], start=False, stop=True),
                        inc=(h == 1))
                if stop_stage == 'ffe1':
                    tok_PB_free = E("V", lambda v: v.tensor_copy(SL(10), PB[:]),
                                    waits=[tf], inc=True)
                    prev_e = tok_PB_free
                    continue
                st, rstd, nmr = ln_stats(PB[:], 5, [tf, prev_e], psum_src=True)
                if stop_stage == 'ffe2':
                    tok_PB_free = st[1]
                    prev_e = st[1]
                    continue
                tnm = E("S", lambda s, r=rstd, nm=nmr: s.activation(
                    SL(10), PB[:], AF.Identity, scale=r, bias=nm), waits=st, inc=True)
                tok_PB_free = tnm
                if stop_stage == 'ffe3':
                    prev_e = tnm
                    continue
                tout = E("G", lambda g, tc=tc: g.tensor_tensor(
                    H1[:, tc, :], H1[:, tc, :], SL(10), ALU.add),
                    waits=[tnm], inc=True)
                prev_e = tout
                E("G", lambda g, n=n, tc=tc: g.dma_start(out=out_r[n][tc], in_=H1[:, tc, :]),
                  inc=True, dma=True)
            if stop_stage in ('ffe1', 'ffe2', 'ffe3'):
                stage_dump([K.tok("S"), K.tok("V"), K.tok("T")])
            tok_x0_free = K.tok("G")
            tok_h1_free = K.tok("G")
            tok_nu_guard = K.tok("T")
            for i in range(10):
                tok_sl_free[i] = tqb
            tok_poolw_free = K.tok("T")
            t_cast_done = None

        E("G", lambda g: g.wait_ge(sem["G"], K.cnt["G"]))
        K.check()

        waited = {e: {} for e in Sched.ENGS}

        def replay(e, eng):
            for waits, fn, inc, amt, sm in K.q[e]:
                for we, wv in waits:
                    if waited[e].get(we, 0) < wv:
                        eng.wait_ge(sem[we], wv)
                        waited[e][we] = wv
                inst = fn(eng)
                if inc:
                    inst.then_inc(sem[sm], amt)

        @blk.tensor
        def _(tensor):
            replay("T", tensor)

        @blk.vector
        def _(vector):
            replay("V", vector)

        @blk.scalar
        def _(scalar):
            replay("S", scalar)

        @blk.sync
        def _(sync):
            replay("D", sync)

        @blk.gpsimd
        def _(gpsimd):
            replay("G", gpsimd)

    return nc


def make_inputs(inp, b, T):
    p = {k[8:]: np.asarray(v, np.float32) for k, v in inp.items()
         if k.startswith("L0_attn_") and k != "L0_attn_x"}
    f = {k[7:]: np.asarray(v, np.float32) for k, v in inp.items() if k.startswith("L0_ffn_")}
    x = np.asarray(inp["L0_attn_x"][b], np.float32)[:T]
    wck = np.ascontiguousarray(
        p["Wck"].T.reshape(8, 128, 32, 128).transpose(1, 2, 0, 3).reshape(128, 32768))
    d = {
        "x": np.ascontiguousarray(x),
        "WkT": _lhsT(p["Wk"].T, 8).reshape(128, -1),
        "WvT": _lhsT(p["Wv"].T, 8).reshape(128, -1),
        "WrT": _lhsT(p["Wr"].T, 8).reshape(128, -1),
        "WcrT": _lhsT(p["Wcr"].T, 8).reshape(128, -1),
        "WckT": wck,
        "WcvT": np.ascontiguousarray(
            p["Wcv"].T.astype(np.float32).reshape(32, 128, 2, 512)
            .transpose(1, 2, 0, 3).reshape(128, 32768)),
        "U1T": _lhsT(f["U1"], 8).reshape(128, -1),
        "V1T": _lhsT(f["V1"], 2).reshape(128, -1),
        "U2T": _lhsT(f["U2"], 32).reshape(128, -1),
        "V2T": _lhsT(f["V2"], 2).reshape(128, -1),
        "fmrows": np.ascontiguousarray(np.stack(
            [_fm(p["rk"]), _fm(p["rv"]), _fm(p["rr"]), _fm(p["crk"]), _fm(p["crr"]),
             _fm(p["ln1_g"]), _fm(p["ln1_b"]), _fm(p["ln2_g"]), _fm(p["ln2_b"])], axis=2)),
        "fb1fm": np.ascontiguousarray(np.asarray(f["b1"], np.float32).reshape(32, 128).T),
        "wrow": np.asarray(p["w"], np.float32).reshape(1, -1).copy(),
        "urow": np.asarray(p["u"], np.float32).reshape(1, -1).copy(),
        "fb2row": np.asarray(f["b2"], np.float32).reshape(1, -1).copy(),
        "Rf32": np.triu(np.ones((128, 128), np.float32)),
        "Rbf": np.triu(np.ones((128, 128))).astype(BF),
        "identbf": np.eye(128).astype(BF),
        "iotacol": np.arange(128, dtype=np.float32).reshape(-1, 1),
        "ones1": np.ones((1, 128), np.float32),
        "onescb": np.ones((128, 1)).astype(BF),
        "onescf": np.ones((128, 1), np.float32),
    }
    return d


def kernel(**inputs):
    T = 8192
    nc = build_program(T)
    in_maps = [make_inputs(inputs, b, T) for b in range(8)]
    res = run_bass_kernel_spmd(nc, in_maps, core_ids=list(range(8)))
    return np.stack([res.results[b]["out"] for b in range(8)], axis=0)


# version 12
# speedup vs baseline: 1.1376x; 1.0288x over previous
import numpy as np
import ml_dtypes
import concourse.bass as bass
import concourse.mybir as mybir
from concourse.bass_utils import run_bass_kernel_spmd
from contextlib import ExitStack

F32 = mybir.dt.float32
BF16 = mybir.dt.bfloat16
AF = mybir.ActivationFunctionType
ALU = mybir.AluOpType
BF = ml_dtypes.bfloat16

C = 1024
L = 128
TT_TILE = 512
NCHUNK = TT_TILE // L
PC_CLAMP = -15.0
GC1 = 0.044715 * 0.7978845608028654
GC2 = 0.7978845608028654


def _lhsT(Wcd, ng):
    return np.ascontiguousarray(
        np.asarray(Wcd, np.float32).reshape(ng, 128, -1).transpose(1, 0, 2))


def _fm(v):
    return np.ascontiguousarray(np.asarray(v, np.float32).reshape(8, 128).T)


class Sched:
    ENGS = "TVSDG"
    SEMS = "TVSDGP"

    def __init__(self, serialize=False):
        self.q = {e: [] for e in self.ENGS}
        self.cnt = {e: 0 for e in self.SEMS}
        self.serialize = serialize
        self.last = None

    def emit(self, e, fn, waits=(), inc=False, dma=False):
        w = [t for t in waits if t is not None]
        if self.serialize:
            inc = True
            if self.last is not None:
                w = w + [self.last]
        sem = e
        if e == "G" and not dma:
            sem = "P"
        amt = 16 if dma else 1
        self.q[e].append((w, fn, inc, amt, sem))
        if inc:
            self.cnt[sem] += amt
            tok = (sem, self.cnt[sem])
            if self.serialize:
                self.last = tok
            return tok
        return None

    def tok(self, e):
        return (e, self.cnt[e])

    def check(self):
        ptr = {e: 0 for e in self.ENGS}
        done = {s: 0 for s in self.SEMS}
        progress = True
        while progress:
            progress = False
            for e in self.ENGS:
                while ptr[e] < len(self.q[e]):
                    waits, _, inc, amt, sem = self.q[e][ptr[e]]
                    if any(done[we] < wv for we, wv in waits):
                        break
                    if inc:
                        done[sem] += amt
                    ptr[e] += 1
                    progress = True
        stuck = {e: (ptr[e], len(self.q[e])) for e in self.ENGS if ptr[e] < len(self.q[e])}
        if stuck:
            msgs = []
            for e in stuck:
                waits, _, _, _, _ = self.q[e][ptr[e]]
                unmet = [(we, wv, done[we]) for we, wv in waits if done[we] < wv]
                msgs.append(f"{e}@{ptr[e]}/{len(self.q[e])} needs {unmet}")
            raise RuntimeError("deadlock: " + "; ".join(msgs))


def build_program(T, dbg=False, serialize=False, stop_stage='out'):
    nt = T // TT_TILE
    nc = bass.Bass("TRN2", target_bir_lowering=False, detect_race_conditions=False)

    def din(name, shape, dtype=mybir.dt.float32):
        return nc.dram_tensor(name, list(shape), dtype, kind="ExternalInput").ap()

    x_d = din("x", [T, C])
    out_d = nc.dram_tensor("out", [T, C], mybir.dt.float32, kind="ExternalOutput").ap()
    wk_d = din("WkT", [128, 8192]); wv_d = din("WvT", [128, 8192])
    wr_d = din("WrT", [128, 8192]); wcr_d = din("WcrT", [128, 8192])
    wck_d = din("WckT", [128, 32768]); wcv_d = din("WcvT", [128, 32768])
    u1_d = din("U1T", [128, 2048]); v1_d = din("V1T", [128, 8192])
    u2_d = din("U2T", [128, 8192]); v2_d = din("V2T", [128, 2048])
    fmr_d = din("fmrows", [128, 8, 9])
    fb1_d = din("fb1fm", [128, 32])
    wrow_d = din("wrow", [1, 1024]); urow_d = din("urow", [1, 1024])
    fb2_d = din("fb2row", [1, 1024])
    Rf_d = din("Rf32", [128, 128]); Rb_d = din("Rbf", [128, 128], BF16)
    id_d = din("identbf", [128, 128], BF16)
    iota_d = din("iotacol", [128, 1]); ones1_d = din("ones1", [1, 128])
    onescb_d = din("onescb", [128, 1], BF16); onescf_d = din("onescf", [128, 1])
    wscr = {}
    for nm, w in [("WkT", 8192), ("WvT", 8192), ("WrT", 8192), ("WcrT", 8192),
                  ("WckT", 32768), ("WcvT", 32768), ("U1T", 2048), ("V1T", 8192),
                  ("U2T", 8192), ("V2T", 2048)]:
        wscr[nm] = nc.dram_tensor("s_" + nm, [128, w], BF16).ap()
    wraw = {"WkT": wk_d, "WvT": wv_d, "WrT": wr_d, "WcrT": wcr_d, "WckT": wck_d,
            "WcvT": wcv_d, "U1T": u1_d, "V1T": v1_d, "U2T": u2_d, "V2T": v2_d}
    dbg_d = {}
    if dbg:
        for nm in ["h1", "x1", "attn", "x2"]:
            dbg_d[nm] = nc.dram_tensor("dbg_" + nm, [T, C], mybir.dt.float32,
                                       kind="ExternalOutput").ap()

    es = ExitStack()
    with es:
        sb = lambda n, s, d=F32: es.enter_context(nc.sbuf_tensor(n, list(s), d))
        pst = lambda n, s, d=F32: es.enter_context(nc.psum_tensor(n, list(s), d))
        POOLW = sb("poolw", [128, 24576], BF16)
        SA = sb("sarena", [128, 11, 1024])
        SAb = SA.bitcast(BF16)
        GB = sb("gbpair", [128, 2048], BF16)
        MIX = sb("mix", [128, 3, 8, 512], BF16)
        TRF = sb("trfm", [128, 8, 513], BF16)
        TMS = sb("tmstage", [128, 1024], BF16)
        XC = sb("xc", [128, 1024])
        H1 = sb("h1t", [128, 4, 1024])
        KCA = sb("kcarena", [128, 16384], BF16)
        PQB = sb("pqbuf", [128, 1024], BF16)
        ROWA = sb("rowa", [128, 1024]); ROWB = sb("rowb", [128, 1024])
        ROWC = sb("rowc", [128, 1024])
        STAT = sb("stat", [128, 64])
        IOTW = sb("iotw", [128, 1024]); EU = sb("eu", [128, 1024])
        RF = sb("rf", [128, 128]); RB = sb("rb", [128, 128], BF16)
        IDB = sb("idb", [128, 128], BF16)
        ONES1 = sb("onesone", [1, 128]); IOTAC = sb("iotac", [128, 1])
        ONESCB = sb("onescbs", [128, 1], BF16); ONESCF = sb("onescfs", [128, 1])
        MIXCO = sb("mixco", [128, 8, 17])
        FB1 = sb("fb1", [128, 32])
        CAR1 = sb("car1", [128, 8, 1]); CAR2 = sb("car2", [128, 8, 1])
        CONSTS = sb("consts", [128, 3])
        nc.const_aps.register(F32, 1e-5, CONSTS[:, 0:1])
        nc.const_aps.register(F32, 1e-38, CONSTS[:, 1:2])
        nc.const_aps.register(F32, 1.0, CONSTS[:, 2:3])
        PA = pst("PA", [128, 1024]); PB = pst("PB", [128, 1024])
        PT = pst("PT", [128, 1024], BF16)
        PC = pst("PC", [128, 1024])
        sem = {e: es.enter_context(nc.semaphore("sem" + e)) for e in Sched.SEMS}
        blk = es.enter_context(nc.Block())

        K = Sched(serialize=serialize)
        E = K.emit
        FMR = SA[:, 10, 0:72].rearrange("p (a k) -> p a k", a=8)
        r_phc = ROWA[0:1, :]; r_phat = ROWA[32:33, :]; r_s0 = ROWA[64:65, :]
        r_mu = ROWB[0:1, :]; r_nhat = ROWB[32:33, :]; r_s1 = ROWB[64:65, :]
        r_nu = ROWC[0:1, :]; r_Lw = ROWC[32:33, :]; r_s2 = ROWC[64:65, :]
        r_wu = ROWA[0:1, :]; r_fb2 = ROWC[0:1, :]; r_s3 = ROWB[64:65, :]
        r_phat32 = ROWA[32:33, :]; r_lnt2 = ROWB[32:33, :]
        SL = lambda i: SA[:, i, :]
        SL10B = SA[96:128, 10, :]
        SL10R = SA[127:128, 10, :]
        SLh = lambda i, h: SA[:, i, 512 * h:512 * (h + 1)]

        ldt = []
        for dst, src in [(RF[:], Rf_d), (RB[:], Rb_d), (IDB[:], id_d),
                         (ONES1[:], ones1_d), (IOTAC[:], iota_d),
                         (ONESCB[:], onescb_d), (ONESCF[:], onescf_d),
                         (FMR, fmr_d), (FB1[:], fb1_d),
                         (r_wu, wrow_d), (r_s3, urow_d)]:
            ldt.append(E("D", lambda d, o=dst, s=src: d.dma_start(out=o, in_=s),
                         inc=True, dma=True))
        tld = ldt[-1]
        t_lw = E("S", lambda s: s.activation(r_Lw, r_wu, AF.Identity, scale=float(L)),
                 waits=[tld], inc=True)
        t_eur = E("S", lambda s: s.activation(r_mu, r_s3, AF.Exp), inc=True)
        E("T", lambda t: t.matmul(PA[:, 0:512], ONES1[:], r_mu[:, 0:512],
                                  start=True, stop=True), waits=[t_eur])
        t_mm1 = E("T", lambda t: t.matmul(PA[:, 512:1024], ONES1[:], r_mu[:, 512:1024],
                                          start=True, stop=True), inc=True)
        E("T", lambda t: t.matmul(PB[:, 0:512], ONES1[:], r_wu[:, 0:512],
                                  start=True, stop=True), waits=[tld])
        t_mm2 = E("T", lambda t: t.matmul(PB[:, 512:1024], ONES1[:], r_wu[:, 512:1024],
                                          start=True, stop=True), inc=True)
        t_eu = E("V", lambda v: v.tensor_copy(EU[:], PA[:]), waits=[t_mm1], inc=True)
        t_iotw = E("V", lambda v: v.tensor_scalar(IOTW[:], PB[:], IOTAC[:, 0:1], None,
                                                  ALU.mult, ALU.bypass),
                   waits=[t_mm2], inc=True)
        fv = lambda i: FMR[:, :, i]
        mco = lambda i: MIXCO[:, :, i]
        tpm = None
        for dsti, ai, bi in [(0, 0, 5), (2, 1, 5), (4, 2, 5), (7, 3, 7), (9, 4, 7)]:
            tpm = E("V", lambda v, d=dsti, a=ai, b=bi: v.tensor_tensor(mco(d), fv(a), fv(b),
                                                                       ALU.mult),
                    waits=[tld], inc=True)
        for dsti, gi, pi in [(1, 5, 0), (3, 5, 2), (5, 5, 4), (8, 7, 7), (10, 7, 9)]:
            tpm = E("V", lambda v, d=dsti, g=gi, p=pi: v.tensor_tensor(mco(d), fv(g), mco(p),
                                                                       ALU.subtract),
                    waits=[tpm], inc=True)
        E("V", lambda v: v.tensor_copy(mco(6), fv(6)))
        E("V", lambda v: v.tensor_copy(mco(11), fv(8)))
        for dsti, ri, bi in [(12, 0, 6), (13, 1, 6), (14, 2, 6), (15, 3, 8), (16, 4, 8)]:
            tpm = E("V", lambda v, d=dsti, r=ri, b=bi: v.tensor_tensor(mco(d), fv(r), fv(b),
                                                                       ALU.mult),
                    waits=[tpm], inc=True)
        E("V", lambda v: v.memset(CONSTS[:, 0:1], 1e-5))
        E("V", lambda v: v.memset(CONSTS[:, 1:2], 1e-38))
        E("V", lambda v: v.memset(CONSTS[:, 2:3], 1.0))
        E("V", lambda v: v.memset(r_phat, -10000.0))
        E("V", lambda v: v.memset(r_nhat, 0.0))
        E("V", lambda v: v.memset(CAR1[:], 0.0))
        t_init = E("V", lambda v: v.memset(CAR2[:], 0.0), inc=True)

        jobs = []
        for nm in ["WkT", "WvT", "WrT", "WcrT", "WckT", "WcvT", "U1T", "V1T", "U2T", "V2T"]:
            tot = wraw[nm].shape[1]
            for c0 in range(0, tot, 2048):
                jobs.append((wraw[nm][:, c0:c0 + 2048], wscr[nm][:, c0:c0 + 2048]))
        prev_out = [None, None]
        t_cast_done = None
        for i, (srcap, dstap) in enumerate(jobs):
            b = i % 2
            stg = SA[:, 2 * b:2 * b + 2, :].rearrange("p a k -> p (a k)")
            tin = E("D", lambda d, s=srcap, g=stg: d.dma_start(out=g, in_=s),
                    waits=[prev_out[b]], inc=True, dma=True)
            tout = E("G", lambda g, s=stg, dd=dstap: g.dma_start(out=dd, in_=s),
                     waits=[tin], inc=True, dma=True)
            prev_out[b] = tout
            t_cast_done = tout

        last_ln = [t_init]

        def ln_stats(src_ap, idx, tok_in, psum_src=False):
            sm = STAT[:, 4 * idx:4 * idx + 1]
            sq = STAT[:, 4 * idx + 2:4 * idx + 3]
            tsq = E("S", lambda s, a=src_ap: s.activation(SL(10), a, AF.Square),
                    waits=list(tok_in) + [last_ln[0]], inc=True)
            t1 = E("V", lambda v, a=src_ap, o=sm: v.reduce_sum(o, a, axis=mybir.AxisListType.X),
                   waits=list(tok_in) + [last_ln[0]] + ([tsq] if psum_src else []), inc=True)
            t2 = E("V", lambda v, o=sq: v.reduce_sum(o, SL(10), axis=mybir.AxisListType.X),
                   waits=[tsq], inc=True)
            m = STAT[:, 32:33]; v2 = STAT[:, 36:37]; rstd = STAT[:, 40:41]
            nmr = STAT[:, 44:45]; tmp = STAT[:, 48:49]; tmp2 = STAT[:, 52:53]
            ta = E("V", lambda v: v.tensor_single_scalar(m, sm, 1.0 / 1024.0, ALU.mult),
                   waits=[t1, t2], inc=True)
            tb = E("V", lambda v: v.tensor_single_scalar(v2, sq, 1.0 / 1024.0, ALU.mult),
                   waits=[ta], inc=True)
            tc2 = E("V", lambda v: v.tensor_tensor(tmp, m, m, ALU.mult), waits=[tb], inc=True)
            td2 = E("V", lambda v: v.tensor_tensor(v2, v2, tmp, ALU.subtract),
                    waits=[tc2], inc=True)
            tv = E("V", lambda v: v.tensor_single_scalar(v2, v2, 1e-5, ALU.add),
                   waits=[td2], inc=True)
            E("S", lambda s: s.activation(tmp2, v2, AF.Ln), waits=[tv])
            tr = E("S", lambda s: s.activation(rstd, tmp2, AF.Exp, scale=-0.5), inc=True)
            te2 = E("V", lambda v: v.tensor_tensor(tmp, m, rstd, ALU.mult), waits=[tr], inc=True)
            tn = E("V", lambda v: v.tensor_single_scalar(nmr, tmp, -1.0, ALU.mult),
                   waits=[te2], inc=True)
            last_ln[0] = tn
            return [tr, tn], rstd, nmr

        def transpose_to_fm(tm_bf_ap, dst_fm_ap, waits, pt_free):
            tt = None
            for a in range(8):
                tt = E("T", lambda t, a=a, s=tm_bf_ap: t.transpose(
                    PT[:, 128 * a:128 * (a + 1)], s[:, 128 * a:128 * (a + 1)], IDB[:]),
                    waits=(list(waits) + [pt_free]) if a == 0 else (), inc=(a == 7))
            return E("S", lambda s, d=dst_fm_ap: s.copy(
                d, PT[:, :].rearrange("p (a t) -> p a t", a=8)), waits=[tt], inc=True)

        tok_PA_free = t_eu; tok_PB_free = t_iotw
        tok_PT_free = K.tok("V")
        tok_PC_free = None
        tok_PC_free2 = None
        tok_sl10_free = None
        tok_x0_free = None
        tok_h1_free = None
        tok_nu_guard = None
        tok_prol_guard = t_eur
        tok_mix_free = None
        tok_state = t_init
        tok_trf_free = None
        tok_gb_free = None
        tok_gb_free2 = None
        tok_sl_free = {i: None for i in range(11)}
        tok_poolw_free = None

        x_r2 = x_d.rearrange("(n tc p) c -> n tc p c", p=128, tc=4)
        out_r = out_d.rearrange("(n tc p) c -> n tc p c", p=128, tc=4)
        dbg_r = {k2: v2.rearrange("(n tc p) c -> n tc p c", p=128, tc=4)
                 for k2, v2 in dbg_d.items()}

        def load_w(dst2d, nm, waits):
            return E("D", lambda d, o=dst2d, s=wscr[nm]: d.dma_start(out=o, in_=s),
                     waits=waits, inc=True, dma=True)

        for n in range(nt):
            wk_sb = POOLW[:, 0:8192]; wv_sb = POOLW[:, 8192:16384]; wr_sb = POOLW[:, 16384:24576]
            t_wk = load_w(wk_sb, "WkT", [tok_poolw_free, t_cast_done])
            t_wv = load_w(wv_sb, "WvT", [])
            t_wr = load_w(wr_sb, "WrT", [])
            wk3 = wk_sb.rearrange("p (a k) -> p a k", a=8)
            wv3 = wv_sb.rearrange("p (a k) -> p a k", a=8)
            wr3 = wr_sb.rearrange("p (a k) -> p a k", a=8)

            tcar = E("V", lambda v: v.tensor_copy(TRF[:, :, 0:1], CAR1[:]),
                     waits=[tok_trf_free, t_init], inc=True)
            t_h1 = []
            t_xc = None
            for tc in range(4):
                x0c = XC[:, :]; h1c = H1[:, tc, :]
                t_x = E("D", lambda d, n=n, tc=tc: d.dma_start(out=XC[:], in_=x_r2[n][tc]),
                        waits=[t_xc, tok_x0_free] + ([t_cast_done] if tc == 0 else []),
                        inc=True, dma=True)
                st, rstd, nmr = ln_stats(x0c, 0, [t_x])
                th = E("S", lambda s, a=x0c, o=h1c, r=rstd, nm=nmr: s.activation(
                    o, a, AF.Identity, scale=r, bias=nm), waits=st + [tok_h1_free], inc=True)
                t_xc = th
                if stop_stage == 'h1x':
                    t_h1.append(th)
                    t_xc = th
                    continue
                st2, rstd2, nmr2 = ln_stats(h1c, 1, [th])
                tx = E("S", lambda s, a=h1c, r=rstd2, nm=nmr2: s.activation(
                    TMS[:], a, AF.Identity, scale=r, bias=nm), waits=st2, inc=True)
                te = transpose_to_fm(TMS[:], TRF[:, :, 1 + 128 * tc:1 + 128 * (tc + 1)],
                                     [tx, tcar], tok_PT_free)
                tok_PT_free = te
                t_h1.append(th)
                if dbg:
                    E("G", lambda g, n=n, tc=tc, a=h1c: g.dma_start(
                        out=dbg_r["h1"][n][tc], in_=a), waits=[th], inc=True, dma=True)
            t_trf = tok_PT_free

            def stage_dump(waits):
                tt = None
                for tc2 in range(4):
                    tt = E("G", lambda g, n=n, tc2=tc2: g.dma_start(out=out_r[n][tc2],
                                                                   in_=H1[:, tc2, :]),
                           waits=list(waits), inc=True, dma=True)
                return tt

            if stop_stage in ('h1', 'h1x'):
                stage_dump([K.tok("S"), K.tok("V")])
                tok_x0_free = K.tok("G"); tok_h1_free = K.tok("G")
                tok_poolw_free = K.tok("T"); tok_mix_free = K.tok("T")
                tok_trf_free = t_trf
                continue

            tmix = None
            for mi, (cg, cg1) in enumerate([(0, 1), (2, 3), (4, 5)]):
                for a in range(8):
                    cur = TRF[:, a, 1:513]; prv = TRF[:, a, 0:512]
                    o = MIX[:, mi, a, :]
                    E("V", lambda v, c=cur, g=cg, a=a: v.tensor_scalar(
                        SLh(10, 0), c, MIXCO[:, a, g:g + 1], MIXCO[:, a, 6:7],
                        ALU.mult, ALU.add),
                      waits=[t_trf, tok_mix_free] if (mi == 0 and a == 0) else ())
                    E("V", lambda v, p=prv, g=cg1, a=a: v.tensor_scalar(
                        SLh(10, 1), p, MIXCO[:, a, g:g + 1], None, ALU.mult, ALU.bypass))
                    tmix = E("V", lambda v, o=o: v.tensor_tensor(o, SLh(10, 0), SLh(10, 1),
                                                                 ALU.add), inc=True)
                    if n == 0:
                        tmix = E("V", lambda v, o=o, c=cur, g=cg, a=a, mi=mi: v.tensor_scalar(
                            o[:, 0:1], c[:, 0:1], MIXCO[:, a, g:g + 1],
                            MIXCO[:, a, 12 + mi:13 + mi], ALU.mult, ALU.add), inc=True)
            tcarS = E("V", lambda v: v.tensor_copy(CAR1[:], TRF[:, :, 512:513]), inc=True)
            t_mixA = tmix

            for tc in range(NCHUNK):
                def ddump(name, ap, tok, dt2=F32, _n=n, _tc=tc):
                    if not (dbg and _n == 0 and _tc == 0):
                        return None
                    dd = nc.dram_tensor("dbgw_" + name, list(ap.shape), dt2,
                                        kind="ExternalOutput").ap()
                    return E("G", lambda g, a=ap, o=dd: g.dma_start(out=o, in_=a),
                             waits=[tok], inc=True, dma=True)
                ts1 = E("V", lambda v: v.tensor_scalar(r_phc, r_phat, -1.0, -PC_CLAMP,
                                                       ALU.mult, ALU.min),
                        waits=[tok_state, tok_prol_guard], inc=True)
                tok_prol_guard = None
                ts2 = E("V", lambda v: v.tensor_scalar(r_s0, r_phat, -PC_CLAMP, 0.0,
                                                       ALU.add, ALU.min), waits=[ts1], inc=True)
                ts3a = E("S", lambda s: s.activation(r_mu, r_s0, AF.Exp), waits=[ts2], inc=True)
                ts3 = E("V", lambda v: v.tensor_single_scalar(r_mu, r_mu, 1e-38, ALU.add),
                       waits=[ts3a], inc=True)
                ts4a = E("V", lambda v: v.tensor_copy(r_nu, r_nhat),
                         waits=[tok_nu_guard, ts3], inc=True)
                ts4 = E("V", lambda v: v.tensor_tensor(r_nu, r_mu, r_nu, ALU.mult),
                        waits=[ts4a], inc=True)
                tok_nu_guard = None
                tk = None
                for h in range(2):
                    for a in range(8):
                        E("T", lambda t, h=h, a=a, tc=tc: t.matmul(
                            PB[:, 512 * h:512 * (h + 1)],
                            MIX[:, 0, a, 128 * tc:128 * (tc + 1)],
                            wk3[:, a, 512 * h:512 * (h + 1)],
                            start=(a == 0), stop=False),
                            waits=[t_mixA, t_wk, tok_PB_free] if (h == 0 and a == 0) else ())
                    tk = E("T", lambda t, h=h: t.matmul(
                        PB[:, 512 * h:512 * (h + 1)], ONES1[:],
                        r_phc[:, 512 * h:512 * (h + 1)], start=False, stop=True),
                        waits=[ts1] if h == 0 else (), inc=(h == 1))
                tz = E("V", lambda v: v.tensor_tensor(SL(1), PB[:], IOTW[:], ALU.subtract),
                       waits=[tk, tok_sl_free[1]], inc=True)
                tok_PB_free = tz
                tvp = None
                for h in range(2):
                    for a in range(8):
                        tvp = E("T", lambda t, h=h, a=a, tc=tc: t.matmul(
                            PA[:, 512 * h:512 * (h + 1)],
                            MIX[:, 1, a, 128 * tc:128 * (tc + 1)],
                            wv3[:, a, 512 * h:512 * (h + 1)],
                            start=(a == 0), stop=(a == 7)),
                            waits=[t_mixA, t_wv, tok_PA_free] if (h == 0 and a == 0) else (),
                            inc=(h == 1 and a == 7))
                tg = E("S", lambda s: s.activation(SL(2), SL(1), AF.Exp),
                       waits=[tz, tok_sl_free[2]], inc=True)
                tgb = E("S", lambda s: s.copy(GB[:, 0:1024], SL(2)),
                        waits=[tg, tok_gb_free, tok_gb_free2], inc=True)
                tvf = E("S", lambda s: s.copy(SL(3), PA[:]), waits=[tvp, tok_sl_free[3]],
                        inc=True)
                tok_PA_free = tvf
                trp = None
                for h in range(2):
                    for a in range(8):
                        trp = E("T", lambda t, h=h, a=a, tc=tc: t.matmul(
                            PA[:, 512 * h:512 * (h + 1)],
                            MIX[:, 2, a, 128 * tc:128 * (tc + 1)],
                            wr3[:, a, 512 * h:512 * (h + 1)],
                            start=(a == 0), stop=(a == 7)),
                            waits=[tvf, t_wr] if (h == 0 and a == 0) else (),
                            inc=(h == 1 and a == 7))
                tF = None
                for h in range(2):
                    E("T", lambda t, h=h: t.matmul(
                        PB[:, 512 * h:512 * (h + 1)], RB[:], GB[:, 512 * h:512 * (h + 1)],
                        start=True, stop=False), waits=[tgb] if h == 0 else ())
                    tF = E("T", lambda t, h=h: t.matmul(
                        PB[:, 512 * h:512 * (h + 1)], ONES1[:],
                        r_mu[:, 512 * h:512 * (h + 1)], start=False, stop=True),
                        waits=[ts3] if h == 0 else (), inc=(h == 1))

                ter = E("S", lambda s: s.activation(SL(4), PA[:], AF.Exp, scale=-1.0),
                        waits=[trp, tok_sl_free[4]], inc=True)
                tok_PA_free = ter
                tFE = E("V", lambda v: v.tensor_tensor(SL(5), PB[:], GB[:, 0:1024],
                                                       ALU.subtract),
                        waits=[tF, tok_sl_free[5]], inc=True)
                tok_PB_free = tFE
                tt3 = E("S", lambda s: s.activation(SL(6), SL(5), AF.Ln),
                        waits=[tFE, tok_sl_free[6]], inc=True)
                tcpF = E("S", lambda s: s.copy(SL10B, PB[96:128, :]),
                         waits=[tF, tok_sl10_free] + ([t_mixA] if tc == 0 else []),
                         inc=True)
                tdmaF = E("D", lambda d: d.dma_start(out=r_s0, in_=SL10R),
                          waits=[tcpF, tok_state], inc=True, dma=True)
                tst6 = E("S", lambda s: s.activation(r_lnt2, r_s0, AF.Ln),
                         waits=[tdmaF], inc=True)
                tst5b = E("S", lambda s: s.activation(r_s1, r_lnt2, AF.Exp, scale=-1.0),
                         waits=[tst6], inc=True)
                td = E("V", lambda v: v.tensor_tensor(SL(7), SL(6), SL(1), ALU.subtract),
                       waits=[tt3, tok_sl_free[7]], inc=True)
                tad = E("S", lambda s: s.activation(SL(6), SL(7), AF.Abs),
                        waits=[td], inc=True)
                ted = E("S", lambda s: s.activation(SL(7), SL(6), AF.Exp, scale=-1.0),
                        waits=[tad], inc=True)
                tgam = E("S", lambda s: s.activation(SL(6), SL(7), AF.Ln,
                                                     bias=CONSTS[:, 2:3]),
                         waits=[ted], inc=True)
                tst7a = E("V", lambda v: v.tensor_single_scalar(r_phat32, r_phc, -1.0,
                                                                ALU.mult),
                          waits=[tst6], inc=True)
                tst7b = E("V", lambda v: v.tensor_tensor(r_phat32, r_phat32, r_Lw, ALU.add),
                         waits=[tst7a], inc=True)
                tst8 = E("V", lambda v: v.tensor_tensor(r_phat, r_phat32, r_lnt2, ALU.add),
                        waits=[tst7b], inc=True)
                tQ = None
                for h in range(2):
                    tQ = E("T", lambda t, h=h: t.matmul(
                        PB[:, 512 * h:512 * (h + 1)], RF[:], SLh(6, h), start=True, stop=True),
                        waits=[tgam, tFE, tdmaF] if h == 0 else (), inc=(h == 1))
                tqp = E("V", lambda v: v.tensor_tensor(SL(8), PB[:], SL(6), ALU.subtract),
                        waits=[tQ, tok_sl_free[8]], inc=True)
                tEQ = E("S", lambda s: s.activation(SL(9), SL(8), AF.Exp),
                        waits=[tqp, tok_sl_free[9]], inc=True)
                tcpQ = E("S", lambda s: s.copy(SL10B, PB[96:128, :]),
                         waits=[tQ, tdmaF], inc=True)
                tdmaQ = E("D", lambda d: d.dma_start(out=r_s0, in_=SL10R),
                          waits=[tcpQ, tst6], inc=True, dma=True)
                tst10 = E("S", lambda s: s.activation(r_s2, r_s0, AF.Exp),
                          waits=[tdmaQ], inc=True)
                tok_PB_free = tcpQ
                dz = ddump("z", SL(1), tz)
                dg = ddump("g", SL(2), tg)
                dfe = ddump("fe", SL(5), tFE)
                dgam = ddump("gam", SL(6), tgam)
                dqp = ddump("qp", SL(8), tqp)
                tzq = E("V", lambda v: v.tensor_tensor(SL(1), SL(1), SL(8), ALU.subtract),
                        waits=[tqp, dz], inc=True)
                tgt = E("S", lambda s: s.activation(SL(0), SL(1), AF.Exp),
                        waits=[tzq, tok_sl_free[0]], inc=True)
                tgtv = E("V", lambda v: v.tensor_tensor(GB[:, 1024:2048], SL(0), SL(3),
                                                        ALU.mult),
                         waits=[tgt, tvf, tok_gb_free, tok_gb_free2], inc=True)
                tP = None
                for h in range(2):
                    E("T", lambda t, h=h: t.matmul(
                        PC[:, 512 * h:512 * (h + 1)], RB[:],
                        GB[:, 1024 + 512 * h:1024 + 512 * (h + 1)], start=True, stop=False),
                      waits=[tgtv, tok_PC_free, tok_PC_free2] if h == 0 else ())
                    tP = E("T", lambda t, h=h: t.matmul(
                        PC[:, 512 * h:512 * (h + 1)], ONES1[:],
                        r_nu[:, 512 * h:512 * (h + 1)], start=False, stop=True),
                        waits=[ts4] if h == 0 else (), inc=(h == 1))
                deq = ddump("eq", SL(9), tEQ)
                dgtv = ddump("gtv", GB[:, 1024:2048], tgtv, BF16)
                ten = E("V", lambda v: v.tensor_tensor(SL(7), EU[:], SL(2), ALU.mult),
                        waits=[tgam], inc=True)
                tden = E("V", lambda v: v.tensor_tensor(SL(5), SL(5), SL(7), ALU.add),
                         waits=[ten, tt3, dfe], inc=True)
                te1p = E("V", lambda v: v.tensor_single_scalar(SL(4), SL(4), 1.0, ALU.add),
                         waits=[ter], inc=True)
                tden2 = E("V", lambda v: v.tensor_tensor(SL(5), SL(5), SL(4), ALU.mult),
                          inc=True)
                dden2 = ddump("den2", SL(5), tden2)
                tln5 = E("S", lambda s: s.activation(SL(4), SL(5), AF.Ln),
                         waits=[tden2, dqp, dden2], inc=True)
                trec = E("S", lambda s: s.activation(SL(8), SL(4), AF.Exp, scale=-1.0),
                         waits=[tln5, tEQ], inc=True)
                tn1 = E("V", lambda v: v.tensor_tensor(SL(6), PC[:], GB[:, 1024:2048],
                                                       ALU.subtract),
                       waits=[tP, dgam], inc=True)
                tok_PC_free2 = tn1
                tn2 = E("V", lambda v: v.tensor_tensor(SL(6), SL(6), SL(9), ALU.mult),
                        inc=True)
                tdn = E("V", lambda v: v.tensor_tensor(SL(7), SL(7), SL(3), ALU.mult),
                       waits=[tden], inc=True)
                tnum = E("V", lambda v: v.tensor_tensor(SL(6), SL(6), SL(7), ALU.add),
                         inc=True)
                dnum = ddump("num", SL(6), tnum)
                tyz = E("V", lambda v: v.tensor_tensor(SL(6), SL(6), SL(8), ALU.mult),
                        waits=[trec, dnum], inc=True)
                tx1 = E("V", lambda v, tc=tc: v.tensor_tensor(
                    H1[:, tc, :], H1[:, tc, :], SL(6), ALU.add),
                    waits=[t_h1[tc]], inc=True)
                tcpP = E("V", lambda v: v.tensor_copy(SL10B, PC[96:128, :]),
                         waits=[tP, tdmaQ], inc=True)
                tdmaP = E("D", lambda d: d.dma_start(out=r_s0, in_=SL10R),
                          waits=[tcpP, tst10], inc=True, dma=True)
                tst12 = E("V", lambda v: v.tensor_tensor(r_s0, r_s0, r_s2, ALU.mult),
                         waits=[tdmaP, tst8], inc=True)
                tok_PC_free = tdmaP
                tst13 = E("V", lambda v: v.tensor_tensor(r_nhat, r_s0, r_s1, ALU.mult),
                         waits=[tst12], inc=True)
                tok_state = tst13
                tok_sl10_free = tdmaP
                tok_sl_free[0] = tgtv; tok_sl_free[1] = tgt; tok_sl_free[2] = tdn
                tok_sl_free[3] = tdn; tok_sl_free[4] = trec; tok_sl_free[5] = tln5
                tok_sl_free[6] = tyz; tok_sl_free[7] = tnum; tok_sl_free[8] = tyz
                tok_sl_free[9] = tn2
                tok_gb_free = tn1; tok_gb_free2 = tP
                if dbg:
                    E("G", lambda g, n=n, tc=tc: g.dma_start(
                        out=dbg_r["x1"][n][tc], in_=H1[:, tc, :]),
                      waits=[tx1], inc=True, dma=True)
            t_x1 = K.tok("V")
            tok_mix_free = K.tok("T")
            tok_poolw_free = K.tok("T")

            if stop_stage == 'x1':
                stage_dump([t_x1, K.tok("S")])
                tok_x0_free = K.tok("G"); tok_h1_free = K.tok("G")
                tok_trf_free = t_trf
                continue

            tcar2 = E("V", lambda v: v.tensor_copy(TRF[:, :, 0:1], CAR2[:]),
                      waits=[tcarS], inc=True)
            for tc in range(4):
                x1c = H1[:, tc, :]
                st, rstd, nmr = ln_stats(x1c, 2, [t_x1])
                tx = E("S", lambda s, a=x1c, r=rstd, nm=nmr: s.activation(
                    TMS[:], a, AF.Identity, scale=r, bias=nm), waits=st, inc=True)
                te = transpose_to_fm(TMS[:], TRF[:, :, 1 + 128 * tc:1 + 128 * (tc + 1)],
                                     [tx, tcar2], tok_PT_free)
                tok_PT_free = te
            t_trf2 = tok_PT_free
            tmixb = None
            for mi, (cg, cg1) in enumerate([(7, 8), (9, 10)]):
                for a in range(8):
                    cur = TRF[:, a, 1:513]; prv = TRF[:, a, 0:512]
                    o = MIX[:, mi, a, :]
                    E("V", lambda v, c=cur, g=cg, a=a: v.tensor_scalar(
                        SLh(10, 0), c, MIXCO[:, a, g:g + 1], MIXCO[:, a, 11:12],
                        ALU.mult, ALU.add),
                      waits=[t_trf2, tok_mix_free, tok_sl10_free]
                      if (mi == 0 and a == 0) else ())
                    E("V", lambda v, p=prv, g=cg1, a=a: v.tensor_scalar(
                        SLh(10, 1), p, MIXCO[:, a, g:g + 1], None, ALU.mult, ALU.bypass))
                    tmixb = E("V", lambda v, o=o: v.tensor_tensor(o, SLh(10, 0), SLh(10, 1),
                                                                  ALU.add), inc=True)
                    if n == 0:
                        tmixb = E("V", lambda v, o=o, c=cur, g=cg, a=a, mi=mi: v.tensor_scalar(
                            o[:, 0:1], c[:, 0:1], MIXCO[:, a, g:g + 1],
                            MIXCO[:, a, 15 + mi:16 + mi], ALU.mult, ALU.add), inc=True)
            tcarS2 = E("V", lambda v: v.tensor_copy(CAR2[:], TRF[:, :, 512:513]), inc=True)
            t_mixB = tmixb

            wcr_sb = POOLW[:, 0:8192]
            t_wcr = load_w(wcr_sb, "WcrT", [tok_poolw_free])
            wcr3 = wcr_sb.rearrange("p (a k) -> p a k", a=8)
            wckh_sb = POOLW[:, 8192:24576]
            ECR = SAb[:, 4:6, :].rearrange("p a k -> p (a k)")
            tecr = []
            for tc in range(4):
                trc = None
                for h in range(2):
                    for a in range(8):
                        trc = E("T", lambda t, h=h, a=a, tc=tc: t.matmul(
                            PB[:, 512 * h:512 * (h + 1)],
                            MIX[:, 1, a, 128 * tc:128 * (tc + 1)],
                            wcr3[:, a, 512 * h:512 * (h + 1)],
                            start=(a == 0), stop=(a == 7)),
                            waits=[t_mixB, t_wcr, tok_PB_free] if (h == 0 and a == 0) else (),
                            inc=(h == 1 and a == 7))
                t_e = E("S", lambda s, tc=tc: s.activation(
                    ECR[:, 1024 * tc:1024 * (tc + 1)], PB[:], AF.Exp, scale=-1.0),
                    waits=[trc] + ([tok_sl_free[4], tok_sl_free[5]] if tc == 0 else []),
                    inc=True)
                tok_PB_free = t_e
                tecr.append(t_e)
            KC = KCA[:, :]
            t_kc = None
            for wh in range(2):
                twl = E("D", lambda d, wh=wh: d.dma_start(
                    out=wckh_sb, in_=wscr["WckT"][:, 16384 * wh:16384 * (wh + 1)]),
                    waits=[tok_poolw_free if wh == 0 else t_kc], inc=True, dma=True)
                wck4 = wckh_sb.rearrange("p (hs a k) -> p hs a k", hs=16, a=8)
                for hs in range(16):
                    hsg = 16 * wh + hs
                    tkp = None
                    for a in range(8):
                        tkp = E("T", lambda t, hs=hs, a=a, w4=wck4: t.matmul(
                            PA[:, 0:512], w4[:, hs, a, :], MIX[:, 0, a, :],
                            start=(a == 0), stop=(a == 7)),
                            waits=([twl, t_mixB, tok_PA_free] +
                                   ([tok_sl_free[0], tok_sl_free[1], tok_sl_free[2],
                                     tok_sl_free[3], tok_sl_free[6], tok_sl_free[7]]
                                    if (wh == 0 and hs == 0) else []))
                            if a == 0 else (), inc=(a == 7))
                    tr1 = E("V", lambda v: v.tensor_single_scalar(
                        SLh(10, 0), PA[:, 0:512], 0.0, ALU.max),
                        waits=[tkp, t_kc], inc=True)
                    t_kc = E("G", lambda g, hsg=hsg: g.tensor_tensor(
                        KC[:, 512 * hsg:512 * (hsg + 1)], SLh(10, 0), SLh(10, 0), ALU.mult),
                        waits=[tr1], inc=True)
                    tok_PA_free = t_kc

            if dbg and n == 0:
                for _nm, _ap, _tok, _dt in [("xck", MIX[:, 0, :, :], t_mixB, BF16),
                                            ("xcr", MIX[:, 1, :, :], t_mixB, BF16),
                                            ("kc", KC, t_kc, BF16),
                                            ("ecr", ECR, tecr[3], BF16)]:
                    _dd = nc.dram_tensor("dbgw_" + _nm, list(_ap.shape), _dt,
                                         kind="ExternalOutput").ap()
                    E("G", lambda g, a=_ap, o=_dd: g.dma_start(out=o, in_=a),
                      waits=[_tok], inc=True, dma=True)
            wcv_sb = POOLW[:, 0:16384]
            kcv = KC.rearrange("p (g k) -> p g k", g=32)
            tep_last = None
            tat_prev = None
            tsv_prev = None
            t_sv = {}
            for dh in range(2):
                t_wcv = E("D", lambda d, dh=dh: d.dma_start(
                    out=wcv_sb, in_=wscr["WcvT"][:, 16384 * dh:16384 * (dh + 1)]),
                    waits=[K.tok("T")], inc=True, dma=True)
                wcv3 = wcv_sb.rearrange("p (a k) -> p a k", a=32)
                for tc in range(4):
                    PX = (PA if tc % 2 == 0 else PB)
                    col = 512 * (tc // 2)
                    tvc = None
                    for hs in range(32):
                        tvc = E("T", lambda t, hs=hs, tc=tc, PX=PX, col=col, w3=wcv3: t.matmul(
                            PX[:, col:col + 512], kcv[:, hs, 128 * tc:128 * (tc + 1)],
                            w3[:, hs, :], start=(hs == 0), stop=(hs == 31)),
                            waits=([t_wcv, t_kc, t_sv.get((dh - 1, tc)),
                                    t_sv.get((dh, tc - 2))] +
                                   ([tok_PB_free] if (dh == 0 and tc == 1) else []))
                            if hs == 0 else (),
                            inc=(hs == 31))
                    tln2 = E("S", lambda s, tc=tc, dh=dh: s.activation(
                        SLh(7, 0), ECR[:, 1024 * tc + 512 * dh:1024 * tc + 512 * dh + 512],
                        AF.Ln, bias=CONSTS[:, 2:3]),
                        waits=[tecr[tc], tsv_prev] +
                              ([tok_sl_free[7]] if (dh == 0 and tc == 0) else []),
                        inc=True)
                    trc2 = E("S", lambda s: s.activation(SLh(7, 1), SLh(7, 0), AF.Exp,
                                                         scale=-1.0),
                             waits=[tln2], inc=True)
                    tsv = E("V", lambda v, PX=PX, col=col: v.tensor_tensor(
                        SLh(6, 0), PX[:, col:col + 512], SLh(7, 1), ALU.mult),
                        waits=[trc2, tvc, tat_prev] +
                              ([tok_sl_free[6]] if (dh == 0 and tc == 0) else []),
                        inc=True)
                    tsv_prev = tsv
                    tat = E("V", lambda v, tc=tc, dh=dh: v.tensor_tensor(
                        H1[:, tc, 512 * dh:512 * dh + 512], H1[:, tc, 512 * dh:512 * dh + 512],
                        SLh(6, 0), ALU.add), waits=[tsv], inc=True)
                    tat_prev = tat
                    t_sv[(dh, tc)] = tsv
                    tep_last = tat
                    if dbg:
                        E("G", lambda g, n=n, tc=tc: g.dma_start(
                            out=dbg_r["attn"][n][tc], in_=H1[:, tc, :]),
                          waits=[tat], inc=True, dma=True)
            tok_PB_free = tep_last; tok_PA_free = tep_last
            t_attn = K.tok("V")
            tok_sl_free[4] = tep_last; tok_sl_free[5] = tep_last
            tok_sl_free[6] = tep_last; tok_sl_free[7] = tep_last
            tok_poolw_free = K.tok("T")

            if stop_stage == 'attn':
                stage_dump([t_attn, K.tok("S")])
                tok_x0_free = K.tok("G"); tok_h1_free = K.tok("G")
                tok_trf_free = t_trf
                continue

            prev_c = None
            for tc in range(4):
                ac = H1[:, tc, :]
                t_x = E("D", lambda d, n=n, tc=tc: d.dma_start(out=XC[:], in_=x_r2[n][tc]),
                        waits=[t_xc, prev_c], inc=True, dma=True)
                st, rstd, nmr = ln_stats(ac, 3, [t_attn, prev_c])
                tnm = E("S", lambda s, a=ac, r=rstd, nm=nmr: s.activation(
                    SL(10), a, AF.Identity, scale=r, bias=nm), waits=st, inc=True)
                tx2 = E("V", lambda v, tc=tc: v.tensor_tensor(H1[:, tc, :], XC[:], SL(10),
                                                              ALU.add),
                        waits=[tnm, t_x], inc=True)
                t_xc = tx2
                st2, rstd2, nmr2 = ln_stats(ac, 4, [tx2])
                th2 = E("S", lambda s, x=ac, r=rstd2, nm=nmr2: s.activation(
                    TMS[:], x, AF.Identity, scale=r, bias=nm), waits=st2, inc=True)
                te = transpose_to_fm(TMS[:], MIX[:, 0, :, 128 * tc:128 * (tc + 1)],
                                     [th2, K.tok("T")], tok_PT_free)
                tok_PT_free = te
                prev_c = tx2
                if dbg:
                    E("G", lambda g, n=n, tc=tc: g.dma_start(
                        out=dbg_r["x2"][n][tc], in_=H1[:, tc, :]),
                      waits=[tx2], inc=True, dma=True)
            t_h2f = tok_PT_free
            t_x2 = K.tok("V")

            if stop_stage == 'x2':
                stage_dump([t_x2, K.tok("S"), K.tok("T")])
                tok_x0_free = K.tok("G"); tok_h1_free = K.tok("G")
                tok_trf_free = t_trf
                continue

            u1_sb = POOLW[:, 0:2048]; v1_sb = POOLW[:, 2048:10240]
            u2_sb = POOLW[:, 10240:18432]; v2_sb = POOLW[:, 18432:20480]
            t_u1 = load_w(u1_sb, "U1T", [tok_poolw_free])
            t_v1 = load_w(v1_sb, "V1T", [])
            t_u2 = load_w(u2_sb, "U2T", [])
            t_v2 = load_w(v2_sb, "V2T", [])
            PBUF = PQB[:, :].rearrange("p (a k) -> p a k", a=2)
            QBUF = PBUF
            u13 = u1_sb.rearrange("p (a k) -> p a k", a=8)
            v13 = v1_sb.rearrange("p (a k) -> p a k", a=2)
            u23 = u2_sb.rearrange("p (a k) -> p a k", a=32)
            v23 = v2_sb.rearrange("p (a k) -> p a k", a=2)
            for rs in range(2):
                for a in range(8):
                    E("T", lambda t, rs=rs, a=a: t.matmul(
                        PA[:, 512 * rs:512 * (rs + 1)], u13[:, a, 128 * rs:128 * (rs + 1)],
                        MIX[:, 0, a, :], start=(a == 0), stop=(a == 7)),
                        waits=[t_u1, t_h2f, tok_PA_free] if (rs == 0 and a == 0) else (),
                        inc=(rs == 1 and a == 7))
            tpb = E("S", lambda s: s.copy(PQB[:, :], PA[:]),
                    waits=[K.tok("T")], inc=True)
            tok_PA_free = tpb
            if stop_stage == 'ffp':
                stage_dump([tpb, K.tok("S"), K.tok("T")])
                tok_x0_free = K.tok("G"); tok_h1_free = K.tok("G")
                tok_trf_free = t_trf; tok_nu_guard = K.tok("T")
                continue
            HB = KC
            t_hh = None
            tsg_p = None
            tr2 = None
            for fs in range(32):
                th = None
                for rs in range(2):
                    th = E("T", lambda t, fs=fs, rs=rs: t.matmul(
                        PB[:, 0:512], v13[:, rs, 128 * fs:128 * (fs + 1)], PBUF[:, rs, :],
                        start=(rs == 0), stop=(rs == 1)),
                        waits=([t_v1, tpb, tok_PB_free] if fs == 0 else [tok_PB_free])
                        if rs == 0 else (),
                        inc=(rs == 1))
                tug = E("V", lambda v, fs=fs: v.tensor_scalar(
                    SLh(10, 0), PB[:, 0:512], FB1[:, fs:fs + 1], None, ALU.add, ALU.bypass),
                    waits=[th], inc=True)
                tok_PB_free = tug
                E("V", lambda v: v.tensor_tensor(SLh(10, 1), SLh(10, 0), SLh(10, 0), ALU.mult))
                E("V", lambda v: v.tensor_scalar(SLh(10, 1), SLh(10, 1), GC1, GC2,
                                                 ALU.mult, ALU.add))
                t3 = E("V", lambda v: v.tensor_tensor(SLh(10, 1), SLh(10, 1), SLh(10, 0),
                                                      ALU.mult), inc=True)
                tsg = E("S", lambda s: s.activation(SLh(9, 0), SLh(10, 1), AF.Exp,
                                                    scale=-2.0),
                        waits=[t3] + ([tok_sl_free[9]] if fs == 0 else []), inc=True)
                tln9 = E("S", lambda s: s.activation(SLh(9, 1), SLh(9, 0), AF.Ln,
                                                     bias=CONSTS[:, 2:3]),
                         waits=[tsg], inc=True)
                tr2 = E("S", lambda s: s.activation(SLh(9, 0), SLh(9, 1), AF.Exp,
                                                    scale=-1.0), waits=[tln9], inc=True)
                t_hh = E("V", lambda v, fs=fs: v.tensor_tensor(
                    HB[:, 512 * fs:512 * (fs + 1)], SLh(10, 0), SLh(9, 0), ALU.mult),
                    waits=([t_kc] if fs == 0 else []) + [tr2], inc=True)
            if stop_stage == 'ffh':
                stage_dump([t_hh, K.tok("S"), K.tok("T")])
                tok_x0_free = K.tok("G"); tok_h1_free = K.tok("G")
                tok_trf_free = t_trf; tok_nu_guard = K.tok("T")
                continue
            for rs in range(2):
                for fs in range(32):
                    E("T", lambda t, rs=rs, fs=fs: t.matmul(
                        PA[:, 512 * rs:512 * (rs + 1)], u23[:, fs, 128 * rs:128 * (rs + 1)],
                        HB[:, 512 * fs:512 * (fs + 1)], start=(fs == 0), stop=(fs == 31)),
                        waits=[t_u2, t_hh, tok_PA_free] if (rs == 0 and fs == 0) else (),
                        inc=(rs == 1 and fs == 31))
            tqb = E("S", lambda s: s.copy(PQB[:, :], PA[:]),
                    waits=[K.tok("T")], inc=True)
            tok_PA_free = tqb
            if stop_stage == 'ffq':
                stage_dump([tqb, K.tok("S"), K.tok("T")])
                tok_x0_free = K.tok("G"); tok_h1_free = K.tok("G")
                tok_trf_free = t_trf; tok_nu_guard = K.tok("T")
                continue
            t_fb2 = E("D", lambda d: d.dma_start(out=r_fb2, in_=fb2_d),
                      waits=[tok_mix_free, tok_state], inc=True, dma=True)
            prev_e = None
            for tc in range(4):
                tf = None
                for h in range(2):
                    for rs in range(2):
                        E("T", lambda t, h=h, rs=rs, tc=tc: t.matmul(
                            PB[:, 512 * h:512 * (h + 1)],
                            QBUF[:, rs, 128 * tc:128 * (tc + 1)],
                            v23[:, rs, 512 * h:512 * (h + 1)], start=(rs == 0), stop=False),
                            waits=[t_v2, tqb, tok_PB_free, t_fb2]
                            if (h == 0 and rs == 0) else ())
                    tf = E("T", lambda t, h=h: t.matmul(
                        PB[:, 512 * h:512 * (h + 1)], ONES1[:],
                        r_fb2[:, 512 * h:512 * (h + 1)], start=False, stop=True),
                        inc=(h == 1))
                if stop_stage == 'ffe1':
                    tok_PB_free = E("V", lambda v: v.tensor_copy(SL(10), PB[:]),
                                    waits=[tf], inc=True)
                    prev_e = tok_PB_free
                    continue
                st, rstd, nmr = ln_stats(PB[:], 5, [tf, prev_e], psum_src=True)
                if stop_stage == 'ffe2':
                    tok_PB_free = st[1]
                    prev_e = st[1]
                    continue
                tnm = E("S", lambda s, r=rstd, nm=nmr: s.activation(
                    SL(10), PB[:], AF.Identity, scale=r, bias=nm), waits=st, inc=True)
                tok_PB_free = tnm
                if stop_stage == 'ffe3':
                    prev_e = tnm
                    continue
                tout = E("G", lambda g, tc=tc: g.tensor_tensor(
                    H1[:, tc, :], H1[:, tc, :], SL(10), ALU.add),
                    waits=[tnm], inc=True)
                prev_e = tout
                E("G", lambda g, n=n, tc=tc: g.dma_start(out=out_r[n][tc], in_=H1[:, tc, :]),
                  inc=True, dma=True)
            if stop_stage in ('ffe1', 'ffe2', 'ffe3'):
                stage_dump([K.tok("S"), K.tok("V"), K.tok("T")])
            tok_x0_free = K.tok("G")
            tok_h1_free = K.tok("G")
            tok_nu_guard = K.tok("T")
            for i in range(10):
                tok_sl_free[i] = tqb
            tok_poolw_free = K.tok("T")
            t_cast_done = None

        E("G", lambda g: g.wait_ge(sem["G"], K.cnt["G"]))
        K.check()

        waited = {e: {} for e in Sched.ENGS}

        def replay(e, eng):
            for waits, fn, inc, amt, sm in K.q[e]:
                for we, wv in waits:
                    if waited[e].get(we, 0) < wv:
                        eng.wait_ge(sem[we], wv)
                        waited[e][we] = wv
                inst = fn(eng)
                if inc:
                    inst.then_inc(sem[sm], amt)

        @blk.tensor
        def _(tensor):
            replay("T", tensor)

        @blk.vector
        def _(vector):
            replay("V", vector)

        @blk.scalar
        def _(scalar):
            replay("S", scalar)

        @blk.sync
        def _(sync):
            replay("D", sync)

        @blk.gpsimd
        def _(gpsimd):
            replay("G", gpsimd)

    return nc


def make_inputs(inp, b, T):
    p = {k[8:]: np.asarray(v, np.float32) for k, v in inp.items()
         if k.startswith("L0_attn_") and k != "L0_attn_x"}
    f = {k[7:]: np.asarray(v, np.float32) for k, v in inp.items() if k.startswith("L0_ffn_")}
    x = np.asarray(inp["L0_attn_x"][b], np.float32)[:T]
    wck = np.ascontiguousarray(
        p["Wck"].T.reshape(8, 128, 32, 128).transpose(1, 2, 0, 3).reshape(128, 32768))
    d = {
        "x": np.ascontiguousarray(x),
        "WkT": _lhsT(p["Wk"].T, 8).reshape(128, -1),
        "WvT": _lhsT(p["Wv"].T, 8).reshape(128, -1),
        "WrT": _lhsT(p["Wr"].T, 8).reshape(128, -1),
        "WcrT": _lhsT(p["Wcr"].T, 8).reshape(128, -1),
        "WckT": wck,
        "WcvT": np.ascontiguousarray(
            p["Wcv"].T.astype(np.float32).reshape(32, 128, 2, 512)
            .transpose(1, 2, 0, 3).reshape(128, 32768)),
        "U1T": _lhsT(f["U1"], 8).reshape(128, -1),
        "V1T": _lhsT(f["V1"], 2).reshape(128, -1),
        "U2T": _lhsT(f["U2"], 32).reshape(128, -1),
        "V2T": _lhsT(f["V2"], 2).reshape(128, -1),
        "fmrows": np.ascontiguousarray(np.stack(
            [_fm(p["rk"]), _fm(p["rv"]), _fm(p["rr"]), _fm(p["crk"]), _fm(p["crr"]),
             _fm(p["ln1_g"]), _fm(p["ln1_b"]), _fm(p["ln2_g"]), _fm(p["ln2_b"])], axis=2)),
        "fb1fm": np.ascontiguousarray(np.asarray(f["b1"], np.float32).reshape(32, 128).T),
        "wrow": np.asarray(p["w"], np.float32).reshape(1, -1).copy(),
        "urow": np.asarray(p["u"], np.float32).reshape(1, -1).copy(),
        "fb2row": np.asarray(f["b2"], np.float32).reshape(1, -1).copy(),
        "Rf32": np.triu(np.ones((128, 128), np.float32)),
        "Rbf": np.triu(np.ones((128, 128))).astype(BF),
        "identbf": np.eye(128).astype(BF),
        "iotacol": np.arange(128, dtype=np.float32).reshape(-1, 1),
        "ones1": np.ones((1, 128), np.float32),
        "onescb": np.ones((128, 1)).astype(BF),
        "onescf": np.ones((128, 1), np.float32),
    }
    return d


def kernel(**inputs):
    T = 8192
    nc = build_program(T)
    in_maps = [make_inputs(inputs, b, T) for b in range(8)]
    res = run_bass_kernel_spmd(nc, in_maps, core_ids=list(range(8)))
    return np.stack([res.results[b]["out"] for b in range(8)], axis=0)


# version 13
# speedup vs baseline: 1.2957x; 1.1390x over previous
import numpy as np
import ml_dtypes
import concourse.bass as bass
import concourse.mybir as mybir
from concourse.bass_utils import run_bass_kernel_spmd
from contextlib import ExitStack

F32 = mybir.dt.float32
BF16 = mybir.dt.bfloat16
AF = mybir.ActivationFunctionType
ALU = mybir.AluOpType
BF = ml_dtypes.bfloat16

C = 1024
L = 128
TT_TILE = 512
NCHUNK = TT_TILE // L
PC_CLAMP = -15.0
GC1 = 0.044715 * 0.7978845608028654
GC2 = 0.7978845608028654


def _lhsT(Wcd, ng):
    return np.ascontiguousarray(
        np.asarray(Wcd, np.float32).reshape(ng, 128, -1).transpose(1, 0, 2))


def _fm(v):
    return np.ascontiguousarray(np.asarray(v, np.float32).reshape(8, 128).T)


class Sched:
    ENGS = "TVSDG"
    SEMS = "TVSDGP"

    def __init__(self, serialize=False):
        self.q = {e: [] for e in self.ENGS}
        self.cnt = {e: 0 for e in self.SEMS}
        self.serialize = serialize
        self.last = None

    def emit(self, e, fn, waits=(), inc=False, dma=False):
        w = [t for t in waits if t is not None]
        if self.serialize:
            inc = True
            if self.last is not None:
                w = w + [self.last]
        sem = e
        if e == "G" and not dma:
            sem = "P"
        amt = 16 if dma else 1
        self.q[e].append((w, fn, inc, amt, sem))
        if inc:
            self.cnt[sem] += amt
            tok = (sem, self.cnt[sem])
            if self.serialize:
                self.last = tok
            return tok
        return None

    def tok(self, e):
        return (e, self.cnt[e])

    def check(self):
        ptr = {e: 0 for e in self.ENGS}
        done = {s: 0 for s in self.SEMS}
        progress = True
        while progress:
            progress = False
            for e in self.ENGS:
                while ptr[e] < len(self.q[e]):
                    waits, _, inc, amt, sem = self.q[e][ptr[e]]
                    if any(done[we] < wv for we, wv in waits):
                        break
                    if inc:
                        done[sem] += amt
                    ptr[e] += 1
                    progress = True
        stuck = {e: (ptr[e], len(self.q[e])) for e in self.ENGS if ptr[e] < len(self.q[e])}
        if stuck:
            msgs = []
            for e in stuck:
                waits, _, _, _, _ = self.q[e][ptr[e]]
                unmet = [(we, wv, done[we]) for we, wv in waits if done[we] < wv]
                msgs.append(f"{e}@{ptr[e]}/{len(self.q[e])} needs {unmet}")
            raise RuntimeError("deadlock: " + "; ".join(msgs))


def build_program(T, dbg=False, serialize=False, stop_stage='out'):
    nt = T // TT_TILE
    nc = bass.Bass("TRN2", target_bir_lowering=False, detect_race_conditions=False)

    def din(name, shape, dtype=mybir.dt.float32):
        return nc.dram_tensor(name, list(shape), dtype, kind="ExternalInput").ap()

    x_d = din("x", [T, C])
    out_d = nc.dram_tensor("out", [T, C], mybir.dt.float32, kind="ExternalOutput").ap()
    wk_d = din("WkT", [128, 8192]); wv_d = din("WvT", [128, 8192])
    wr_d = din("WrT", [128, 8192]); wcr_d = din("WcrT", [128, 8192])
    wck_d = din("WckT", [128, 32768]); wcv_d = din("WcvT", [128, 32768])
    u1_d = din("U1T", [128, 2048]); v1_d = din("V1T", [128, 8192])
    u2_d = din("U2T", [128, 8192]); v2_d = din("V2T", [128, 2048])
    fmr_d = din("fmrows", [128, 8, 9])
    fb1_d = din("fb1fm", [128, 32])
    wrow_d = din("wrow", [1, 1024]); urow_d = din("urow", [1, 1024])
    fb2_d = din("fb2row", [1, 1024])
    Rf_d = din("Rf32", [128, 128]); Rb_d = din("Rbf", [128, 128], BF16)
    id_d = din("identbf", [128, 128], BF16)
    iota_d = din("iotacol", [128, 1]); ones1_d = din("ones1", [1, 128])
    onescb_d = din("onescb", [128, 1], BF16); onescf_d = din("onescf", [128, 1])
    wscr = {}
    for nm, w in [("WkT", 8192), ("WvT", 8192), ("WrT", 8192), ("WcrT", 8192),
                  ("WckT", 32768), ("WcvT", 32768), ("U1T", 2048), ("V1T", 8192),
                  ("U2T", 8192), ("V2T", 2048)]:
        wscr[nm] = nc.dram_tensor("s_" + nm, [128, w], BF16).ap()
    wraw = {"WkT": wk_d, "WvT": wv_d, "WrT": wr_d, "WcrT": wcr_d, "WckT": wck_d,
            "WcvT": wcv_d, "U1T": u1_d, "V1T": v1_d, "U2T": u2_d, "V2T": v2_d}
    dbg_d = {}
    if dbg:
        for nm in ["h1", "x1", "attn", "x2"]:
            dbg_d[nm] = nc.dram_tensor("dbg_" + nm, [T, C], mybir.dt.float32,
                                       kind="ExternalOutput").ap()

    es = ExitStack()
    with es:
        sb = lambda n, s, d=F32: es.enter_context(nc.sbuf_tensor(n, list(s), d))
        pst = lambda n, s, d=F32: es.enter_context(nc.psum_tensor(n, list(s), d))
        POOLW = sb("poolw", [128, 24576], BF16)
        SA = sb("sarena", [128, 11, 1024])
        SAb = SA.bitcast(BF16)
        GB = sb("gbpair", [128, 2048], BF16)
        MIX = sb("mix", [128, 3, 8, 512], BF16)
        TRF = sb("trfm", [128, 8, 513], BF16)
        TMS = sb("tmstage", [128, 1024], BF16)
        XC = sb("xc", [128, 1024])
        H1 = sb("h1t", [128, 4, 1024])
        KCA = sb("kcarena", [128, 16384], BF16)
        PQB = sb("pqbuf", [128, 1024], BF16)
        ROWA = sb("rowa", [128, 1024]); ROWB = sb("rowb", [128, 1024])
        ROWC = sb("rowc", [128, 1024])
        STAT = sb("stat", [128, 64])
        IOTW = sb("iotw", [128, 1024]); EU = sb("eu", [128, 1024])
        RF = sb("rf", [128, 128]); RB = sb("rb", [128, 128], BF16)
        IDB = sb("idb", [128, 128], BF16)
        ONES1 = sb("onesone", [1, 128]); IOTAC = sb("iotac", [128, 1])
        ONESCB = sb("onescbs", [128, 1], BF16); ONESCF = sb("onescfs", [128, 1])
        MIXCO = sb("mixco", [128, 8, 17])
        FB1 = sb("fb1", [128, 32])
        CAR1 = sb("car1", [128, 8, 1]); CAR2 = sb("car2", [128, 8, 1])
        CONSTS = sb("consts", [128, 3])
        nc.const_aps.register(F32, 1e-5, CONSTS[:, 0:1])
        nc.const_aps.register(F32, 1e-38, CONSTS[:, 1:2])
        nc.const_aps.register(F32, 1.0, CONSTS[:, 2:3])
        PA = pst("PA", [128, 1024]); PB = pst("PB", [128, 1024])
        PT = pst("PT", [128, 1024], BF16)
        PC = pst("PC", [128, 1024])
        sem = {e: es.enter_context(nc.semaphore("sem" + e)) for e in Sched.SEMS}
        blk = es.enter_context(nc.Block())

        K = Sched(serialize=serialize)
        E = K.emit
        FMR = SA[:, 10, 0:72].rearrange("p (a k) -> p a k", a=8)
        r_phc = ROWA[0:1, :]; r_phat = ROWA[32:33, :]; r_s0 = ROWA[64:65, :]
        r_mu = ROWB[0:1, :]; r_nhat = ROWB[32:33, :]; r_s1 = ROWB[64:65, :]
        r_nu = ROWC[0:1, :]; r_Lw = ROWC[32:33, :]; r_s2 = ROWC[64:65, :]
        r_wu = ROWA[0:1, :]; r_fb2 = ROWC[0:1, :]; r_s3 = ROWB[64:65, :]
        r_phat32 = ROWA[32:33, :]; r_lnt2 = ROWB[32:33, :]
        SL = lambda i: SA[:, i, :]
        SL10B = SA[96:128, 10, :]
        SL10R = SA[127:128, 10, :]
        SLh = lambda i, h: SA[:, i, 512 * h:512 * (h + 1)]

        ldt = []
        for dst, src in [(RF[:], Rf_d), (RB[:], Rb_d), (IDB[:], id_d),
                         (ONES1[:], ones1_d), (IOTAC[:], iota_d),
                         (ONESCB[:], onescb_d), (ONESCF[:], onescf_d),
                         (FMR, fmr_d), (FB1[:], fb1_d),
                         (r_wu, wrow_d), (r_s3, urow_d)]:
            ldt.append(E("D", lambda d, o=dst, s=src: d.dma_start(out=o, in_=s),
                         inc=True, dma=True))
        tld = ldt[-1]
        t_lw = E("S", lambda s: s.activation(r_Lw, r_wu, AF.Identity, scale=float(L)),
                 waits=[tld], inc=True)
        t_eur = E("S", lambda s: s.activation(r_mu, r_s3, AF.Exp), inc=True)
        E("T", lambda t: t.matmul(PA[:, 0:512], ONES1[:], r_mu[:, 0:512],
                                  start=True, stop=True), waits=[t_eur])
        t_mm1 = E("T", lambda t: t.matmul(PA[:, 512:1024], ONES1[:], r_mu[:, 512:1024],
                                          start=True, stop=True), inc=True)
        E("T", lambda t: t.matmul(PB[:, 0:512], ONES1[:], r_wu[:, 0:512],
                                  start=True, stop=True), waits=[tld])
        t_mm2 = E("T", lambda t: t.matmul(PB[:, 512:1024], ONES1[:], r_wu[:, 512:1024],
                                          start=True, stop=True), inc=True)
        t_eu = E("V", lambda v: v.tensor_copy(EU[:], PA[:]), waits=[t_mm1], inc=True)
        t_iotw = E("V", lambda v: v.tensor_scalar(IOTW[:], PB[:], IOTAC[:, 0:1], None,
                                                  ALU.mult, ALU.bypass),
                   waits=[t_mm2], inc=True)
        fv = lambda i: FMR[:, :, i]
        mco = lambda i: MIXCO[:, :, i]
        tpm = None
        for dsti, ai, bi in [(0, 0, 5), (2, 1, 5), (4, 2, 5), (7, 3, 7), (9, 4, 7)]:
            tpm = E("V", lambda v, d=dsti, a=ai, b=bi: v.tensor_tensor(mco(d), fv(a), fv(b),
                                                                       ALU.mult),
                    waits=[tld], inc=True)
        for dsti, gi, pi in [(1, 5, 0), (3, 5, 2), (5, 5, 4), (8, 7, 7), (10, 7, 9)]:
            tpm = E("V", lambda v, d=dsti, g=gi, p=pi: v.tensor_tensor(mco(d), fv(g), mco(p),
                                                                       ALU.subtract),
                    waits=[tpm], inc=True)
        E("V", lambda v: v.tensor_copy(mco(6), fv(6)))
        E("V", lambda v: v.tensor_copy(mco(11), fv(8)))
        for dsti, ri, bi in [(12, 0, 6), (13, 1, 6), (14, 2, 6), (15, 3, 8), (16, 4, 8)]:
            tpm = E("V", lambda v, d=dsti, r=ri, b=bi: v.tensor_tensor(mco(d), fv(r), fv(b),
                                                                       ALU.mult),
                    waits=[tpm], inc=True)
        E("V", lambda v: v.memset(CONSTS[:, 0:1], 1e-5))
        E("V", lambda v: v.memset(CONSTS[:, 1:2], 1e-38))
        E("V", lambda v: v.memset(CONSTS[:, 2:3], 1.0))
        E("V", lambda v: v.memset(r_phat, -10000.0))
        E("V", lambda v: v.memset(r_nhat, 0.0))
        E("V", lambda v: v.memset(CAR1[:], 0.0))
        t_init = E("V", lambda v: v.memset(CAR2[:], 0.0), inc=True)

        jobs = []
        for nm in ["WkT", "WvT", "WrT", "WcrT", "WckT", "WcvT", "U1T", "V1T", "U2T", "V2T"]:
            tot = wraw[nm].shape[1]
            for c0 in range(0, tot, 2048):
                jobs.append((wraw[nm][:, c0:c0 + 2048], wscr[nm][:, c0:c0 + 2048]))
        prev_out = [None, None]
        t_cast_done = None
        for i, (srcap, dstap) in enumerate(jobs):
            b = i % 2
            stg = SA[:, 2 * b:2 * b + 2, :].rearrange("p a k -> p (a k)")
            tin = E("D", lambda d, s=srcap, g=stg: d.dma_start(out=g, in_=s),
                    waits=[prev_out[b]], inc=True, dma=True)
            tout = E("G", lambda g, s=stg, dd=dstap: g.dma_start(out=dd, in_=s),
                     waits=[tin], inc=True, dma=True)
            prev_out[b] = tout
            t_cast_done = tout

        last_ln = [t_init]

        def ln_stats(src_ap, idx, tok_in, psum_src=False):
            sm = STAT[:, 4 * idx:4 * idx + 1]
            sq = STAT[:, 4 * idx + 2:4 * idx + 3]
            tsq = E("S", lambda s, a=src_ap: s.activation(SL(10), a, AF.Square),
                    waits=list(tok_in) + [last_ln[0]], inc=True)
            t1 = E("V", lambda v, a=src_ap, o=sm: v.reduce_sum(o, a, axis=mybir.AxisListType.X),
                   waits=list(tok_in) + [last_ln[0]] + ([tsq] if psum_src else []), inc=True)
            t2 = E("V", lambda v, o=sq: v.reduce_sum(o, SL(10), axis=mybir.AxisListType.X),
                   waits=[tsq], inc=True)
            m = STAT[:, 32:33]; v2 = STAT[:, 36:37]; rstd = STAT[:, 40:41]
            nmr = STAT[:, 44:45]; tmp = STAT[:, 48:49]; tmp2 = STAT[:, 52:53]
            ta = E("V", lambda v: v.tensor_single_scalar(m, sm, 1.0 / 1024.0, ALU.mult),
                   waits=[t1, t2], inc=True)
            tb = E("V", lambda v: v.tensor_single_scalar(v2, sq, 1.0 / 1024.0, ALU.mult),
                   waits=[ta], inc=True)
            tc2 = E("V", lambda v: v.tensor_tensor(tmp, m, m, ALU.mult), waits=[tb], inc=True)
            td2 = E("V", lambda v: v.tensor_tensor(v2, v2, tmp, ALU.subtract),
                    waits=[tc2], inc=True)
            tv = E("V", lambda v: v.tensor_single_scalar(v2, v2, 1e-5, ALU.add),
                   waits=[td2], inc=True)
            E("S", lambda s: s.activation(tmp2, v2, AF.Ln), waits=[tv])
            tr = E("S", lambda s: s.activation(rstd, tmp2, AF.Exp, scale=-0.5), inc=True)
            te2 = E("V", lambda v: v.tensor_tensor(tmp, m, rstd, ALU.mult), waits=[tr], inc=True)
            tn = E("V", lambda v: v.tensor_single_scalar(nmr, tmp, -1.0, ALU.mult),
                   waits=[te2], inc=True)
            last_ln[0] = tn
            return [tr, tn], rstd, nmr

        def transpose_to_fm(tm_bf_ap, dst_fm_ap, waits, pt_free):
            tt = None
            for a in range(8):
                tt = E("T", lambda t, a=a, s=tm_bf_ap: t.transpose(
                    PT[:, 128 * a:128 * (a + 1)], s[:, 128 * a:128 * (a + 1)], IDB[:]),
                    waits=(list(waits) + [pt_free]) if a == 0 else (), inc=(a == 7))
            return E("S", lambda s, d=dst_fm_ap: s.copy(
                d, PT[:, :].rearrange("p (a t) -> p a t", a=8)), waits=[tt], inc=True)

        tok_PA_free = t_eu; tok_PB_free = t_iotw
        tok_PT_free = K.tok("V")
        tok_PC_free = None
        tok_PC_free2 = None
        tok_sl10_free = None
        tok_x0_free = None
        tok_h1_free = None
        tok_nu_guard = None
        tok_prol_guard = t_eur
        tok_mix_free = None
        tok_state = t_init
        tok_trf_free = None
        tok_gb_free = None
        tok_gb_free2 = None
        tok_sl_free = {i: None for i in range(11)}
        tok_poolw_free = None

        x_r2 = x_d.rearrange("(n tc p) c -> n tc p c", p=128, tc=4)
        out_r = out_d.rearrange("(n tc p) c -> n tc p c", p=128, tc=4)
        dbg_r = {k2: v2.rearrange("(n tc p) c -> n tc p c", p=128, tc=4)
                 for k2, v2 in dbg_d.items()}

        def load_w(dst2d, nm, waits):
            return E("D", lambda d, o=dst2d, s=wscr[nm]: d.dma_start(out=o, in_=s),
                     waits=waits, inc=True, dma=True)

        for n in range(nt):
            wk_sb = POOLW[:, 0:8192]; wv_sb = POOLW[:, 8192:16384]; wr_sb = POOLW[:, 16384:24576]
            t_wk = load_w(wk_sb, "WkT", [tok_poolw_free, t_cast_done])
            t_wv = load_w(wv_sb, "WvT", [])
            t_wr = load_w(wr_sb, "WrT", [])
            wk3 = wk_sb.rearrange("p (a k) -> p a k", a=8)
            wv3 = wv_sb.rearrange("p (a k) -> p a k", a=8)
            wr3 = wr_sb.rearrange("p (a k) -> p a k", a=8)

            tcar = E("V", lambda v: v.tensor_copy(TRF[:, :, 0:1], CAR1[:]),
                     waits=[tok_trf_free, t_init], inc=True)
            t_h1 = []
            t_xc = None
            for tc in range(4):
                x0c = XC[:, :]; h1c = H1[:, tc, :]
                t_x = E("D", lambda d, n=n, tc=tc: d.dma_start(out=XC[:], in_=x_r2[n][tc]),
                        waits=[t_xc, tok_x0_free] + ([t_cast_done] if tc == 0 else []),
                        inc=True, dma=True)
                st, rstd, nmr = ln_stats(x0c, 0, [t_x])
                th = E("S", lambda s, a=x0c, o=h1c, r=rstd, nm=nmr: s.activation(
                    o, a, AF.Identity, scale=r, bias=nm), waits=st + [tok_h1_free], inc=True)
                t_xc = th
                if stop_stage == 'h1x':
                    t_h1.append(th)
                    t_xc = th
                    continue
                st2, rstd2, nmr2 = ln_stats(h1c, 1, [th])
                tx = E("S", lambda s, a=h1c, r=rstd2, nm=nmr2: s.activation(
                    TMS[:], a, AF.Identity, scale=r, bias=nm), waits=st2, inc=True)
                te = transpose_to_fm(TMS[:], TRF[:, :, 1 + 128 * tc:1 + 128 * (tc + 1)],
                                     [tx, tcar], tok_PT_free)
                tok_PT_free = te
                t_h1.append(th)
                if dbg:
                    E("G", lambda g, n=n, tc=tc, a=h1c: g.dma_start(
                        out=dbg_r["h1"][n][tc], in_=a), waits=[th], inc=True, dma=True)
            t_trf = tok_PT_free

            def stage_dump(waits):
                tt = None
                for tc2 in range(4):
                    tt = E("G", lambda g, n=n, tc2=tc2: g.dma_start(out=out_r[n][tc2],
                                                                   in_=H1[:, tc2, :]),
                           waits=list(waits), inc=True, dma=True)
                return tt

            if stop_stage in ('h1', 'h1x'):
                stage_dump([K.tok("S"), K.tok("V")])
                tok_x0_free = K.tok("G"); tok_h1_free = K.tok("G")
                tok_poolw_free = K.tok("T"); tok_mix_free = K.tok("T")
                tok_trf_free = t_trf
                continue

            tmix = None
            for mi, (cg, cg1) in enumerate([(0, 1), (2, 3), (4, 5)]):
                for a in range(8):
                    cur = TRF[:, a, 1:513]; prv = TRF[:, a, 0:512]
                    o = MIX[:, mi, a, :]
                    E("V", lambda v, c=cur, g=cg, a=a: v.tensor_scalar(
                        SLh(10, 0), c, MIXCO[:, a, g:g + 1], MIXCO[:, a, 6:7],
                        ALU.mult, ALU.add),
                      waits=[t_trf, tok_mix_free] if (mi == 0 and a == 0) else ())
                    E("V", lambda v, p=prv, g=cg1, a=a: v.tensor_scalar(
                        SLh(10, 1), p, MIXCO[:, a, g:g + 1], None, ALU.mult, ALU.bypass))
                    tmix = E("V", lambda v, o=o: v.tensor_tensor(o, SLh(10, 0), SLh(10, 1),
                                                                 ALU.add), inc=True)
                    if n == 0:
                        tmix = E("V", lambda v, o=o, c=cur, g=cg, a=a, mi=mi: v.tensor_scalar(
                            o[:, 0:1], c[:, 0:1], MIXCO[:, a, g:g + 1],
                            MIXCO[:, a, 12 + mi:13 + mi], ALU.mult, ALU.add), inc=True)
            tcarS = E("V", lambda v: v.tensor_copy(CAR1[:], TRF[:, :, 512:513]), inc=True)
            t_mixA = tmix

            for tc in range(NCHUNK):
                def ddump(name, ap, tok, dt2=F32, _n=n, _tc=tc):
                    if not (dbg and _n == 0 and _tc == 0):
                        return None
                    dd = nc.dram_tensor("dbgw_" + name, list(ap.shape), dt2,
                                        kind="ExternalOutput").ap()
                    return E("G", lambda g, a=ap, o=dd: g.dma_start(out=o, in_=a),
                             waits=[tok], inc=True, dma=True)
                ts1 = E("V", lambda v: v.tensor_scalar(r_phc, r_phat, -1.0, -PC_CLAMP,
                                                       ALU.mult, ALU.min),
                        waits=[tok_state, tok_prol_guard], inc=True)
                tok_prol_guard = None
                ts2 = E("V", lambda v: v.tensor_scalar(r_s0, r_phat, -PC_CLAMP, 0.0,
                                                       ALU.add, ALU.min), waits=[ts1], inc=True)
                ts3a = E("S", lambda s: s.activation(r_mu, r_s0, AF.Exp), waits=[ts2], inc=True)
                ts3 = E("V", lambda v: v.tensor_single_scalar(r_mu, r_mu, 1e-38, ALU.add),
                       waits=[ts3a], inc=True)
                ts4a = E("V", lambda v: v.tensor_copy(r_nu, r_nhat),
                         waits=[tok_nu_guard, ts3], inc=True)
                ts4 = E("V", lambda v: v.tensor_tensor(r_nu, r_mu, r_nu, ALU.mult),
                        waits=[ts4a], inc=True)
                tok_nu_guard = None
                tk = None
                for h in range(2):
                    for a in range(8):
                        E("T", lambda t, h=h, a=a, tc=tc: t.matmul(
                            PB[:, 512 * h:512 * (h + 1)],
                            MIX[:, 0, a, 128 * tc:128 * (tc + 1)],
                            wk3[:, a, 512 * h:512 * (h + 1)],
                            start=(a == 0), stop=False),
                            waits=[t_mixA, t_wk, tok_PB_free] if (h == 0 and a == 0) else ())
                    tk = E("T", lambda t, h=h: t.matmul(
                        PB[:, 512 * h:512 * (h + 1)], ONES1[:],
                        r_phc[:, 512 * h:512 * (h + 1)], start=False, stop=True),
                        waits=[ts1] if h == 0 else (), inc=(h == 1))
                tz = E("V", lambda v: v.tensor_tensor(SL(1), PB[:], IOTW[:], ALU.subtract),
                       waits=[tk, tok_sl_free[1]], inc=True)
                tok_PB_free = tz
                tvp = None
                for h in range(2):
                    for a in range(8):
                        tvp = E("T", lambda t, h=h, a=a, tc=tc: t.matmul(
                            PA[:, 512 * h:512 * (h + 1)],
                            MIX[:, 1, a, 128 * tc:128 * (tc + 1)],
                            wv3[:, a, 512 * h:512 * (h + 1)],
                            start=(a == 0), stop=(a == 7)),
                            waits=[t_mixA, t_wv, tok_PA_free] if (h == 0 and a == 0) else (),
                            inc=(h == 1 and a == 7))
                tg = E("S", lambda s: s.activation(SL(2), SL(1), AF.Exp),
                       waits=[tz, tok_sl_free[2]], inc=True)
                tgb = E("S", lambda s: s.copy(GB[:, 0:1024], SL(2)),
                        waits=[tg, tok_gb_free, tok_gb_free2], inc=True)
                tvf = E("S", lambda s: s.copy(SL(3), PA[:]), waits=[tvp, tok_sl_free[3]],
                        inc=True)
                tok_PA_free = tvf
                trp = None
                for h in range(2):
                    for a in range(8):
                        trp = E("T", lambda t, h=h, a=a, tc=tc: t.matmul(
                            PA[:, 512 * h:512 * (h + 1)],
                            MIX[:, 2, a, 128 * tc:128 * (tc + 1)],
                            wr3[:, a, 512 * h:512 * (h + 1)],
                            start=(a == 0), stop=(a == 7)),
                            waits=[tvf, t_wr] if (h == 0 and a == 0) else (),
                            inc=(h == 1 and a == 7))
                tF = None
                for h in range(2):
                    E("T", lambda t, h=h: t.matmul(
                        PB[:, 512 * h:512 * (h + 1)], RB[:], GB[:, 512 * h:512 * (h + 1)],
                        start=True, stop=False), waits=[tgb] if h == 0 else ())
                    tF = E("T", lambda t, h=h: t.matmul(
                        PB[:, 512 * h:512 * (h + 1)], ONES1[:],
                        r_mu[:, 512 * h:512 * (h + 1)], start=False, stop=True),
                        waits=[ts3] if h == 0 else (), inc=(h == 1))

                ter = E("S", lambda s: s.activation(SL(4), PA[:], AF.Exp, scale=-1.0),
                        waits=[trp, tok_sl_free[4]], inc=True)
                tok_PA_free = ter
                tFE = E("V", lambda v: v.tensor_tensor(SL(5), PB[:], GB[:, 0:1024],
                                                       ALU.subtract),
                        waits=[tF, tok_sl_free[5]], inc=True)
                tok_PB_free = tFE
                tt3 = E("S", lambda s: s.activation(SL(6), SL(5), AF.Ln),
                        waits=[tFE, tok_sl_free[6]], inc=True)
                tcpF = E("S", lambda s: s.copy(SL10B, PB[96:128, :]),
                         waits=[tF, tok_sl10_free] + ([t_mixA] if tc == 0 else []),
                         inc=True)
                tdmaF = E("D", lambda d: d.dma_start(out=r_s0, in_=SL10R),
                          waits=[tcpF, tok_state], inc=True, dma=True)
                tst6 = E("S", lambda s: s.activation(r_lnt2, r_s0, AF.Ln),
                         waits=[tdmaF], inc=True)
                tst5b = E("S", lambda s: s.activation(r_s1, r_lnt2, AF.Exp, scale=-1.0),
                         waits=[tst6], inc=True)
                td = E("V", lambda v: v.tensor_tensor(SL(7), SL(6), SL(1), ALU.subtract),
                       waits=[tt3, tok_sl_free[7]], inc=True)
                tad = E("S", lambda s: s.activation(SL(6), SL(7), AF.Abs),
                        waits=[td], inc=True)
                ted = E("S", lambda s: s.activation(SL(7), SL(6), AF.Exp, scale=-1.0),
                        waits=[tad], inc=True)
                tgam = E("S", lambda s: s.activation(SL(6), SL(7), AF.Ln,
                                                     bias=CONSTS[:, 2:3]),
                         waits=[ted], inc=True)
                tst7a = E("V", lambda v: v.tensor_single_scalar(r_phat32, r_phc, -1.0,
                                                                ALU.mult),
                          waits=[tst6], inc=True)
                tst7b = E("V", lambda v: v.tensor_tensor(r_phat32, r_phat32, r_Lw, ALU.add),
                         waits=[tst7a], inc=True)
                tst8 = E("V", lambda v: v.tensor_tensor(r_phat, r_phat32, r_lnt2, ALU.add),
                        waits=[tst7b], inc=True)
                tQ = None
                for h in range(2):
                    tQ = E("T", lambda t, h=h: t.matmul(
                        PB[:, 512 * h:512 * (h + 1)], RF[:], SLh(6, h), start=True, stop=True),
                        waits=[tgam, tFE, tdmaF] if h == 0 else (), inc=(h == 1))
                tqp = E("V", lambda v: v.tensor_tensor(SL(8), PB[:], SL(6), ALU.subtract),
                        waits=[tQ, tok_sl_free[8]], inc=True)
                tEQ = E("S", lambda s: s.activation(SL(9), SL(8), AF.Exp),
                        waits=[tqp, tok_sl_free[9]], inc=True)
                tcpQ = E("S", lambda s: s.copy(SL10B, PB[96:128, :]),
                         waits=[tQ, tdmaF], inc=True)
                tdmaQ = E("D", lambda d: d.dma_start(out=r_s0, in_=SL10R),
                          waits=[tcpQ, tst6], inc=True, dma=True)
                tst10 = E("S", lambda s: s.activation(r_s2, r_s0, AF.Exp),
                          waits=[tdmaQ], inc=True)
                tok_PB_free = tcpQ
                dz = ddump("z", SL(1), tz)
                dg = ddump("g", SL(2), tg)
                dfe = ddump("fe", SL(5), tFE)
                dgam = ddump("gam", SL(6), tgam)
                dqp = ddump("qp", SL(8), tqp)
                tzq = E("V", lambda v: v.tensor_tensor(SL(1), SL(1), SL(8), ALU.subtract),
                        waits=[tqp, dz], inc=True)
                tgt = E("S", lambda s: s.activation(SL(0), SL(1), AF.Exp),
                        waits=[tzq, tok_sl_free[0]], inc=True)
                tgtv = E("V", lambda v: v.tensor_tensor(GB[:, 1024:2048], SL(0), SL(3),
                                                        ALU.mult),
                         waits=[tgt, tvf, tok_gb_free, tok_gb_free2], inc=True)
                tP = None
                for h in range(2):
                    E("T", lambda t, h=h: t.matmul(
                        PC[:, 512 * h:512 * (h + 1)], RB[:],
                        GB[:, 1024 + 512 * h:1024 + 512 * (h + 1)], start=True, stop=False),
                      waits=[tgtv, tok_PC_free, tok_PC_free2] if h == 0 else ())
                    tP = E("T", lambda t, h=h: t.matmul(
                        PC[:, 512 * h:512 * (h + 1)], ONES1[:],
                        r_nu[:, 512 * h:512 * (h + 1)], start=False, stop=True),
                        waits=[ts4] if h == 0 else (), inc=(h == 1))
                deq = ddump("eq", SL(9), tEQ)
                dgtv = ddump("gtv", GB[:, 1024:2048], tgtv, BF16)
                ten = E("V", lambda v: v.tensor_tensor(SL(7), EU[:], SL(2), ALU.mult),
                        waits=[tgam], inc=True)
                tden = E("V", lambda v: v.tensor_tensor(SL(5), SL(5), SL(7), ALU.add),
                         waits=[ten, tt3, dfe], inc=True)
                te1p = E("V", lambda v: v.tensor_single_scalar(SL(4), SL(4), 1.0, ALU.add),
                         waits=[ter], inc=True)
                tden2 = E("V", lambda v: v.tensor_tensor(SL(5), SL(5), SL(4), ALU.mult),
                          inc=True)
                dden2 = ddump("den2", SL(5), tden2)
                tln5 = E("S", lambda s: s.activation(SL(4), SL(5), AF.Ln),
                         waits=[tden2, dqp, dden2], inc=True)
                trec = E("S", lambda s: s.activation(SL(8), SL(4), AF.Exp, scale=-1.0),
                         waits=[tln5, tEQ], inc=True)
                tn1 = E("V", lambda v: v.tensor_tensor(SL(6), PC[:], GB[:, 1024:2048],
                                                       ALU.subtract),
                       waits=[tP, dgam], inc=True)
                tok_PC_free2 = tn1
                tn2 = E("V", lambda v: v.tensor_tensor(SL(6), SL(6), SL(9), ALU.mult),
                        inc=True)
                tdn = E("V", lambda v: v.tensor_tensor(SL(7), SL(7), SL(3), ALU.mult),
                       waits=[tden], inc=True)
                tnum = E("V", lambda v: v.tensor_tensor(SL(6), SL(6), SL(7), ALU.add),
                         inc=True)
                dnum = ddump("num", SL(6), tnum)
                tyz = E("V", lambda v: v.tensor_tensor(SL(6), SL(6), SL(8), ALU.mult),
                        waits=[trec, dnum], inc=True)
                tx1 = E("V", lambda v, tc=tc: v.tensor_tensor(
                    H1[:, tc, :], H1[:, tc, :], SL(6), ALU.add),
                    waits=[t_h1[tc]], inc=True)
                tcpP = E("V", lambda v: v.tensor_copy(SL10B, PC[96:128, :]),
                         waits=[tP, tdmaQ], inc=True)
                tdmaP = E("D", lambda d: d.dma_start(out=r_s0, in_=SL10R),
                          waits=[tcpP, tst10], inc=True, dma=True)
                tst12 = E("V", lambda v: v.tensor_tensor(r_s0, r_s0, r_s2, ALU.mult),
                         waits=[tdmaP, tst8], inc=True)
                tok_PC_free = tdmaP
                tst13 = E("V", lambda v: v.tensor_tensor(r_nhat, r_s0, r_s1, ALU.mult),
                         waits=[tst12], inc=True)
                tok_state = tst13
                tok_sl10_free = tdmaP
                tok_sl_free[0] = tgtv; tok_sl_free[1] = tgt; tok_sl_free[2] = tdn
                tok_sl_free[3] = tdn; tok_sl_free[4] = trec; tok_sl_free[5] = tln5
                tok_sl_free[6] = tyz; tok_sl_free[7] = tnum; tok_sl_free[8] = tyz
                tok_sl_free[9] = tn2
                tok_gb_free = tn1; tok_gb_free2 = tP
                if dbg:
                    E("G", lambda g, n=n, tc=tc: g.dma_start(
                        out=dbg_r["x1"][n][tc], in_=H1[:, tc, :]),
                      waits=[tx1], inc=True, dma=True)
            t_x1 = K.tok("V")
            tok_mix_free = K.tok("T")
            tok_poolw_free = K.tok("T")

            if stop_stage == 'x1':
                stage_dump([t_x1, K.tok("S")])
                tok_x0_free = K.tok("G"); tok_h1_free = K.tok("G")
                tok_trf_free = t_trf
                continue

            tcar2 = E("V", lambda v: v.tensor_copy(TRF[:, :, 0:1], CAR2[:]),
                      waits=[tcarS], inc=True)
            for tc in range(4):
                x1c = H1[:, tc, :]
                st, rstd, nmr = ln_stats(x1c, 2, [t_x1])
                tx = E("S", lambda s, a=x1c, r=rstd, nm=nmr: s.activation(
                    TMS[:], a, AF.Identity, scale=r, bias=nm), waits=st, inc=True)
                te = transpose_to_fm(TMS[:], TRF[:, :, 1 + 128 * tc:1 + 128 * (tc + 1)],
                                     [tx, tcar2], tok_PT_free)
                tok_PT_free = te
            t_trf2 = tok_PT_free
            tmixb = None
            for mi, (cg, cg1) in enumerate([(7, 8), (9, 10)]):
                for a in range(8):
                    cur = TRF[:, a, 1:513]; prv = TRF[:, a, 0:512]
                    o = MIX[:, mi, a, :]
                    E("V", lambda v, c=cur, g=cg, a=a: v.tensor_scalar(
                        SLh(10, 0), c, MIXCO[:, a, g:g + 1], MIXCO[:, a, 11:12],
                        ALU.mult, ALU.add),
                      waits=[t_trf2, tok_mix_free, tok_sl10_free]
                      if (mi == 0 and a == 0) else ())
                    E("V", lambda v, p=prv, g=cg1, a=a: v.tensor_scalar(
                        SLh(10, 1), p, MIXCO[:, a, g:g + 1], None, ALU.mult, ALU.bypass))
                    tmixb = E("V", lambda v, o=o: v.tensor_tensor(o, SLh(10, 0), SLh(10, 1),
                                                                  ALU.add), inc=True)
                    if n == 0:
                        tmixb = E("V", lambda v, o=o, c=cur, g=cg, a=a, mi=mi: v.tensor_scalar(
                            o[:, 0:1], c[:, 0:1], MIXCO[:, a, g:g + 1],
                            MIXCO[:, a, 15 + mi:16 + mi], ALU.mult, ALU.add), inc=True)
            tcarS2 = E("V", lambda v: v.tensor_copy(CAR2[:], TRF[:, :, 512:513]), inc=True)
            t_mixB = tmixb

            wcr_sb = POOLW[:, 0:8192]
            t_wcr = load_w(wcr_sb, "WcrT", [tok_poolw_free])
            wcr3 = wcr_sb.rearrange("p (a k) -> p a k", a=8)
            wckh_sb = POOLW[:, 8192:24576]
            ECR = SAb[:, 4:6, :].rearrange("p a k -> p (a k)")
            tecr = []
            for tc in range(4):
                trc = None
                for h in range(2):
                    for a in range(8):
                        trc = E("T", lambda t, h=h, a=a, tc=tc: t.matmul(
                            PB[:, 512 * h:512 * (h + 1)],
                            MIX[:, 1, a, 128 * tc:128 * (tc + 1)],
                            wcr3[:, a, 512 * h:512 * (h + 1)],
                            start=(a == 0), stop=(a == 7)),
                            waits=[t_mixB, t_wcr, tok_PB_free] if (h == 0 and a == 0) else (),
                            inc=(h == 1 and a == 7))
                t_e = E("S", lambda s, tc=tc: s.activation(
                    ECR[:, 1024 * tc:1024 * (tc + 1)], PB[:], AF.Exp, scale=-1.0),
                    waits=[trc] + ([tok_sl_free[4], tok_sl_free[5]] if tc == 0 else []),
                    inc=True)
                tok_PB_free = t_e
                tecr.append(t_e)
            KC = KCA[:, :]
            t_kc = None
            for wh in range(2):
                twl = E("D", lambda d, wh=wh: d.dma_start(
                    out=wckh_sb, in_=wscr["WckT"][:, 16384 * wh:16384 * (wh + 1)]),
                    waits=[tok_poolw_free if wh == 0 else t_kc], inc=True, dma=True)
                wck4 = wckh_sb.rearrange("p (hs a k) -> p hs a k", hs=16, a=8)
                if wh == 0:
                    kc_pa_free = {0: tok_PA_free, 1: tok_PA_free}
                    kc_sl_free = {0: None, 1: None}
                for hs in range(16):
                    hsg = 16 * wh + hs
                    par = hsg % 2
                    tkp = None
                    for a in range(8):
                        tkp = E("T", lambda t, hs=hs, a=a, par=par, w4=wck4: t.matmul(
                            PA[:, 512 * par:512 * (par + 1)], w4[:, hs, a, :],
                            MIX[:, 0, a, :],
                            start=(a == 0), stop=(a == 7)),
                            waits=([twl, t_mixB, kc_pa_free[par]] +
                                   ([tok_sl_free[0], tok_sl_free[1], tok_sl_free[2],
                                     tok_sl_free[3], tok_sl_free[6], tok_sl_free[7]]
                                    if (wh == 0 and hs == 0) else []))
                            if a == 0 else (), inc=(a == 7))
                    tr1 = E("V", lambda v, par=par: v.tensor_single_scalar(
                        SLh(10, par), PA[:, 512 * par:512 * (par + 1)], 0.0, ALU.max),
                        waits=[tkp, kc_sl_free[par]], inc=True)
                    kc_pa_free[par] = tr1
                    t_kc = E("G", lambda g, hsg=hsg, par=par: g.tensor_tensor(
                        KC[:, 512 * hsg:512 * (hsg + 1)], SLh(10, par), SLh(10, par),
                        ALU.mult),
                        waits=[tr1], inc=True)
                    kc_sl_free[par] = t_kc
                    tok_PA_free = t_kc

            if dbg and n == 0:
                for _nm, _ap, _tok, _dt in [("xck", MIX[:, 0, :, :], t_mixB, BF16),
                                            ("xcr", MIX[:, 1, :, :], t_mixB, BF16),
                                            ("kc", KC, t_kc, BF16),
                                            ("ecr", ECR, tecr[3], BF16)]:
                    _dd = nc.dram_tensor("dbgw_" + _nm, list(_ap.shape), _dt,
                                         kind="ExternalOutput").ap()
                    E("G", lambda g, a=_ap, o=_dd: g.dma_start(out=o, in_=a),
                      waits=[_tok], inc=True, dma=True)
            wcv_sb = POOLW[:, 0:16384]
            kcv = KC.rearrange("p (g k) -> p g k", g=32)
            tep_last = None
            tat_prev = None
            tsv_prev = None
            t_sv = {}
            for dh in range(2):
                t_wcv = E("D", lambda d, dh=dh: d.dma_start(
                    out=wcv_sb, in_=wscr["WcvT"][:, 16384 * dh:16384 * (dh + 1)]),
                    waits=[K.tok("T")], inc=True, dma=True)
                wcv3 = wcv_sb.rearrange("p (a k) -> p a k", a=32)
                for tc in range(4):
                    PX = (PA if tc % 2 == 0 else PB)
                    col = 512 * (tc // 2)
                    tvc = None
                    for hs in range(32):
                        tvc = E("T", lambda t, hs=hs, tc=tc, PX=PX, col=col, w3=wcv3: t.matmul(
                            PX[:, col:col + 512], kcv[:, hs, 128 * tc:128 * (tc + 1)],
                            w3[:, hs, :], start=(hs == 0), stop=(hs == 31)),
                            waits=([t_wcv, t_kc, t_sv.get((dh - 1, tc)),
                                    t_sv.get((dh, tc - 2))] +
                                   ([tok_PB_free] if (dh == 0 and tc == 1) else []))
                            if hs == 0 else (),
                            inc=(hs == 31))
                    tln2 = E("S", lambda s, tc=tc, dh=dh: s.activation(
                        SLh(7, 0), ECR[:, 1024 * tc + 512 * dh:1024 * tc + 512 * dh + 512],
                        AF.Ln, bias=CONSTS[:, 2:3]),
                        waits=[tecr[tc], tsv_prev] +
                              ([tok_sl_free[7]] if (dh == 0 and tc == 0) else []),
                        inc=True)
                    trc2 = E("S", lambda s: s.activation(SLh(7, 1), SLh(7, 0), AF.Exp,
                                                         scale=-1.0),
                             waits=[tln2], inc=True)
                    tsv = E("V", lambda v, PX=PX, col=col: v.tensor_tensor(
                        SLh(6, 0), PX[:, col:col + 512], SLh(7, 1), ALU.mult),
                        waits=[trc2, tvc, tat_prev] +
                              ([tok_sl_free[6]] if (dh == 0 and tc == 0) else []),
                        inc=True)
                    tsv_prev = tsv
                    tat = E("V", lambda v, tc=tc, dh=dh: v.tensor_tensor(
                        H1[:, tc, 512 * dh:512 * dh + 512], H1[:, tc, 512 * dh:512 * dh + 512],
                        SLh(6, 0), ALU.add), waits=[tsv], inc=True)
                    tat_prev = tat
                    t_sv[(dh, tc)] = tsv
                    tep_last = tat
                    if dbg:
                        E("G", lambda g, n=n, tc=tc: g.dma_start(
                            out=dbg_r["attn"][n][tc], in_=H1[:, tc, :]),
                          waits=[tat], inc=True, dma=True)
            tok_PB_free = tep_last; tok_PA_free = tep_last
            t_attn = K.tok("V")
            tok_sl_free[4] = tep_last; tok_sl_free[5] = tep_last
            tok_sl_free[6] = tep_last; tok_sl_free[7] = tep_last
            tok_poolw_free = K.tok("T")

            if stop_stage == 'attn':
                stage_dump([t_attn, K.tok("S")])
                tok_x0_free = K.tok("G"); tok_h1_free = K.tok("G")
                tok_trf_free = t_trf
                continue

            prev_c = None
            for tc in range(4):
                ac = H1[:, tc, :]
                t_x = E("D", lambda d, n=n, tc=tc: d.dma_start(out=XC[:], in_=x_r2[n][tc]),
                        waits=[t_xc, prev_c], inc=True, dma=True)
                st, rstd, nmr = ln_stats(ac, 3, [t_attn, prev_c])
                tnm = E("S", lambda s, a=ac, r=rstd, nm=nmr: s.activation(
                    SL(10), a, AF.Identity, scale=r, bias=nm), waits=st, inc=True)
                tx2 = E("V", lambda v, tc=tc: v.tensor_tensor(H1[:, tc, :], XC[:], SL(10),
                                                              ALU.add),
                        waits=[tnm, t_x], inc=True)
                t_xc = tx2
                st2, rstd2, nmr2 = ln_stats(ac, 4, [tx2])
                th2 = E("S", lambda s, x=ac, r=rstd2, nm=nmr2: s.activation(
                    TMS[:], x, AF.Identity, scale=r, bias=nm), waits=st2, inc=True)
                te = transpose_to_fm(TMS[:], MIX[:, 0, :, 128 * tc:128 * (tc + 1)],
                                     [th2, K.tok("T")], tok_PT_free)
                tok_PT_free = te
                prev_c = tx2
                if dbg:
                    E("G", lambda g, n=n, tc=tc: g.dma_start(
                        out=dbg_r["x2"][n][tc], in_=H1[:, tc, :]),
                      waits=[tx2], inc=True, dma=True)
            t_h2f = tok_PT_free
            t_x2 = K.tok("V")

            if stop_stage == 'x2':
                stage_dump([t_x2, K.tok("S"), K.tok("T")])
                tok_x0_free = K.tok("G"); tok_h1_free = K.tok("G")
                tok_trf_free = t_trf
                continue

            u1_sb = POOLW[:, 0:2048]; v1_sb = POOLW[:, 2048:10240]
            u2_sb = POOLW[:, 10240:18432]; v2_sb = POOLW[:, 18432:20480]
            t_u1 = load_w(u1_sb, "U1T", [tok_poolw_free])
            t_v1 = load_w(v1_sb, "V1T", [])
            t_u2 = load_w(u2_sb, "U2T", [])
            t_v2 = load_w(v2_sb, "V2T", [])
            PBUF = PQB[:, :].rearrange("p (a k) -> p a k", a=2)
            QBUF = PBUF
            u13 = u1_sb.rearrange("p (a k) -> p a k", a=8)
            v13 = v1_sb.rearrange("p (a k) -> p a k", a=2)
            u23 = u2_sb.rearrange("p (a k) -> p a k", a=32)
            v23 = v2_sb.rearrange("p (a k) -> p a k", a=2)
            for rs in range(2):
                for a in range(8):
                    E("T", lambda t, rs=rs, a=a: t.matmul(
                        PA[:, 512 * rs:512 * (rs + 1)], u13[:, a, 128 * rs:128 * (rs + 1)],
                        MIX[:, 0, a, :], start=(a == 0), stop=(a == 7)),
                        waits=[t_u1, t_h2f, tok_PA_free] if (rs == 0 and a == 0) else (),
                        inc=(rs == 1 and a == 7))
            tpb = E("S", lambda s: s.copy(PQB[:, :], PA[:]),
                    waits=[K.tok("T")], inc=True)
            tok_PA_free = tpb
            if stop_stage == 'ffp':
                stage_dump([tpb, K.tok("S"), K.tok("T")])
                tok_x0_free = K.tok("G"); tok_h1_free = K.tok("G")
                tok_trf_free = t_trf; tok_nu_guard = K.tok("T")
                continue
            HB = KC
            t_hh = None
            tsg_p = None
            tr2 = None
            t_hh_list = []
            ff_pb_free = {0: tok_PB_free, 1: tok_PB_free}
            for fs in range(32):
                par = fs % 2
                th = None
                for rs in range(2):
                    th = E("T", lambda t, fs=fs, rs=rs, par=par: t.matmul(
                        PB[:, 512 * par:512 * (par + 1)],
                        v13[:, rs, 128 * fs:128 * (fs + 1)], PBUF[:, rs, :],
                        start=(rs == 0), stop=(rs == 1)),
                        waits=(([t_v1, tpb] if fs == 0 else []) + [ff_pb_free[par]])
                        if rs == 0 else (),
                        inc=(rs == 1))
                tug = E("V", lambda v, fs=fs, par=par: v.tensor_scalar(
                    SLh(10, 0), PB[:, 512 * par:512 * (par + 1)], FB1[:, fs:fs + 1], None,
                    ALU.add, ALU.bypass),
                    waits=[th], inc=True)
                ff_pb_free[par] = tug
                tok_PB_free = tug
                E("V", lambda v: v.tensor_tensor(SLh(10, 1), SLh(10, 0), SLh(10, 0), ALU.mult))
                E("V", lambda v: v.tensor_scalar(SLh(10, 1), SLh(10, 1), GC1, GC2,
                                                 ALU.mult, ALU.add))
                t3 = E("V", lambda v: v.tensor_tensor(SLh(10, 1), SLh(10, 1), SLh(10, 0),
                                                      ALU.mult), inc=True)
                tsg = E("S", lambda s: s.activation(SLh(9, 0), SLh(10, 1), AF.Exp,
                                                    scale=-2.0),
                        waits=[t3] + ([tok_sl_free[9]] if fs == 0 else []), inc=True)
                tln9 = E("S", lambda s: s.activation(SLh(9, 1), SLh(9, 0), AF.Ln,
                                                     bias=CONSTS[:, 2:3]),
                         waits=[tsg], inc=True)
                tr2 = E("S", lambda s: s.activation(SLh(9, 0), SLh(9, 1), AF.Exp,
                                                    scale=-1.0), waits=[tln9], inc=True)
                t_hh = E("V", lambda v, fs=fs: v.tensor_tensor(
                    HB[:, 512 * fs:512 * (fs + 1)], SLh(10, 0), SLh(9, 0), ALU.mult),
                    waits=([t_kc] if fs == 0 else []) + [tr2], inc=True)
                t_hh_list.append(t_hh)
            if stop_stage == 'ffh':
                stage_dump([t_hh, K.tok("S"), K.tok("T")])
                tok_x0_free = K.tok("G"); tok_h1_free = K.tok("G")
                tok_trf_free = t_trf; tok_nu_guard = K.tok("T")
                continue
            for rs in range(2):
                for fs in range(32):
                    E("T", lambda t, rs=rs, fs=fs: t.matmul(
                        PA[:, 512 * rs:512 * (rs + 1)], u23[:, fs, 128 * rs:128 * (rs + 1)],
                        HB[:, 512 * fs:512 * (fs + 1)], start=(fs == 0), stop=(fs == 31)),
                        waits=(([t_u2, tok_PA_free] if fs == 0 else []) +
                               ([t_hh_list[fs]] if rs == 0 else []))
                        if (rs == 0 or fs == 0) else (),
                        inc=(rs == 1 and fs == 31))
            tqb = E("S", lambda s: s.copy(PQB[:, :], PA[:]),
                    waits=[K.tok("T")], inc=True)
            tok_PA_free = tqb
            if stop_stage == 'ffq':
                stage_dump([tqb, K.tok("S"), K.tok("T")])
                tok_x0_free = K.tok("G"); tok_h1_free = K.tok("G")
                tok_trf_free = t_trf; tok_nu_guard = K.tok("T")
                continue
            t_fb2 = E("D", lambda d: d.dma_start(out=r_fb2, in_=fb2_d),
                      waits=[tok_mix_free, tok_state], inc=True, dma=True)
            prev_e = None
            e_free = {0: [tok_PB_free, ff_pb_free[0]], 1: [tqb]}
            for tc in range(4):
                PX = PB if tc % 2 == 0 else PA
                tf = None
                for h in range(2):
                    for rs in range(2):
                        E("T", lambda t, h=h, rs=rs, tc=tc, PX=PX: t.matmul(
                            PX[:, 512 * h:512 * (h + 1)],
                            QBUF[:, rs, 128 * tc:128 * (tc + 1)],
                            v23[:, rs, 512 * h:512 * (h + 1)], start=(rs == 0), stop=False),
                            waits=[t_v2, tqb, t_fb2] + e_free[tc % 2]
                            if (h == 0 and rs == 0) else ())
                    tf = E("T", lambda t, h=h, PX=PX: t.matmul(
                        PX[:, 512 * h:512 * (h + 1)], ONES1[:],
                        r_fb2[:, 512 * h:512 * (h + 1)], start=False, stop=True),
                        inc=(h == 1))
                if stop_stage == 'ffe1':
                    tok_PB_free = E("V", lambda v, PX=PX: v.tensor_copy(SL(10), PX[:]),
                                    waits=[tf], inc=True)
                    prev_e = tok_PB_free
                    e_free[tc % 2] = [tok_PB_free]
                    continue
                st, rstd, nmr = ln_stats(PX[:], 5, [tf, prev_e], psum_src=True)
                if stop_stage == 'ffe2':
                    tok_PB_free = st[1]
                    prev_e = st[1]
                    e_free[tc % 2] = [st[1]]
                    continue
                tnm = E("S", lambda s, r=rstd, nm=nmr, PX=PX: s.activation(
                    SL(10), PX[:], AF.Identity, scale=r, bias=nm), waits=st, inc=True)
                tok_PB_free = tnm
                e_free[tc % 2] = [tnm]
                if stop_stage == 'ffe3':
                    prev_e = tnm
                    continue
                tout = E("G", lambda g, tc=tc: g.tensor_tensor(
                    H1[:, tc, :], H1[:, tc, :], SL(10), ALU.add),
                    waits=[tnm], inc=True)
                prev_e = tout
                E("G", lambda g, n=n, tc=tc: g.dma_start(out=out_r[n][tc], in_=H1[:, tc, :]),
                  inc=True, dma=True)
            if stop_stage in ('ffe1', 'ffe2', 'ffe3'):
                stage_dump([K.tok("S"), K.tok("V"), K.tok("T")])
            tok_PB_free = e_free[0][0]
            tok_PA_free = e_free[1][0]
            tok_x0_free = K.tok("G")
            tok_h1_free = K.tok("G")
            tok_nu_guard = K.tok("T")
            for i in range(10):
                tok_sl_free[i] = tqb
            tok_poolw_free = K.tok("T")
            t_cast_done = None

        E("G", lambda g: g.wait_ge(sem["G"], K.cnt["G"]))
        K.check()

        waited = {e: {} for e in Sched.ENGS}

        def replay(e, eng):
            for waits, fn, inc, amt, sm in K.q[e]:
                for we, wv in waits:
                    if waited[e].get(we, 0) < wv:
                        eng.wait_ge(sem[we], wv)
                        waited[e][we] = wv
                inst = fn(eng)
                if inc:
                    inst.then_inc(sem[sm], amt)

        @blk.tensor
        def _(tensor):
            replay("T", tensor)

        @blk.vector
        def _(vector):
            replay("V", vector)

        @blk.scalar
        def _(scalar):
            replay("S", scalar)

        @blk.sync
        def _(sync):
            replay("D", sync)

        @blk.gpsimd
        def _(gpsimd):
            replay("G", gpsimd)

    return nc


def make_inputs(inp, b, T):
    p = {k[8:]: np.asarray(v, np.float32) for k, v in inp.items()
         if k.startswith("L0_attn_") and k != "L0_attn_x"}
    f = {k[7:]: np.asarray(v, np.float32) for k, v in inp.items() if k.startswith("L0_ffn_")}
    x = np.asarray(inp["L0_attn_x"][b], np.float32)[:T]
    wck = np.ascontiguousarray(
        p["Wck"].T.reshape(8, 128, 32, 128).transpose(1, 2, 0, 3).reshape(128, 32768))
    d = {
        "x": np.ascontiguousarray(x),
        "WkT": _lhsT(p["Wk"].T, 8).reshape(128, -1),
        "WvT": _lhsT(p["Wv"].T, 8).reshape(128, -1),
        "WrT": _lhsT(p["Wr"].T, 8).reshape(128, -1),
        "WcrT": _lhsT(p["Wcr"].T, 8).reshape(128, -1),
        "WckT": wck,
        "WcvT": np.ascontiguousarray(
            p["Wcv"].T.astype(np.float32).reshape(32, 128, 2, 512)
            .transpose(1, 2, 0, 3).reshape(128, 32768)),
        "U1T": _lhsT(f["U1"], 8).reshape(128, -1),
        "V1T": _lhsT(f["V1"], 2).reshape(128, -1),
        "U2T": _lhsT(f["U2"], 32).reshape(128, -1),
        "V2T": _lhsT(f["V2"], 2).reshape(128, -1),
        "fmrows": np.ascontiguousarray(np.stack(
            [_fm(p["rk"]), _fm(p["rv"]), _fm(p["rr"]), _fm(p["crk"]), _fm(p["crr"]),
             _fm(p["ln1_g"]), _fm(p["ln1_b"]), _fm(p["ln2_g"]), _fm(p["ln2_b"])], axis=2)),
        "fb1fm": np.ascontiguousarray(np.asarray(f["b1"], np.float32).reshape(32, 128).T),
        "wrow": np.asarray(p["w"], np.float32).reshape(1, -1).copy(),
        "urow": np.asarray(p["u"], np.float32).reshape(1, -1).copy(),
        "fb2row": np.asarray(f["b2"], np.float32).reshape(1, -1).copy(),
        "Rf32": np.triu(np.ones((128, 128), np.float32)),
        "Rbf": np.triu(np.ones((128, 128))).astype(BF),
        "identbf": np.eye(128).astype(BF),
        "iotacol": np.arange(128, dtype=np.float32).reshape(-1, 1),
        "ones1": np.ones((1, 128), np.float32),
        "onescb": np.ones((128, 1)).astype(BF),
        "onescf": np.ones((128, 1), np.float32),
    }
    return d


def kernel(**inputs):
    T = 8192
    nc = build_program(T)
    in_maps = [make_inputs(inputs, b, T) for b in range(8)]
    res = run_bass_kernel_spmd(nc, in_maps, core_ids=list(range(8)))
    return np.stack([res.results[b]["out"] for b in range(8)], axis=0)
